# version 18
# baseline (speedup 1.0000x reference)
import contextlib
import numpy as np
import concourse.bass as bass
import concourse.mybir as mybir
from concourse.bass import IndirectOffsetOnAxis
from concourse.bass_utils import run_bass_kernel_spmd

F32 = mybir.dt.float32
BF16 = mybir.dt.bfloat16
I32 = mybir.dt.int32
U32 = mybir.dt.uint32
AF = mybir.ActivationFunctionType
ALU = mybir.AluOpType
AX = mybir.AxisListType

D = 1024
SEQ = 2048
NBATCH = 2
T = NBATCH * SEQ
NT = T // 128
NTH = NT // 2
NCC = D // 128

HEADS = 16
DHEAD = 64
PM = 256
PEPS = 1e-4
PNORM = float(DHEAD) ** -0.25
PRATIO = float(PM) ** -0.5

E0 = 8
H0 = 2048
H1R = 4096
E1 = 8
H1E = 4096
CAP = 320
LN_EPS = 1e-5
MASKVAL = -1.25e9


def build_consts():
    c = {}
    c["idf32"] = np.eye(128, dtype=np.float32)
    c["idbf"] = np.eye(128, dtype=np.float32)
    c["ones_row"] = np.ones((1, 128), dtype=np.float32)
    c["iota8"] = np.broadcast_to(np.arange(8, dtype=np.float32), (128, 8)).copy()
    c["ut_strict"] = np.triu(np.ones((128, 128), dtype=np.float32), 1)
    c["causal_add"] = np.where(np.arange(128)[:, None] <= np.arange(128)[None, :],
                               0.0, MASKVAL).astype(np.float32)
    return c


class Ctx:
    def __init__(self, nc, sems):
        self.nc = nc
        self.s = dict(sems)
        self.n = {k: 0 for k in self.s}
        self.bufpe = {}

    def guard_buf(self, eng, name):
        v = self.bufpe.get(name, 0)
        if v > 0:
            eng.wait_ge(self.s["pe"], v)

    def consumed(self, name):
        self.bufpe[name] = self.n["pe"]

    def inc(self, instr, sem, k=1):
        instr.then_inc(self.s[sem], k)
        self.n[sem] += k
        return self.n[sem]

    def wait(self, eng, sem, val=None):
        v = self.n[sem] if val is None else val
        if v > 0:
            eng.wait_ge(self.s[sem], v)


def vw(c, ins):
    c.inc(ins, "dv")
    c.wait(c.nc.vector, "dv")
    return ins


def emit_join(c):
    nc = c.nc
    for eng in (nc.sync, nc.scalar, nc.vector, nc.tensor, nc.gpsimd):
        for s in ("pe", "dv", "ac", "ld", "gd", "sd", "sp"):
            c.wait(eng, s)


class Carver:
    def __init__(self, region):
        self.r = region
        self.off = 0

    def f32(self, shape):
        n = int(np.prod(shape[1:]))
        v = self.r[:, self.off:self.off + n]
        self.off += n
        if len(shape) == 3:
            v = v.rearrange("p (a b) -> p a b", a=shape[1])
        return v

    def u32(self, shape):
        n = int(np.prod(shape[1:]))
        v = self.r[:, self.off:self.off + n].bitcast(U32)
        self.off += n
        if len(shape) == 3:
            v = v.rearrange("p (a b) -> p a b", a=shape[1])
        return v

    def bf16(self, shape):
        n = int(np.prod(shape[1:]))
        assert n % 2 == 0
        v = self.r[:, self.off:self.off + n // 2].bitcast(BF16)
        self.off += n // 2
        if len(shape) == 3:
            v = v.rearrange("p (a b) -> p a b", a=shape[1])
        return v


def emit_stream_load(c, xin, stream):
    nc = c.nc
    src = xin.rearrange("(t p) f -> p t f", p=128)
    for i in range(4):
        ins = nc.sync.dma_start(out=stream[:, 8 * i:8 * i + 8, :],
                                in_=src[:, 8 * i:8 * i + 8, :])
        c.inc(ins, "ld", 16)


def emit_stream_store(c, xout, stream):
    nc = c.nc
    dst = xout.rearrange("(t p) f -> p t f", p=128)
    for i in range(4):
        ins = nc.sync.dma_start(out=dst[:, 8 * i:8 * i + 8, :],
                                in_=stream[:, 8 * i:8 * i + 8, :])
        c.inc(ins, "st", 16)
    c.wait(nc.sync, "st")


def emit_spill(c, stream, half, scratch):
    nc = c.nc
    sl = slice(half * NTH, (half + 1) * NTH)
    ins = nc.sync.dma_start(out=scratch[:, 0:8, :], in_=stream[:, sl, :][:, 0:8, :])
    c.inc(ins, "sp", 16)
    ins = nc.sync.dma_start(out=scratch[:, 8:16, :], in_=stream[:, sl, :][:, 8:16, :])
    c.inc(ins, "sp", 16)


def emit_unspill(c, stream, half, scratch):
    nc = c.nc
    sl = slice(half * NTH, (half + 1) * NTH)
    ins = nc.sync.dma_start(out=stream[:, sl, :][:, 0:8, :], in_=scratch[:, 0:8, :])
    c.inc(ins, "sp", 16)
    ins = nc.sync.dma_start(out=stream[:, sl, :][:, 8:16, :], in_=scratch[:, 8:16, :])
    c.inc(ins, "sp", 16)


def emit_tm_to_fm(c, stream, tiles, xfm, B):
    nc = c.nc
    xcast, idbf, ptr = B["xcast"], B["idbf"], B["p_tr"]
    pe0, dv0, ac0 = c.n["pe"], c.n["dv"], c.n["ac"]
    for i, t in enumerate(tiles):
        buf = xcast[i % 2]
        if i >= 2:
            c.wait(nc.vector, "pe", pe0 + 2 * (i - 1))
        ins = nc.vector.tensor_copy(buf[:], stream[:, t, :])
        c.inc(ins, "dv")
        for h in range(2):
            pb = ptr[h]
            nc.tensor.wait_ge(c.s["dv"], dv0 + i + 1)
            if i >= 1:
                nc.tensor.wait_ge(c.s["ac"], ac0 + 2 * (i - 1) + h + 1)
            for j in range(4):
                cc = h * 4 + j
                last = nc.tensor.transpose(pb[:, j * 128:(j + 1) * 128],
                                           buf[:, cc * 128:(cc + 1) * 128],
                                           idbf[:])
            c.inc(last, "pe")
            c.wait(nc.scalar, "pe")
            for j in range(4):
                cc = h * 4 + j
                ins = nc.scalar.copy(xfm[:, cc, i * 128:(i + 1) * 128],
                                     pb[:, j * 128:(j + 1) * 128])
            c.inc(ins, "ac")


def emit_ln(c, stream, tiles, sq32, stats):
    nc = c.nc
    for t in tiles:
        ssum, ssq = stats[:, 0:1], stats[:, 1:2]
        mean, var = stats[:, 2:3], stats[:, 3:4]
        rstd, tmp = stats[:, 4:5], stats[:, 5:6]
        ins = nc.scalar.activation(out=sq32[:, 0:D], in_=stream[:, t, :],
                                   func=AF.Square, accum_out=ssq)
        c.inc(ins, "ac")
        c.wait(nc.vector, "ac")
        ins = nc.vector.reduce_sum(ssum, stream[:, t, :], axis=AX.X)
        c.inc(ins, "dv")
        c.wait(nc.vector, "dv")
        nc.vector.tensor_scalar_mul(mean, ssum, 1.0 / D)
        nc.vector.tensor_scalar_mul(var, ssq, 1.0 / D)
        nc.vector.tensor_mul(tmp, mean, mean)
        nc.vector.tensor_sub(var, var, tmp)
        ins = nc.vector.tensor_scalar_add(var, var, LN_EPS)
        c.inc(ins, "dv")
        c.wait(nc.scalar, "dv")
        nc.scalar.activation(out=tmp, in_=var, func=AF.Ln)
        ins = nc.scalar.activation(out=rstd, in_=tmp, func=AF.Exp, scale=-0.5)
        c.inc(ins, "ac")
        c.wait(nc.vector, "ac")
        ins = nc.vector.tensor_scalar(out=stream[:, t, :], in0=stream[:, t, :],
                                      scalar1=mean, scalar2=rstd,
                                      op0=ALU.subtract, op1=ALU.mult)
        c.inc(ins, "dv")


def emit_l0_ffn_half(c, stream, W, B, half, cv):
    nc = c.nc
    xfm = B["xfm"]
    stats, silu32 = B["stats"], B["silu32"]
    p_r = B["p_u"][:, 0:8]
    p_u, p_g, p_y = B["p_u"], B["p_g"], B["p_y"]
    rt = B["rt"]

    hstore = cv.bf16([128, 16, 2048])
    mask9 = B["mask9"]
    tiles = list(range(half * NTH, (half + 1) * NTH))

    c.wait(nc.vector, "sp")
    emit_tm_to_fm(c, stream, tiles, xfm, B)
    c.wait(nc.tensor, "gd")
    c.wait(nc.tensor, "ac")
    c.wait(nc.tensor, "sp")
    c.wait(nc.vector, "sp")
    dv0 = c.n["dv"]
    for i in range(NTH):
        if i >= 1:
            nc.tensor.wait_ge(c.s["dv"], dv0 + 2 * i)
        for cc in range(NCC):
            last = nc.tensor.matmul(p_r, xfm[:, cc, i * 128:(i + 1) * 128],
                                    rt[:, cc, :], start=(cc == 0),
                                    stop=(cc == NCC - 1))
        c.inc(last, "pe")
        c.wait(nc.vector, "pe")
        ins = nc.vector.reduce_max(stats[:, 0:1], p_r, axis=AX.X)
        c.inc(ins, "dv")
        c.wait(nc.vector, "dv")
        ins = nc.vector.tensor_scalar(out=mask9[:, i, :], in0=p_r,
                                      scalar1=stats[:, 0:1], scalar2=None,
                                      op0=ALU.is_ge)
        c.inc(ins, "dv")
    wug, wdn = B["wug"], B["wdn"]
    for e in range(9):
        shared = (e == 8)
        if shared:
            upT, gateT, downT = W["up_sT"], W["gate_sT"], W["down_sT"]
        else:
            upT, gateT, downT = W["upT"][e], W["gateT"][e], W["downT"][e]
        pe_b, dv_b, ac_b = c.n["pe"], c.n["dv"], c.n["ac"]
        k = 0
        for hp in range(8):
            wb = wug[hp % 2]
            c.guard_buf(nc.gpsimd, "l0f_wug%d" % (hp % 2))
            ins = nc.gpsimd.dma_start(
                out=wb[:, :, 0:256],
                in_=upT[:, hp * 256:(hp + 1) * 256].rearrange(
                    "(cc p) hh -> p cc hh", p=128))
            c.inc(ins, "gd", 16)
            ins = nc.gpsimd.dma_start(
                out=wb[:, :, 256:512],
                in_=gateT[:, hp * 256:(hp + 1) * 256].rearrange(
                    "(cc p) hh -> p cc hh", p=128))
            c.inc(ins, "gd", 16)
            gd_mark = c.n["gd"]
            for hh in range(2):
                hc = hp * 2 + hh
                for hw in range(4):
                    tw = slice(hw * 512, (hw + 1) * 512)
                    nc.tensor.wait_ge(c.s["gd"], gd_mark)
                    if k >= 1:
                        nc.tensor.wait_ge(c.s["ac"], ac_b + k)
                        nc.tensor.wait_ge(c.s["dv"], dv_b + k)
                    else:
                        c.wait(nc.tensor, "ac", ac_b)
                        c.wait(nc.tensor, "dv", dv_b)
                    for cc in range(NCC):
                        nc.tensor.matmul(
                            p_u[:], wb[:, cc, hh * 128:(hh + 1) * 128],
                            xfm[:, cc, tw], start=(cc == 0),
                            stop=(cc == NCC - 1))
                    for cc in range(NCC):
                        lg = nc.tensor.matmul(
                            p_g[:], wb[:, cc, 256 + hh * 128:256 + (hh + 1) * 128],
                            xfm[:, cc, tw], start=(cc == 0),
                            stop=(cc == NCC - 1))
                    c.inc(lg, "pe")
                    c.wait(nc.scalar, "pe")
                    if k >= 1:
                        c.wait(nc.scalar, "dv", dv_b + k)
                    ins = nc.scalar.activation(out=silu32[:, 0:512],
                                               in_=p_g[:], func=AF.Silu)
                    c.inc(ins, "ac")
                    c.wait(nc.vector, "ac")
                    ins = nc.vector.tensor_mul(hstore[:, hc, tw], p_u[:],
                                               silu32[:, 0:512])
                    c.inc(ins, "dv")
                    k += 1
                    c.consumed("l0f_wug%d" % (hp % 2))
        for ch in range(2):
            cw = slice(ch * 512, (ch + 1) * 512)
            c.guard_buf(nc.gpsimd, "l0f_wdn")
            ins = nc.gpsimd.dma_start(
                out=wdn[:, 0:8, :],
                in_=downT[0:1024, ch * 512:(ch + 1) * 512].rearrange(
                    "(hc p) cf -> p hc cf", p=128))
            c.inc(ins, "gd", 16)
            ins = nc.gpsimd.dma_start(
                out=wdn[:, 8:16, :],
                in_=downT[1024:2048, ch * 512:(ch + 1) * 512].rearrange(
                    "(hc p) cf -> p hc cf", p=128))
            c.inc(ins, "gd", 16)
            gd_dn = c.n["gd"]
            dv_c = c.n["dv"]
            for ts in range(NTH):
                pb = p_y[ts % 2]
                tile = tiles[ts]
                nc.tensor.wait_ge(c.s["gd"], gd_dn)
                c.wait(nc.tensor, "dv", dv_b + 64)
                if ts >= 2:
                    c.wait(nc.tensor, "dv", dv_c + ts - 1)
                else:
                    c.wait(nc.tensor, "dv", dv_c)
                for hc in range(16):
                    last = nc.tensor.matmul(
                        pb[:], hstore[:, hc, ts * 128:(ts + 1) * 128],
                        wdn[:, hc, :], start=(hc == 0), stop=(hc == 15))
                c.inc(last, "pe")
                c.consumed("l0f_wdn")
                c.wait(nc.vector, "pe")
                if shared:
                    ins = nc.vector.tensor_add(stream[:, tile, cw], pb[:],
                                               stream[:, tile, cw])
                else:
                    ins = nc.vector.scalar_tensor_tensor(
                        out=stream[:, tile, cw], in0=pb[:],
                        scalar=mask9[:, ts, e:e + 1],
                        in1=stream[:, tile, cw],
                        op0=ALU.mult, op1=ALU.add)
                c.inc(ins, "dv")
    c.wait(nc.scalar, "dv")
    emit_ln(c, stream, tiles, silu32, stats)


def emit_block_halves(c, stream, scratchA, scratchB, emit_half):
    nc = c.nc
    emit_spill(c, stream, 1, scratchB)
    cv = Carver(stream.ap()[:, NTH:NT, :].rearrange("p a b -> p (a b)"))
    emit_half(0, cv)
    emit_join(c)
    emit_spill(c, stream, 0, scratchA)
    emit_unspill(c, stream, 1, scratchB)
    cv = Carver(stream.ap()[:, 0:NTH, :].rearrange("p a b -> p (a b)"))
    emit_half(1, cv)
    emit_join(c)
    emit_unspill(c, stream, 0, scratchA)
    c.wait(nc.sync, "sp")
    emit_join(c)


def emit_l0_ffn(c, stream, W, B, scratchA, scratchB):
    nc = c.nc
    ins = nc.gpsimd.dma_start(out=B["rt"][:], in_=W["routerT_tiled"])
    c.inc(ins, "gd", 16)
    emit_block_halves(c, stream, scratchA, scratchB,
                      lambda half, cv: emit_l0_ffn_half(c, stream, W, B, half, cv))




def emit_l1_attn_half(c, stream, W, B, half, cv):
    nc = c.nc
    xfm, stats, silu32 = B["xfm"], B["stats"], B["silu32"]
    causal = B["causal"]
    wvt, wqk, wot_full = B["wvt"], B["wqk"], B["wot_full"]
    p_a, p_b, p_o, p_to = B["p_a"], B["p_b"], B["p_o"], B["p_to"]
    pzb = [B["p_z0"], B["p_z1"]]
    tiles = list(range(half * NTH, (half + 1) * NTH))

    zall = cv.bf16([128, 8, 2048])
    v2 = cv.bf16([128, 16, 130])
    q2 = cv.bf16([128, 2048])
    k2 = cv.bf16([128, 2048])
    pbuf = [cv.bf16([128, 512]) for _ in range(4)]
    rowbuf = [cv.bf16([1, 512]) for _ in range(2)]
    recSB = [cv.f32([128, 512]) for _ in range(2)]
    onesb = cv.bf16([128, 130])

    c.wait(nc.vector, "sp")
    emit_tm_to_fm(c, stream, tiles, xfm, B)
    ins = nc.vector.memset(onesb[:], 1.0)
    c.inc(ins, "dv")
    ins = nc.vector.memset(v2[:, :, 64:65], 1.0)
    c.inc(ins, "dv")
    ins = nc.vector.memset(v2[:, :, 129:130], 1.0)
    c.inc(ins, "dv")
    ins = nc.gpsimd.dma_start(out=wvt[:], in_=W["WVT"].rearrange(
        "(cc p) f -> p cc f", p=128))
    c.inc(ins, "gd", 16)
    wvt_gd = c.n["gd"]
    ins = nc.gpsimd.dma_start(
        out=wot_full[:], in_=W["WOT"][:, 0:512].rearrange(
            "(ah p) m -> p ah m", p=128))
    c.inc(ins, "gd", 16)
    wot_gd = c.n["gd"]
    c.wait(nc.tensor, "ac")
    nc.tensor.wait_ge(c.s["gd"], wvt_gd)
    c.wait(nc.tensor, "dv")

    sbanks = [p_a[:], p_b[:], p_o[:], p_to[:]]
    st = {"gk": 0, "gp": 0, "pending": None}
    smark, amark, dmark, zmark = {}, {}, {}, {}
    normmark = [0, 0]
    bcmark = [0, 0]

    def flush_pending():
        if st["pending"] is None:
            return
        zb, zbi, rmark, hb, api_, cwsl_ = st["pending"]
        st["pending"] = None
        nc.tensor.wait_ge(c.s["dv"], rmark)
        ins = nc.tensor.matmul(zb[64:128, :], onesb[0:1, 0:64],
                               rowbuf[zbi][0:1, 0:512], start=True, stop=True)
        bcmark[zbi] = c.inc(ins, "pe")
        nc.scalar.wait_ge(c.s["pe"], bcmark[zbi])
        nc.scalar.wait_ge(c.s["dv"], normmark[zbi])
        ins = nc.scalar.copy(recSB[zbi][0:64, :], zb[64:128, :])
        cmark = c.inc(ins, "ac")
        nc.vector.wait_ge(c.s["ac"], cmark)
        ins = nc.vector.tensor_mul(zall[hb:hb + 64, api_, cwsl_],
                                   zb[0:64, :], recSB[zbi][0:64, :])
        normmark[zbi] = c.inc(ins, "dv")

    for ap_i in range(8):
        wq_t = wqk[ap_i % 2]
        c.guard_buf(nc.gpsimd, "l1a_wqk%d" % (ap_i % 2))
        ins = nc.gpsimd.dma_start(
            out=wq_t[:, 0:8, :],
            in_=W["WQT"][:, ap_i * 128:(ap_i + 1) * 128].rearrange(
                "(cc p) m -> p cc m", p=128))
        c.inc(ins, "gd", 16)
        ins = nc.gpsimd.dma_start(
            out=wq_t[:, 8:16, :],
            in_=W["WKT"][:, ap_i * 128:(ap_i + 1) * 128].rearrange(
                "(cc p) m -> p cc m", p=128))
        c.inc(ins, "gd", 16)
        gd_qk = c.n["gd"]
        ac_qk = c.n["ac"]
        for w in range(4):
            tw = slice(w * 512, (w + 1) * 512)
            nc.tensor.wait_ge(c.s["gd"], gd_qk)
            if w >= 1:
                nc.tensor.wait_ge(c.s["ac"], ac_qk + 2 * w)
            else:
                if st["gk"] >= 1:
                    la = max(amark.get(st["gk"] - 1 - k, 0) for k in range(4))
                    nc.tensor.wait_ge(c.s["ac"], la)
            for cc in range(NCC):
                lq = nc.tensor.matmul(p_a[:], wq_t[:, cc, :], xfm[:, cc, tw],
                                      start=(cc == 0), stop=(cc == NCC - 1))
            for cc in range(NCC):
                lk = nc.tensor.matmul(p_b[:], wq_t[:, 8 + cc, :], xfm[:, cc, tw],
                                      start=(cc == 0), stop=(cc == NCC - 1))
            c.inc(lk, "pe")
            c.wait(nc.scalar, "pe")
            nc.scalar.mul(q2[:, tw], p_a[:], 0.125)
            ins = nc.scalar.copy(k2[:, tw], p_b[:])
            c.inc(ins, "ac", 2)
        c.consumed("l1a_wqk%d" % (ap_i % 2))
        dv_v = c.n["dv"]
        for ts in range(NTH):
            vb = p_a if ts % 2 == 0 else p_b
            nc.tensor.wait_ge(c.s["ac"], ac_qk + 8)
            if ts >= 2:
                nc.tensor.wait_ge(c.s["dv"], dv_v + 2 * (ts - 1))
            for cc in range(NCC):
                lv = nc.tensor.matmul(
                    vb[:, 0:128], xfm[:, cc, ts * 128:(ts + 1) * 128],
                    wvt[:, cc, ap_i * 128:(ap_i + 1) * 128],
                    start=(cc == 0), stop=(cc == NCC - 1))
            c.inc(lv, "pe")
            c.wait(nc.vector, "pe")
            nc.vector.tensor_copy(v2[:, ts, 0:64], vb[:, 0:64])
            ins = nc.vector.tensor_copy(v2[:, ts, 65:129], vb[:, 64:128])
            c.inc(ins, "dv", 2)
        v2_dv = c.n["dv"]
        ins = nc.scalar.memzero(silu32[:, 0:1])
        c.inc(ins, "ac")
        mb = silu32[:, 0:1]
        qk_ac = c.n["ac"]
        nc.tensor.wait_ge(c.s["dv"], v2_dv)

        for hh in range(2):
            hb = 64 * hh
            vsl = slice(65 * hh, 65 * hh + 65)
            for cw in range(4):
                cwsl = slice(cw * 512, (cw + 1) * 512)
                nch = 4 * cw + 4
                zbi = st["gp"] % 2
                zb = pzb[zbi]
                kids = []

                def issue_S(Cc):
                    kid = st["gk"]
                    st["gk"] += 1
                    kids.append(kid)
                    bk = kid % 4
                    sbank = sbanks[bk]
                    pb = pbuf[bk]
                    if kid >= 4:
                        nc.tensor.wait_ge(c.s["ac"], amark[kid - 4])
                    ins = nc.tensor.matmul(
                        sbank, q2[hb:hb + 64, Cc * 128:(Cc + 1) * 128],
                        k2[hb:hb + 64, cwsl], start=True, stop=True)
                    smark[kid] = c.inc(ins, "pe")
                    j = Cc - 4 * cw
                    zg = zmark.get(kid - 4, 0)
                    if j >= 0:
                        nc.vector.wait_ge(c.s["pe"], max(smark[kid], zg))
                        ins = nc.vector.tensor_add(
                            sbank[:, j * 128:(j + 1) * 128],
                            sbank[:, j * 128:(j + 1) * 128], causal[:])
                        c.inc(ins, "dv")
                        if j > 0:
                            ins = nc.vector.memset(pb[:, 0:j * 128], 0.0)
                            c.inc(ins, "dv")
                        dmark[kid] = c.n["dv"]
                        nc.scalar.wait_ge(c.s["dv"], dmark[kid])
                        nc.scalar.wait_ge(c.s["pe"], zg)
                        ins = nc.scalar.activation(
                            out=pb[:, j * 128:512], in_=sbank[:, j * 128:512],
                            func=AF.Exp, bias=mb)
                        amark[kid] = c.inc(ins, "ac")
                    else:
                        nc.scalar.wait_ge(c.s["pe"], max(smark[kid], zg))
                        ins = nc.scalar.activation(out=pb[:], in_=sbank[:, :],
                                                   func=AF.Exp, bias=mb)
                        amark[kid] = c.inc(ins, "ac")

                def issue_z(i, kid):
                    nc.tensor.wait_ge(c.s["ac"], amark[kid])
                    if kid in dmark and dmark[kid] > 0:
                        nc.tensor.wait_ge(c.s["dv"], dmark[kid])
                    if i == 0:
                        nc.tensor.wait_ge(c.s["dv"], normmark[zbi])
                    ins = nc.tensor.matmul(
                        zb[0:65, :], v2[:, i, vsl], pbuf[kid % 4][:],
                        start=(i == 0), stop=(i == nch - 1),
                        skip_group_check=True)
                    zmark[kid] = c.inc(ins, "pe")

                issue_S(0)
                if nch > 1:
                    issue_S(1)
                flush_pending()
                for i in range(nch):
                    issue_z(i, kids[i])
                    if i + 2 < nch:
                        issue_S(i + 2)
                nc.vector.wait_ge(c.s["pe"], max(zmark[kids[-1]], bcmark[zbi]))
                with nc.allow_low_precision(
                        reason="recip rounded to bf16 for the bcast matmul "
                               "operand; same rounding as f32-recip+cast"):
                    ins = nc.vector.reciprocal(rowbuf[zbi][0:1, 0:512],
                                               zb[64:65, :])
                rmark = c.inc(ins, "dv")
                st["pending"] = (zb, zbi, rmark, hb, ap_i, cwsl)
                st["gp"] += 1
    flush_pending()
    c.wait(nc.tensor, "dv")
    obanks = [pzb[0], pzb[1], p_o[:], p_to[:]]
    evm = [0, 0, 0, 0]
    for fh in range(2):
        if fh == 1:
            c.wait(nc.gpsimd, "pe")
            ins = nc.gpsimd.dma_start(
                out=wot_full[:], in_=W["WOT"][:, 512:1024].rearrange(
                    "(ah p) m -> p ah m", p=128))
            c.inc(ins, "gd", 16)
            wot_gd = c.n["gd"]
        nc.tensor.wait_ge(c.s["gd"], wot_gd)
        for ti in range(NTH):
            tile = tiles[ti]
            ob = obanks[ti % 4]
            nc.tensor.wait_ge(c.s["dv"], evm[ti % 4])
            for api in range(8):
                lo = nc.tensor.matmul(ob[0:128, 0:512],
                                      zall[:, api, ti * 128:(ti + 1) * 128],
                                      wot_full[:, api, :],
                                      start=(api == 0), stop=(api == 7),
                                      skip_group_check=True)
            pem = c.inc(lo, "pe")
            nc.vector.wait_ge(c.s["pe"], pem)
            ins = nc.vector.tensor_add(
                stream[:, tile, fh * 512:(fh + 1) * 512], ob[0:128, 0:512],
                stream[:, tile, fh * 512:(fh + 1) * 512])
            evm[ti % 4] = c.inc(ins, "dv")
    c.wait(nc.scalar, "dv")
    emit_ln(c, stream, tiles, silu32, stats)


def prep_l1_attn_weights(inp):
    WQ = np.asarray(inp["L1_attn_W_Q"])
    WK = np.asarray(inp["L1_attn_W_K"])
    WV = np.asarray(inp["L1_attn_W_V"])
    WO = np.asarray(inp["L1_attn_W_O"])
    return {
        "WQT": np.ascontiguousarray(WQ.transpose(2, 0, 1).reshape(D, D)),
        "WKT": np.ascontiguousarray(WK.transpose(2, 0, 1).reshape(D, D)),
        "WVT": np.ascontiguousarray(WV.transpose(2, 0, 1).reshape(D, D)),
        "WOT": np.ascontiguousarray(WO.transpose(0, 2, 1).reshape(D, D)),
    }


L1_ATTN_SHAPES = {"WQT": [D, D], "WKT": [D, D], "WVT": [D, D], "WOT": [D, D]}




def emit_l0_attn_pass1(c, stream, W, B, half, cv, rebuild_xfm=True):
    nc = c.nc
    xfm, stats = B["xfm"], B["stats"]
    paug, wvt, wqk = B["paug"], B["wvt"], B["wqk"]
    ctxbuf, vsumbuf, kmax = B["ctxbuf"], B["vsumbuf"], B["kmax"]
    p_a, p_b, p_dd, p_dd2, p_c0, p_c1 = (B["p_a"], B["p_b"], B["p_dd"],
                                         B["p_dd2"], B["p_c0"], B["p_c1"])
    tiles = list(range(half * NTH, (half + 1) * NTH))

    v_tm = cv.bf16([128, 16, 1024])
    k2 = cv.bf16([128, 2048])
    augk = cv.bf16([128, 2048])
    kp = cv.bf16([128, 16, 256])
    onesb = cv.bf16([128, 130])
    bb = cv.f32([128, 2])
    bb2 = cv.f32([128, 2])
    kmt = cv.f32([128, NTH])

    c.wait(nc.vector, "sp")
    if rebuild_xfm:
        emit_tm_to_fm(c, stream, tiles, xfm, B)
    ins = nc.vector.memset(onesb[:], 1.0)
    c.inc(ins, "dv")
    c.wait(nc.tensor, "ac")
    c.wait(nc.tensor, "gd")
    c.wait(nc.tensor, "dv")
    dv_v = c.n["dv"]
    for ts in range(NTH):
        if ts >= 1:
            nc.tensor.wait_ge(c.s["dv"], dv_v + ts)
        for half_f in range(2):
            vb = p_a if half_f == 0 else p_b
            for cc in range(NCC):
                lv = nc.tensor.matmul(
                    vb[:], xfm[:, cc, ts * 128:(ts + 1) * 128],
                    B["wvt"][:, cc, half_f * 512:(half_f + 1) * 512],
                    start=(cc == 0), stop=(cc == NCC - 1))
            c.inc(lv, "pe")
            c.wait(nc.vector, "pe")
            ins = nc.vector.tensor_copy(
                v_tm[:, ts, half_f * 512:(half_f + 1) * 512], vb[:])
        c.inc(ins, "dv")
    for ap_i in range(8):
        wk_t = wqk[ap_i % 2]
        c.guard_buf(nc.gpsimd, "l0a_wqk%d" % (ap_i % 2))
        ins = nc.gpsimd.dma_start(
            out=wk_t[:, 0:8, :],
            in_=W["wk"][:, ap_i * 128:(ap_i + 1) * 128].rearrange(
                "(cc p) m -> p cc m", p=128))
        c.inc(ins, "gd", 16)
        gd_k = c.n["gd"]
        ac_k = c.n["ac"]
        for w in range(4):
            tw = slice(w * 512, (w + 1) * 512)
            kb = p_a if w % 2 == 0 else p_b
            nc.tensor.wait_ge(c.s["gd"], gd_k)
            c.wait(nc.tensor, "dv")
            if w >= 2:
                nc.tensor.wait_ge(c.s["ac"], ac_k + w - 1)
            for cc in range(NCC):
                lk = nc.tensor.matmul(kb[:], wk_t[:, cc, :], xfm[:, cc, tw],
                                      start=(cc == 0), stop=(cc == NCC - 1))
            c.inc(lk, "pe")
            c.consumed("l0a_wqk%d" % (ap_i % 2))
            c.wait(nc.scalar, "pe")
            ins = nc.scalar.copy(k2[:, tw], kb[:])
            c.inc(ins, "ac")
        for hh in range(2):
            h = ap_i * 2 + hh
            hb = 64 * hh
            c.wait(nc.vector, "ac")
            c.wait(nc.vector, "pe")
            ins = nc.vector.tensor_copy(augk[0:64, :], k2[hb:hb + 64, :])
            c.inc(ins, "dv")
            c.wait(nc.scalar, "dv")
            ins = nc.scalar.activation(out=augk[64:128, :], in_=k2[hb:hb + 64, :],
                                       func=AF.Square)
            c.inc(ins, "ac")
            c.wait(nc.tensor, "ac")
            mmark = {}
            amark = {}
            dmark = {}
            for ts in range(NTH):
                db = p_dd if ts % 2 == 0 else p_dd2
                if ts >= 2:
                    nc.tensor.wait_ge(c.s["dv"], dmark[ts - 2])
                    nc.tensor.wait_ge(c.s["ac"], amark[ts - 2])
                ins = nc.tensor.matmul(db[:, 0:257],
                                       augk[:, ts * 128:(ts + 1) * 128],
                                       paug[:], start=True, stop=True)
                mmark[ts] = c.inc(ins, "pe")
                nc.vector.wait_ge(c.s["pe"], mmark[ts])
                ins = nc.vector.reduce_max(kmt[:, ts:ts + 1], db[:, 0:256],
                                           axis=AX.X)
                c.inc(ins, "dv")
                ins = nc.vector.tensor_copy(bb2[:, ts % 2:ts % 2 + 1],
                                            db[:, 256:257])
                dmark[ts] = c.inc(ins, "dv")
                nc.scalar.wait_ge(c.s["dv"], dmark[ts])
                ins = nc.scalar.activation(out=kp[:, ts, :], in_=db[:, 0:256],
                                           func=AF.Exp,
                                           bias=bb2[:, ts % 2:ts % 2 + 1])
                amark[ts] = c.inc(ins, "ac")
            c.wait(nc.vector, "dv")
            vw(c, nc.vector.reduce_max(stats[:, 0:1], kmt[:], axis=AX.X))
            ins = nc.vector.tensor_max(kmax[:, 0:1], kmax[:, 0:1], stats[:, 0:1])
            c.inc(ins, "dv")
            c.wait(nc.vector, "dv")
            c.wait(nc.tensor, "ac")
            for mc in range(2):
                pc = p_c0 if mc == 0 else p_c1
                c.wait(nc.tensor, "dv")
                for ts in range(NTH):
                    nc.tensor.matmul(pc[:, 0:64],
                                     kp[:, ts, mc * 128:(mc + 1) * 128],
                                     v_tm[:, ts, h * 64:(h + 1) * 64],
                                     start=(ts == 0), stop=False,
                                     skip_group_check=True)
                    lm = nc.tensor.matmul(pc[:, 64:65], kp[:, ts, mc * 128:(mc + 1) * 128],
                                          onesb[:, 0:1],
                                          start=False, stop=(ts == NTH - 1),
                                          skip_group_check=True)
                c.inc(lm, "pe")
                c.wait(nc.vector, "pe")
                ins = nc.vector.tensor_copy(
                    ctxbuf[:, half, h, mc, 0:65], pc[:, 0:65])
                c.inc(ins, "dv")
            c.wait(nc.tensor, "dv")
            for ts in range(NTH):
                nc.tensor.matmul(p_dd[0:1, 0:64], onesb[:, 0:1],
                                 v_tm[:, ts, h * 64:(h + 1) * 64],
                                 start=(ts == 0), stop=False,
                                 skip_group_check=True)
                lm = nc.tensor.matmul(p_dd[0:1, 64:65], onesb[:, 0:1],
                                      onesb[:, 1:2],
                                      start=False, stop=(ts == NTH - 1),
                                      skip_group_check=True)
            c.inc(lm, "pe")
            c.wait(nc.vector, "pe")
            ins = nc.vector.tensor_scalar_mul(
                vsumbuf[0:1, (half * 16 + h) * 66:(half * 16 + h) * 66 + 65],
                p_dd[0:1, 0:65], PRATIO * PEPS)
            c.inc(ins, "dv")
            c.wait(nc.tensor, "dv")


def emit_l0_attn_pass2(c, stream, W, B, half, cv, rebuild_xfm):
    nc = c.nc
    xfm, stats, silu32 = B["xfm"], B["stats"], B["silu32"]
    paug, wqk, wvt = B["paug"], B["wqk"], B["wvt"]
    wotF = lambda api: wvt[:, api // 2,
                           (api % 2) * 512:(api % 2) * 512 + 512]
    wot_dst = wvt.ap()[:, 0:4, :].rearrange("p a (b f) -> p a b f", b=2)
    ctxbuf, vsumbuf, acol = B["ctxbuf"], B["vsumbuf"], B["acol"]
    p_a, p_b, p_dd, p_dd2, p_c0, p_c1 = (B["p_a"], B["p_b"], B["p_dd"],
                                         B["p_dd2"], B["p_c0"], B["p_c1"])
    p_tr = B["p_tr"]
    tiles = list(range(half * NTH, (half + 1) * NTH))

    q2 = cv.bf16([128, 2048])
    q2sq = cv.bf16([128, 2048])
    qp = cv.bf16([128, 16, 256])
    qpT = cv.bf16([128, 2, 2048])
    attn_fm = cv.bf16([128, 8, 2048])
    ctxf = cv.bf16([128, 2, 66])
    onesb = cv.bf16([128, 130])
    bb = cv.f32([128, 4])
    tmp32 = cv.f32([128, 2, 256])
    rowb = [cv.bf16([1, 512]) for _ in range(2)]
    recq = cv.f32([128, 512])

    c.wait(nc.vector, "sp")
    if rebuild_xfm:
        emit_tm_to_fm(c, stream, tiles, xfm, B)
    ins = nc.vector.memset(onesb[:], 1.0)
    c.inc(ins, "dv")
    epsb = B["kmax"][:, 3:4]
    c.wait(nc.gpsimd, "pe")
    ins = nc.gpsimd.dma_start(
        out=wot_dst, in_=W["wo"][:, 0:512].rearrange(
            "(ah p) m -> p ah m", p=128).rearrange(
            "p (a b) m -> p a b m", b=2))
    c.inc(ins, "gd", 16)
    wot_gd0 = c.n["gd"]

    ddbanks = [p_dd, p_dd2, p_c0, p_c1]
    outst = {"pending": None, "norm": 0, "nb": [0, 0], "bc": [0, 0]}

    def flush_out():
        if outst["pending"] is None:
            return
        ob, pi, rmark, dst = outst["pending"]
        outst["pending"] = None
        nc.tensor.wait_ge(c.s["dv"], rmark)
        ins = nc.tensor.matmul(ob[64:128, :], onesb[0:1, 0:64],
                               rowb[pi][0:1, 0:512], start=True, stop=True)
        outst["bc"][pi] = c.inc(ins, "pe")
        nc.scalar.wait_ge(c.s["pe"], outst["bc"][pi])
        nc.scalar.wait_ge(c.s["dv"], outst["norm"])
        ins = nc.scalar.copy(recq[0:64, :], ob[64:128, :])
        cm = c.inc(ins, "ac")
        nc.vector.wait_ge(c.s["ac"], cm)
        ins = nc.vector.tensor_mul(dst, ob[0:64, :], recq[0:64, :])
        outst["norm"] = outst["nb"][pi] = c.inc(ins, "dv")

    for ap_i in range(8):
        wq_t = wqk[ap_i % 2]
        c.guard_buf(nc.gpsimd, "l0a_wqk%d" % (ap_i % 2))
        ins = nc.gpsimd.dma_start(
            out=wq_t[:, 0:8, :],
            in_=W["wq"][:, ap_i * 128:(ap_i + 1) * 128].rearrange(
                "(cc p) m -> p cc m", p=128))
        c.inc(ins, "gd", 16)
        gd_q = c.n["gd"]
        ac_q = c.n["ac"]
        flush_out()
        c.wait(nc.tensor, "ac")
        c.wait(nc.tensor, "dv")
        for w in range(4):
            tw = slice(w * 512, (w + 1) * 512)
            qb = p_a if w % 2 == 0 else p_b
            nc.tensor.wait_ge(c.s["gd"], gd_q)
            if w >= 2:
                nc.tensor.wait_ge(c.s["ac"], ac_q + w - 1)
            for cc in range(NCC):
                lq = nc.tensor.matmul(qb[:], wq_t[:, cc, :], xfm[:, cc, tw],
                                      start=(cc == 0), stop=(cc == NCC - 1))
            c.inc(lq, "pe")
            c.consumed("l0a_wqk%d" % (ap_i % 2))
            c.wait(nc.scalar, "pe")
            ins = nc.scalar.copy(q2[:, tw], qb[:])
            c.inc(ins, "ac")
        c.wait(nc.scalar, "pe")
        ins = nc.scalar.activation(out=q2sq[:, :], in_=q2[:, :], func=AF.Square)
        sqm = c.inc(ins, "ac")
        bankr = [0, 0, 0, 0]
        for hh in range(2):
            h = ap_i * 2 + hh
            hb = 64 * hh
            vsl0 = (half * 16 + h) * 66
            ctxm = 0
            for mc in range(2):
                pc = p_c0 if mc == 0 else p_c1
                nc.tensor.wait_ge(c.s["ac"], bankr[2 + mc])
                ins = nc.tensor.matmul(pc[:, 0:65], onesb[0:1, 0:128],
                                       vsumbuf[0:1, vsl0:vsl0 + 65],
                                       start=True, stop=True)
                pcm = c.inc(ins, "pe")
                nc.vector.wait_ge(c.s["pe"], pcm)
                ins = nc.vector.scalar_tensor_tensor(
                    out=ctxf[:, mc, 0:65], in0=ctxbuf[:, half, h, mc, 0:65],
                    scalar=acol[:, 0:1], in1=pc[:, 0:65],
                    op0=ALU.mult, op1=ALU.add)
                ctxm = c.inc(ins, "dv")
            nc.tensor.wait_ge(c.s["ac"], sqm)
            amark_dd = {}
            qpm = {}
            for ts in range(NTH):
                db = ddbanks[ts % 4]
                p = ts % 2
                if ts >= 4:
                    nc.tensor.wait_ge(c.s["ac"], qpm[ts - 4])
                elif ts >= 2:
                    nc.tensor.wait_ge(c.s["dv"], ctxm)
                    nc.tensor.wait_ge(c.s["ac"], bankr[ts % 4])
                else:
                    nc.tensor.wait_ge(c.s["ac"], bankr[ts % 4])
                tsl = slice(ts * 128, (ts + 1) * 128)
                nc.tensor.matmul(db[:, 0:257], q2[hb:hb + 64, tsl],
                                 B["paugA"][hb:hb + 64, :],
                                 start=True, stop=False)
                ins = nc.tensor.matmul(db[:, 0:257], q2sq[hb:hb + 64, tsl],
                                       B["paugB"][hb:hb + 64, :],
                                       start=False, stop=True)
                ddm = c.inc(ins, "pe")
                nc.vector.wait_ge(c.s["pe"], ddm)
                ins = nc.vector.reduce_max(stats[:, p:p + 1], db[:, 0:256],
                                           axis=AX.X)
                mm = c.inc(ins, "dv")
                nc.vector.wait_ge(c.s["dv"], mm)
                ins = nc.vector.tensor_sub(bb[:, p:p + 1], db[:, 256:257],
                                           stats[:, p:p + 1])
                sm = c.inc(ins, "dv")
                nc.scalar.wait_ge(c.s["dv"], sm)
                ins = nc.scalar.activation(out=tmp32[:, p, :], in_=db[:, 0:256],
                                           func=AF.Exp, bias=bb[:, p:p + 1])
                em = c.inc(ins, "ac")
                bankr[ts % 4] = em
                ins = nc.scalar.activation(out=qp[:, ts, :], in_=tmp32[:, p, :],
                                           func=AF.Identity, scale=PRATIO,
                                           bias=epsb)
                qpm[ts] = c.inc(ins, "ac")
            nc.tensor.wait_ge(c.s["ac"], qpm[NTH - 1])
            trm = {}
            for ts in range(NTH):
                ptr = p_tr[ts % 2]
                if ts >= 2:
                    nc.tensor.wait_ge(c.s["ac"], trm[ts - 2])
                for mc in range(2):
                    ins = nc.tensor.transpose(
                        ptr[:, mc * 128:(mc + 1) * 128],
                        qp[:, ts, mc * 128:(mc + 1) * 128], B["idbf"][:])
                tm = c.inc(ins, "pe")
                nc.scalar.wait_ge(c.s["pe"], tm)
                ins = nc.scalar.copy(
                    qpT[:, 0:2, ts * 128:(ts + 1) * 128],
                    ptr[:, 0:256].rearrange("p (a b) -> p a b", a=2))
                trm[ts] = c.inc(ins, "ac")
            nc.tensor.wait_ge(c.s["ac"], trm[NTH - 1])
            for win in range(4):
                pi = win % 2
                ob = p_a if pi == 0 else p_b
                wsl = slice(win * 512, (win + 1) * 512)
                nc.tensor.wait_ge(c.s["dv"], outst["nb"][pi])
                nc.tensor.matmul(ob[0:65, 0:512], ctxf[:, 0, 0:65],
                                 qpT[:, 0, wsl], start=True, stop=False,
                                 skip_group_check=True)
                ins = nc.tensor.matmul(ob[0:65, 0:512], ctxf[:, 1, 0:65],
                                       qpT[:, 1, wsl], start=False, stop=True,
                                       skip_group_check=True)
                zm = c.inc(ins, "pe")
                flush_out()
                nc.vector.wait_ge(c.s["pe"], max(zm, outst["bc"][pi]))
                with nc.allow_low_precision(
                        reason="recip rounded to bf16 for the bcast matmul "
                               "operand; same rounding as f32-recip+cast"):
                    ins = nc.vector.reciprocal(rowb[pi][0:1, 0:512],
                                               ob[64:65, 0:512])
                rmark = c.inc(ins, "dv")
                outst["pending"] = (ob, pi, rmark,
                                    attn_fm[hb:hb + 64, ap_i, wsl])
    flush_out()
    c.wait(nc.tensor, "dv")
    obanks = [p_dd[:, 0:512], p_dd2[:, 0:512], p_c0[:, 0:512], p_c1[:, 0:512]]
    evm = [0, 0, 0, 0]
    wot_gd = wot_gd0
    for fh in range(2):
        if fh == 1:
            c.wait(nc.gpsimd, "pe")
            ins = nc.gpsimd.dma_start(
                out=wot_dst, in_=W["wo"][:, 512:1024].rearrange(
                    "(ah p) m -> p ah m", p=128).rearrange(
                    "p (a b) m -> p a b m", b=2))
            c.inc(ins, "gd", 16)
            wot_gd = c.n["gd"]
        nc.tensor.wait_ge(c.s["gd"], wot_gd)
        for ti in range(NTH):
            tile = tiles[ti]
            ob = obanks[ti % 4]
            nc.tensor.wait_ge(c.s["dv"], evm[ti % 4])
            for api in range(8):
                nc.tensor.matmul(ob, attn_fm[:, api, ti * 128:(ti + 1) * 128],
                                 wotF(api), start=(api == 0), stop=False,
                                 skip_group_check=True)
            lo = nc.tensor.matmul(ob, onesb[0:1, 0:128],
                                  B["borow"][0:1, fh * 512:(fh + 1) * 512],
                                  start=False, stop=True,
                                  skip_group_check=True)
            pem = c.inc(lo, "pe")
            nc.vector.wait_ge(c.s["pe"], pem)
            ins = nc.vector.tensor_add(
                stream[:, tile, fh * 512:(fh + 1) * 512], ob,
                stream[:, tile, fh * 512:(fh + 1) * 512])
            evm[ti % 4] = c.inc(ins, "dv")
    c.wait(nc.scalar, "dv")
    emit_ln(c, stream, tiles, silu32, stats)


def emit_l0_attn(c, stream, W, B, scrA, scrB, cc_in, cc_out):
    nc = c.nc
    paug, pload = B["paug"], B["pload"]
    paugA, paugB = B["paugA"], B["paugB"]
    ins = nc.sync.dma_start(out=pload[0:64, :], in_=W["projT"])
    c.inc(ins, "ld", 16)
    ins = nc.sync.dma_start(out=pload[64:128, :], in_=W["projT"])
    c.inc(ins, "ld", 16)
    ins = nc.vector.memset(paug[:], 0.0)
    c.inc(ins, "dv")
    ins = nc.vector.memset(paug[64:128, 256:257], -0.5 * PNORM * PNORM)
    c.inc(ins, "dv")
    ins = nc.vector.memset(paugA[:], 0.0)
    c.inc(ins, "dv")
    ins = nc.vector.memset(paugB[:], 0.0)
    c.inc(ins, "dv")
    ins = nc.vector.memset(paugB[:, 256:257], -0.5 * PNORM * PNORM)
    c.inc(ins, "dv")
    c.wait(nc.scalar, "ld")
    c.wait(nc.scalar, "dv")
    ins = nc.scalar.mul(paug[0:64, 0:256], pload[0:64, :], PNORM)
    c.inc(ins, "ac")
    ins = nc.scalar.mul(paugA[:, 0:256], pload[:], PNORM)
    c.inc(ins, "ac")
    ins = nc.vector.memset(B["kmax"][:, 0:1], -3.0e38)
    c.inc(ins, "dv")
    ins = nc.vector.memset(B["kmax"][:, 3:4], PEPS * PRATIO)
    c.inc(ins, "dv")
    ins = nc.vector.memset(B["onesrow"][:], 1.0)
    c.inc(ins, "dv")
    ins = nc.gpsimd.dma_start(out=B["wvt"][:], in_=W["wv"].rearrange(
        "(cc p) f -> p cc f", p=128))
    c.inc(ins, "gd", 16)
    ins = nc.gpsimd.dma_start(out=B["borow"][:], in_=W["bo_row"])
    c.inc(ins, "gd", 16)
    emit_join(c)

    emit_spill(c, stream, 1, scrB)
    cv = Carver(stream.ap()[:, NTH:NT, :].rearrange("p a b -> p (a b)"))
    emit_l0_attn_pass1(c, stream, W, B, 0, cv)
    emit_join(c)
    emit_spill(c, stream, 0, scrA)
    emit_unspill(c, stream, 1, scrB)
    cv = Carver(stream.ap()[:, 0:NTH, :].rearrange("p a b -> p (a b)"))
    emit_l0_attn_pass1(c, stream, W, B, 1, cv)
    emit_join(c)

    c.wait(nc.tensor, "dv")
    ins = nc.tensor.transpose(B["p_a"][0:1, 0:128], B["kmax"][:, 0:1],
                              B["idf32"][:])
    c.inc(ins, "pe")
    c.wait(nc.vector, "pe")
    ins = nc.vector.reduce_max(B["kmax"][0:1, 1:2], B["p_a"][0:1, 0:128],
                               axis=AX.X)
    c.inc(ins, "dv")
    c.wait(nc.vector, "dv")
    nc.vector.memset(B["stats"][0:1, 0:8], 0.0)
    ins = nc.vector.tensor_copy(B["stats"][0:1, 0:1], B["kmax"][0:1, 1:2])
    c.inc(ins, "dv")
    c.wait(nc.gpsimd, "dv")
    ins = nc.gpsimd.dma_start(out=cc_in, in_=B["stats"][0:1, 0:8])
    c.inc(ins, "gd", 16)
    c.wait(nc.gpsimd, "gd")
    ins = nc.gpsimd.collective_compute(
        "AllReduce", ALU.max, replica_groups=[list(range(8))],
        ins=[cc_in.opt()], outs=[cc_out.opt()])
    c.inc(ins, "cc")
    c.wait(nc.gpsimd, "cc", 1)
    ins = nc.gpsimd.dma_start(out=B["kmax"][0:1, 1:2], in_=cc_out[0:1, 0:1])
    c.inc(ins, "gd", 16)
    c.wait(nc.scalar, "gd")
    ins = nc.scalar.activation(out=B["kmax"][0:1, 2:3], in_=B["kmax"][0:1, 1:2],
                               func=AF.Exp, scale=-1.0)
    c.inc(ins, "ac")
    c.wait(nc.vector, "ac")
    nc.vector.tensor_scalar_mul(B["kmax"][0:1, 2:3], B["kmax"][0:1, 2:3], PRATIO)
    ins = nc.vector.tensor_copy(B["kmbf"][0:1, 1:2], B["kmax"][0:1, 2:3])
    c.inc(ins, "dv")
    c.wait(nc.tensor, "dv")
    ins = nc.tensor.matmul(B["p_c0"][:, 0:1], B["onesrow"][0:1, 0:128],
                           B["kmbf"][0:1, 1:2], start=True, stop=True)
    c.inc(ins, "pe")
    c.wait(nc.scalar, "pe")
    ins = nc.scalar.copy(B["acol"][:, 0:1], B["p_c0"][:, 0:1])
    c.inc(ins, "ac")
    emit_join(c)

    cv = Carver(stream.ap()[:, 0:NTH, :].rearrange("p a b -> p (a b)"))
    emit_l0_attn_pass2(c, stream, W, B, 1, cv, rebuild_xfm=False)
    emit_join(c)
    emit_spill(c, stream, 1, scrB)
    emit_unspill(c, stream, 0, scrA)
    cv = Carver(stream.ap()[:, NTH:NT, :].rearrange("p a b -> p (a b)"))
    emit_l0_attn_pass2(c, stream, W, B, 0, cv, rebuild_xfm=True)
    emit_join(c)
    emit_unspill(c, stream, 1, scrB)
    c.wait(nc.sync, "sp")
    emit_join(c)


def prep_l0_attn_weights(inp):
    return {
        "wq": np.ascontiguousarray(np.asarray(inp["L0_attn_wq"])),
        "wk": np.ascontiguousarray(np.asarray(inp["L0_attn_wk"])),
        "wv": np.ascontiguousarray(np.asarray(inp["L0_attn_wv"])),
        "wo": np.ascontiguousarray(np.asarray(inp["L0_attn_wo"])),
        "bo_row": np.ascontiguousarray(
            np.asarray(inp["L0_attn_bo"]).reshape(1, D)),
        "projT": np.ascontiguousarray(np.asarray(inp["L0_attn_proj"]).T),
    }


L0_ATTN_SHAPES = {"wq": [D, D], "wk": [D, D], "wv": [D, D], "wo": [D, D],
                  "bo_row": [1, D], "projT": [64, 256]}




def emit_l1_ffn_rps(c, stream, W, B, half, cv):
    nc = c.nc
    xfm, stats, silu32 = B["xfm"], B["stats"], B["silu32"]
    w1b, w2b, rw3, rb1t, rb2t = B["w1b"], B["w2b"], B["rw3"], B["rb1t"], B["rb2t"]
    utb, iota8, selbuf, slotbuf = B["utb"], B["iota8"], B["selbuf"], B["slotbuf"]
    offs = B["offs"]
    p_h1, p_h2, p_small = B["p_h1"], B["p_h2"], B["p_small"]
    p_l3 = p_small[:, 0:32]
    p_pb = p_small[:, 32:40]
    p_cs = p_small[0:1, 40:48]
    dispT = B["dispT"]
    tiles = list(range(half * NTH, (half + 1) * NTH))

    h1 = cv.bf16([128, 32, 1024])
    h2sb = B["h2sb"]
    t8a, t8b, t8c = B["t8a"], B["t8b"], B["t8c"]
    c8b, sc8, m8 = B["c8b"], B["sc8"], B["m8"]
    i8 = B["i8"]
    xbf = B["xbf"]
    p_eo = B["p_eo"]
    pbs = [p_h1[:], p_h2[:], p_eo[:, 0:512], p_eo[:, 512:1024]]

    c.wait(nc.vector, "sp")
    emit_tm_to_fm(c, stream, tiles, xfm, B)
    ins = nc.vector.memset(offs[0:1, 0:8], 0.0)
    c.inc(ins, "dv")

    for tt in range(2):
        c.wait(nc.tensor, "ac")
        c.wait(nc.tensor, "sp")
        for H1c in range(32):
            wb = w1b[H1c % 2]
            c.guard_buf(nc.gpsimd, "l1f_w1b%d" % (H1c % 2))
            ins = nc.gpsimd.dma_start(
                out=wb[:],
                in_=W["rW1"][:, H1c * 128:(H1c + 1) * 128].rearrange(
                    "(cc p) m -> p cc m", p=128))
            c.inc(ins, "gd", 16)
            gd_w = c.n["gd"]
            nc.tensor.wait_ge(c.s["gd"], gd_w)
            for win in range(2):
                ttw = slice(tt * 1024 + win * 512, tt * 1024 + win * 512 + 512)
                pb = pbs[(2 * H1c + win) % 4]
                if H1c >= 1:
                    c.wait(nc.tensor, "ac", c.n["ac"] - 2)
                for cc in range(NCC):
                    lm = nc.tensor.matmul(pb, wb[:, cc, :], xfm[:, cc, ttw],
                                          start=(cc == 0), stop=(cc == NCC - 1))
                c.inc(lm, "pe")
                c.consumed("l1f_w1b%d" % (H1c % 2))
                c.wait(nc.scalar, "pe")
                ins = nc.scalar.activation(
                    out=h1[:, H1c, win * 512:(win + 1) * 512], in_=pb,
                    func=AF.Relu, bias=rb1t[:, H1c:H1c + 1])
                c.inc(ins, "ac")
        c.wait(nc.tensor, "ac")
        p_l3 = p_small[:, 0:64]
        pend_l3 = None
        relm = {}

        def emit_l3(pl):
            par, rm, h2c = pl
            nc.tensor.wait_ge(c.s["ac"], rm)
            for sl in range(8):
                lm = nc.tensor.matmul(
                    p_l3[:, sl * 8:(sl + 1) * 8],
                    h2sb[:, par, sl * 128:(sl + 1) * 128],
                    rw3[:, h2c, :], start=(h2c == 0 and sl == 0), stop=False,
                    skip_group_check=True)
            relm[h2c] = c.inc(lm, "pe")

        for H2c in range(32):
            wb = w2b[H2c % 2]
            par = H2c % 2
            c.guard_buf(nc.gpsimd, "l1f_w2b%d" % (H2c % 2))
            ins = nc.gpsimd.dma_start(
                out=wb[:, 0:16, :],
                in_=W["rW2"][0:2048, H2c * 128:(H2c + 1) * 128].rearrange(
                    "(hc p) m -> p hc m", p=128))
            c.inc(ins, "gd", 16)
            ins = nc.gpsimd.dma_start(
                out=wb[:, 16:32, :],
                in_=W["rW2"][2048:4096, H2c * 128:(H2c + 1) * 128].rearrange(
                    "(hc p) m -> p hc m", p=128))
            c.inc(ins, "gd", 16)
            gd_w = c.n["gd"]
            nc.tensor.wait_ge(c.s["gd"], gd_w)
            for win in range(2):
                pb = pbs[(2 * H2c + win) % 4]
                if H2c >= 1:
                    c.wait(nc.tensor, "ac", c.n["ac"] - 2)
                for hc in range(32):
                    lm = nc.tensor.matmul(
                        pb, wb[:, hc, :],
                        h1[:, hc, win * 512:(win + 1) * 512],
                        start=(hc == 0), stop=(hc == 31))
                c.inc(lm, "pe")
                c.consumed("l1f_w2b%d" % (H2c % 2))
                if win == 0 and pend_l3 is not None:
                    emit_l3(pend_l3)
                    pend_l3 = None
                c.wait(nc.scalar, "pe")
                prev = relm.get(H2c - 2)
                if prev is not None:
                    nc.scalar.wait_ge(c.s["pe"], prev)
                ins = nc.scalar.activation(
                    out=h2sb[:, par, win * 512:(win + 1) * 512], in_=pb,
                    func=AF.Relu, bias=rb2t[:, H2c:H2c + 1])
                c.inc(ins, "ac")
            pend_l3 = (par, c.n["ac"], H2c)
        emit_l3(pend_l3)
        for sl in range(8):
            lm = nc.tensor.matmul(p_l3[:, sl * 8:(sl + 1) * 8],
                                  B["onesrow"][0:1, 0:128], B["rb3r"][0:1, :],
                                  start=False, stop=(sl == 7),
                                  skip_group_check=True)
        c.inc(lm, "pe")
        for sl in range(8):
            ts = tt * 8 + sl
            l3 = p_l3[:, sl * 8:(sl + 1) * 8]
            c.wait(nc.vector, "pe")
            vw(c, nc.vector.reduce_max(stats[:, 0:1], l3, axis=AX.X))
            vw(c, nc.vector.tensor_scalar_mul(stats[:, 1:2], stats[:, 0:1], -1.0))
            c.wait(nc.scalar, "dv")
            ins = nc.scalar.activation(out=sc8[:], in_=l3, func=AF.Exp,
                                       bias=stats[:, 1:2],
                                       accum_out=stats[:, 2:3])
            c.inc(ins, "ac")
            c.wait(nc.vector, "ac")
            vw(c, nc.vector.reciprocal(stats[:, 3:4], stats[:, 2:3]))
            vw(c, nc.vector.tensor_scalar_mul(sc8[:], sc8[:], stats[:, 3:4]))
            vw(c, nc.vector.max(m8[:], sc8[:]))
            vw(c, nc.vector.max_index(i8[:], m8[:], sc8[:]))
            vw(c, nc.vector.tensor_copy(selbuf[:, half, ts, 0:1], i8[:, 0:1]))
            vw(c, nc.vector.tensor_copy(selbuf[:, half, ts, 1:2], i8[:, 1:2]))
            vw(c, nc.vector.tensor_copy(selbuf[:, half, ts, 2:3], m8[:, 0:1]))
            vw(c, nc.vector.tensor_copy(selbuf[:, half, ts, 3:4], m8[:, 1:2]))
    for ts in range(NTH):
        e1 = selbuf[:, half, ts, 0:1]
        e2 = selbuf[:, half, ts, 1:2]
        c.wait(nc.vector, "pe")
        nc.vector.tensor_scalar(out=t8a[:], in0=iota8[:], scalar1=e1,
                                scalar2=None, op0=ALU.is_equal)
        ins = nc.vector.tensor_scalar(out=t8b[:], in0=iota8[:], scalar1=e2,
                                      scalar2=None, op0=ALU.is_equal)
        c.inc(ins, "dv")
        c.wait(nc.vector, "dv")
        vw(c, nc.vector.tensor_add(t8c[:], t8a[:], t8b[:]))
        vw(c, nc.vector.tensor_copy(c8b[:], t8c[:]))
        c.wait(nc.tensor, "dv")
        nc.tensor.matmul(p_pb, utb[:], c8b[:], start=True, stop=False,
                         skip_group_check=True)
        ins = nc.tensor.matmul(p_pb, B["onesrow32"][0:1, 0:128],
                               offs[0:1, 0:8], start=False, stop=True,
                               skip_group_check=True)
        c.inc(ins, "pe")
        ins = nc.tensor.matmul(p_cs, B["onescol"][:, 0:1], c8b[:],
                               start=True, stop=True, skip_group_check=True)
        c.inc(ins, "pe")
        c.wait(nc.vector, "pe")
        if "dbg_pb" in B:
            nc.vector.tensor_copy(B["dbg_pb"][:], p_pb)
            nc.vector.tensor_copy(B["dbg_t8c"][:], t8c[:])
            nc.vector.tensor_copy(B["dbg_c8b"][:], c8b[:])
        vw(c, nc.vector.tensor_mul(t8a[:], t8a[:], p_pb))
        vw(c, nc.vector.reduce_sum(stats[:, 0:1], t8a[:], axis=AX.X))
        vw(c, nc.vector.tensor_mul(t8b[:], t8b[:], p_pb))
        vw(c, nc.vector.reduce_sum(stats[:, 1:2], t8b[:], axis=AX.X))
        vw(c, nc.vector.tensor_tensor(stats[:, 2:3], e1, e2, ALU.is_equal))
        vw(c, nc.vector.tensor_add(stats[:, 1:2], stats[:, 1:2], stats[:, 2:3]))
        vw(c, nc.vector.tensor_add(offs[0:1, 0:8], offs[0:1, 0:8], p_cs))
        for kk in range(2):
            pos = stats[:, kk:kk + 1]
            ekf = selbuf[:, half, ts, kk:kk + 1]
            skf = selbuf[:, half, ts, 2 + kk:3 + kk]
            keep = stats[:, 4:5]
            vw(c, nc.vector.tensor_scalar(out=keep, in0=pos, scalar1=float(CAP),
                                          scalar2=None, op0=ALU.is_lt))
            import os as _os
            if _os.environ.get("DBG_L1F_ZS"):
                vw(c, nc.vector.memset(skf, 0.0))
            else:
                vw(c, nc.vector.tensor_mul(skf, skf, keep))
            vw(c, nc.vector.tensor_scalar(out=stats[:, 5:6], in0=ekf,
                                          scalar1=640.0,
                                          scalar2=float(half * CAP),
                                          op0=ALU.mult, op1=ALU.add))
            vw(c, nc.vector.tensor_add(stats[:, 5:6], stats[:, 5:6], pos))
            vw(c, nc.vector.tensor_mul(stats[:, 5:6], stats[:, 5:6], keep))
            vw(c, nc.vector.tensor_scalar(out=stats[:, 6:7], in0=keep,
                                          scalar1=-float(E1 * 2 * CAP),
                                          scalar2=float(E1 * 2 * CAP),
                                          op0=ALU.mult, op1=ALU.add))
            vw(c, nc.vector.tensor_add(stats[:, 5:6], stats[:, 5:6], stats[:, 6:7]))
            vw(c, nc.vector.tensor_copy(slotbuf[:, half, ts, kk:kk + 1],
                                        stats[:, 5:6]))
        ins = nc.vector.tensor_copy(xbf[:], stream[:, tiles[ts], :])
        c.inc(ins, "dv")
        c.wait(nc.gpsimd, "dv")
        for kk in range(2):
            ins = nc.gpsimd.indirect_dma_start(
                out=dispT,
                out_offset=IndirectOffsetOnAxis(
                    ap=slotbuf[:, half, ts, kk:kk + 1], axis=0),
                in_=xbf[:], in_offset=None)
            c.inc(ins, "gd", 16)


def emit_l1_ffn_experts(c, W, B, cv):
    nc = c.nc
    ew1, ew2 = B["ew1"], B["ew2"]
    ebt1, ebt2 = B["ebt1"], B["ebt2"]
    df = B["df"]
    p_h1, p_h2, p_eo = B["p_h1"], B["p_h2"], B["p_eo"]
    dispT, eoutD = B["dispT"], B["eoutD"]
    NR = 2 * CAP

    hh = cv.bf16([128, 32, NR])
    efo = cv.f32([128, NR])
    tmr = cv.f32([128, 5, 128])

    for e in range(E1):
        c.wait(nc.sync, "ac")
        ins = nc.sync.dma_start(out=ebt1[:], in_=W["eb1t"][e])
        c.inc(ins, "ld", 16)
        ins = nc.sync.dma_start(out=ebt2[:], in_=W["eb2t"][e])
        c.inc(ins, "ld", 16)
        ld_b = c.n["ld"]
        c.wait(nc.scalar, "pe")
        c.wait(nc.scalar, "gd")
        for ccd in range(NCC):
            ins = nc.scalar.dma_start_transpose(
                df[:, ccd, :],
                dispT[e * NR:(e + 1) * NR, ccd * 128:(ccd + 1) * 128])
            c.inc(ins, "sd", 16)
        sd_df = c.n["sd"]
        ac_h = c.n["ac"]

        def _issue_w1(Hc):
            wbx = ew1[Hc % 2]
            c.guard_buf(nc.gpsimd, "l1f_w1b%d" % (Hc % 2))
            ins = nc.gpsimd.dma_start(
                out=wbx[:],
                in_=W["eW1"][e][:, Hc * 128:(Hc + 1) * 128].rearrange(
                    "(cc p) m -> p cc m", p=128))
            c.inc(ins, "gd", 16)
            return c.n["gd"]
        gd_marks1 = [_issue_w1(0), _issue_w1(1)]
        for Hc in range(32):
            wb = ew1[Hc % 2]
            gd_w = gd_marks1[Hc]
            nc.tensor.wait_ge(c.s["gd"], gd_w)
            nc.tensor.wait_ge(c.s["sd"], sd_df)
            if Hc >= 2:
                nc.tensor.wait_ge(c.s["ac"], ac_h + 2 * (Hc - 1))
            else:
                nc.tensor.wait_ge(c.s["ld"], ld_b)
                c.wait(nc.tensor, "ac")
                c.wait(nc.tensor, "dv")
            for nh in range(2):
                pb = p_h1 if nh == 0 else p_h2
                for cc in range(NCC):
                    lm = nc.tensor.matmul(
                        pb[:, 0:CAP], wb[:, cc, :],
                        df[:, cc, nh * CAP:(nh + 1) * CAP],
                        start=(cc == 0), stop=(cc == NCC - 1))
                c.inc(lm, "pe")
                c.consumed("l1f_w1b%d" % (Hc % 2))
                c.wait(nc.scalar, "pe")
                ins = nc.scalar.activation(out=hh[:, Hc, nh * CAP:(nh + 1) * CAP],
                                           in_=pb[:, 0:CAP], func=AF.Relu,
                                           bias=ebt1[:, Hc:Hc + 1])
                c.inc(ins, "ac")
            if Hc + 2 < 32:
                gd_marks1.append(_issue_w1(Hc + 2))
        c.wait(nc.tensor, "ac")

        def _issue_w2(fc):
            wbx = ew2[fc % 2]
            c.guard_buf(nc.gpsimd, "l1f_w2b%d" % (fc % 2))
            ins = nc.gpsimd.dma_start(
                out=wbx[:],
                in_=W["eW2"][e][:, fc * 128:(fc + 1) * 128].rearrange(
                    "(hc p) m -> p hc m", p=128))
            c.inc(ins, "gd", 16)
            return c.n["gd"]
        gd_marks2 = [_issue_w2(0), _issue_w2(1)]
        for fc in range(NCC):
            wb = ew2[fc % 2]
            gd_w = gd_marks2[fc]
            nc.tensor.wait_ge(c.s["gd"], gd_w)
            if fc >= 2:
                c.wait(nc.tensor, "ac")
                c.wait(nc.tensor, "dv")
            for nh in range(2):
                pb = p_h1 if nh == 0 else p_h2
                for Hc in range(32):
                    lm = nc.tensor.matmul(
                        pb[:, 0:CAP], wb[:, Hc, :],
                        hh[:, Hc, nh * CAP:(nh + 1) * CAP],
                        start=(Hc == 0), stop=(Hc == 31))
                c.inc(lm, "pe")
                c.consumed("l1f_w2b%d" % (fc % 2))
                c.wait(nc.scalar, "pe")
                ins = nc.scalar.activation(
                    out=efo[:, nh * CAP:(nh + 1) * CAP], in_=pb[:, 0:CAP],
                    func=AF.Identity, bias=ebt2[:, fc:fc + 1])
                c.inc(ins, "ac")
            if fc + 2 < NCC:
                gd_marks2.append(_issue_w2(fc + 2))
            c.wait(nc.tensor, "ac")
            for j in range(5):
                ins = nc.tensor.transpose(
                    p_eo[:, j * 128:(j + 1) * 128],
                    efo[:, j * 128:(j + 1) * 128], B["idf32"][:])
            c.inc(ins, "pe")
            c.wait(nc.vector, "pe")
            ins = nc.vector.tensor_copy(tmr[:], p_eo[:, 0:640].rearrange(
                "p (j m) -> p j m", j=5))
            c.inc(ins, "dv")
            c.wait(nc.gpsimd, "dv")
            ins = nc.gpsimd.dma_start(
                out=eoutD[e * NR:(e + 1) * NR,
                          fc * 128:(fc + 1) * 128].rearrange(
                    "(j p) m -> p j m", p=128),
                in_=tmr[:])
            c.inc(ins, "gd", 16)
    c.wait(nc.gpsimd, "gd")


def emit_l1_ffn_gather(c, stream, B, half, spilled):
    nc = c.nc
    selbuf, slotbuf = B["selbuf"], B["slotbuf"]
    eoutD = B["eoutD"]
    stats, silu32 = B["stats"], B["silu32"]
    tiles = list(range(half * NTH, (half + 1) * NTH))
    cv = Carver(stream.ap()[:, spilled * NTH:(spilled + 1) * NTH, :]
                .rearrange("p a b -> p (a b)"))
    gbufs = [cv.f32([128, 1024]) for _ in range(4)]
    c.wait(nc.vector, "sp")
    c.wait(nc.gpsimd, "sp")
    gm, am = {}, {}
    for i, (ts, kk) in enumerate((t, k) for t in range(NTH) for k in range(2)):
        if i >= 4:
            nc.gpsimd.wait_ge(c.s["dv"], am[i - 4])
        ins = nc.gpsimd.indirect_dma_start(
            out=gbufs[i % 4][:], out_offset=None,
            in_=eoutD,
            in_offset=IndirectOffsetOnAxis(
                ap=slotbuf[:, half, ts, kk:kk + 1], axis=0))
        gm[i] = c.inc(ins, "gd", 16)
        nc.vector.wait_ge(c.s["gd"], gm[i])
        ins = nc.vector.scalar_tensor_tensor(
            out=stream[:, tiles[ts], :], in0=gbufs[i % 4][:],
            scalar=selbuf[:, half, ts, 2 + kk:3 + kk],
            in1=stream[:, tiles[ts], :], op0=ALU.mult, op1=ALU.add)
        am[i] = c.inc(ins, "dv")
    c.wait(nc.scalar, "dv")
    c.wait(nc.scalar, "sp")
    emit_ln(c, stream, tiles, silu32, stats)


def emit_l1_ffn(c, stream, W, B, scrA, scrB):
    nc = c.nc
    ins = nc.gpsimd.dma_start(out=B["rw3"][:], in_=W["rW3"].rearrange(
        "(hc p) m -> p hc m", p=128))
    c.inc(ins, "gd", 16)
    ins = nc.sync.dma_start(out=B["rb1t"][:], in_=W["rb1t"])
    c.inc(ins, "ld", 16)
    ins = nc.sync.dma_start(out=B["rb2t"][:], in_=W["rb2t"])
    c.inc(ins, "ld", 16)
    ins = nc.gpsimd.dma_start(out=B["rb3r"][:], in_=W["rb3"])
    c.inc(ins, "gd", 16)
    ins = nc.gpsimd.dma_start(out=B["utb"][:], in_=B["cin_ut"])
    c.inc(ins, "gd", 16)
    ins = nc.sync.dma_start(out=B["iota8"][:], in_=B["cin_iota"])
    c.inc(ins, "ld", 16)
    ins = nc.vector.memset(B["onesrow"][:], 1.0)
    ins = nc.vector.memset(B["onescol"][:], 1.0)
    ins = nc.vector.memset(B["onesrow32"][:], 1.0)
    ins = nc.vector.memset(B["g"][:], 0.0)
    c.inc(ins, "dv")
    c.wait(nc.gpsimd, "dv")
    ins = nc.gpsimd.dma_start(out=B["eoutD"][E1 * 2 * CAP:E1 * 2 * CAP + 1, :],
                              in_=B["g"][0:1, :])
    c.inc(ins, "gd", 16)
    emit_join(c)

    emit_spill(c, stream, 1, scrB)
    cv = Carver(stream.ap()[:, NTH:NT, :].rearrange("p a b -> p (a b)"))
    emit_l1_ffn_rps(c, stream, W, B, 0, cv)
    emit_join(c)
    emit_spill(c, stream, 0, scrA)
    emit_unspill(c, stream, 1, scrB)
    cv = Carver(stream.ap()[:, 0:NTH, :].rearrange("p a b -> p (a b)"))
    emit_l1_ffn_rps(c, stream, W, B, 1, cv)
    emit_join(c)
    import os as _os
    if _os.environ.get("DBG_L1F"):
        ins = nc.vector.tensor_copy(
            stream[:, NTH, 0:128],
            B["selbuf"].ap().rearrange("p a b cdim -> p (a b cdim)"))
        c.inc(ins, "dv")
        ins = nc.vector.tensor_copy(
            stream[:, NTH + 1, 0:64],
            B["slotbuf"].ap().rearrange("p a b cdim -> p (a b cdim)"))
        c.inc(ins, "dv")
        ins = nc.vector.tensor_copy(stream[:, NTH + 2, 0:128], B["utb"][:])
        nc.vector.tensor_copy(stream[:, NTH + 3, 0:8], B["dbg_t8c"][:])
        nc.vector.tensor_copy(stream[:, NTH + 3, 8:16], B["dbg_c8b"][:])
        nc.vector.tensor_copy(stream[:, NTH + 3, 16:24], B["iota8"][:])
        nc.vector.tensor_copy(stream[0:1, NTH + 3, 24:32], B["offs"][0:1, 0:8])
        ins2 = nc.vector.tensor_copy(stream[:, NTH + 3, 32:40], B["dbg_pb"][:])
        c.inc(ins2, "dv")
        emit_join(c)
        emit_unspill(c, stream, 0, scrA)
        c.wait(nc.sync, "sp")
        emit_join(c)
        return
    cv = Carver(stream.ap()[:, 0:NTH, :].rearrange("p a b -> p (a b)"))
    emit_l1_ffn_experts(c, W, B, cv)
    emit_join(c)
    emit_l1_ffn_gather(c, stream, B, 1, 0)
    emit_join(c)
    emit_spill(c, stream, 1, scrB)
    emit_unspill(c, stream, 0, scrA)
    emit_l1_ffn_gather(c, stream, B, 0, 1)
    emit_join(c)
    emit_unspill(c, stream, 1, scrB)
    c.wait(nc.sync, "sp")
    emit_join(c)


def prep_l1_ffn_weights(inp):
    rb1 = np.asarray(inp["L1_ffn_rb1"])
    rb2 = np.asarray(inp["L1_ffn_rb2"])
    eb1 = np.asarray(inp["L1_ffn_eb1"])
    return {
        "rW1": np.ascontiguousarray(np.asarray(inp["L1_ffn_rW1"])),
        "rW2": np.ascontiguousarray(np.asarray(inp["L1_ffn_rW2"])),
        "rW3": np.ascontiguousarray(np.asarray(inp["L1_ffn_rW3"])),
        "rb1t": np.ascontiguousarray(rb1.reshape(32, 128).T),
        "rb2t": np.ascontiguousarray(rb2.reshape(32, 128).T),
        "rb3": np.ascontiguousarray(np.asarray(inp["L1_ffn_rb3"]).reshape(1, 8)),
        "eW1": np.ascontiguousarray(np.asarray(inp["L1_ffn_eW1"])),
        "eW2": np.ascontiguousarray(np.asarray(inp["L1_ffn_eW2"])),
        "eb1t": np.ascontiguousarray(
            eb1.reshape(E1, 32, 128).transpose(0, 2, 1)),
        "eb2t": np.ascontiguousarray(
            np.asarray(inp["L1_ffn_eb2"]).reshape(E1, 8, 128).transpose(0, 2, 1)),
    }


L1_FFN_SHAPES = {"rW1": [D, H1R], "rW2": [H1R, H1R], "rW3": [H1R, 8],
                 "rb1t": [128, 32], "rb2t": [128, 32], "rb3": [1, 8],
                 "eW1": [E1, D, H1E], "eW2": [E1, H1E, D],
                 "eb1t": [E1, 128, 32], "eb2t": [E1, 128, 8]}


CAP0 = 768


def emit_l0_sparse_rps(c, stream, W, B, half, cv):
    nc = c.nc
    xfm, stats, silu32 = B["xfm"], B["stats"], B["silu32"]
    rt, utb, iota8 = B["rt"], B["utb"], B["iota8"]
    selbuf, slotbuf, offs = B["selbuf"], B["slotbuf"], B["offs"]
    t8a, t8c, c8b = B["t8a"], B["t8c"], B["c8b"]
    m8, i8 = B["m8"], B["i8"]
    xbf = B["xcast"][0]
    p_u, p_g, p_y = B["p_u"], B["p_g"], B["p_y"]
    p_s = B["p_s"]
    p_r = p_s[:, 0:8]
    p_pb = p_s[:, 32:40]
    p_cs = p_s[0:1, 40:48]
    dispL0 = B["dispL0"]
    tiles = list(range(half * NTH, (half + 1) * NTH))

    hstore = cv.bf16([128, 16, 2048])

    c.wait(nc.vector, "sp")
    emit_tm_to_fm(c, stream, tiles, xfm, B)
    c.wait(nc.tensor, "gd")
    c.wait(nc.tensor, "ac")
    c.wait(nc.tensor, "sp")
    for i in range(NTH):
        ti = half * NTH + i
        c.wait(nc.tensor, "dv")
        for cc in range(NCC):
            last = nc.tensor.matmul(p_r, xfm[:, cc, i * 128:(i + 1) * 128],
                                    rt[:, cc, :], start=(cc == 0),
                                    stop=(cc == NCC - 1), skip_group_check=True)
        c.inc(last, "pe")
        c.wait(nc.vector, "pe")
        vw(c, nc.vector.max(m8[:], p_r))
        vw(c, nc.vector.max_index(i8[:], m8[:], p_r))
        vw(c, nc.vector.tensor_copy(selbuf[:, ti, 0:1], i8[:, 0:1]))
        e1 = selbuf[:, ti, 0:1]
        vw(c, nc.vector.tensor_scalar(out=t8a[:], in0=iota8[:], scalar1=e1,
                                      scalar2=None, op0=ALU.is_equal))
        vw(c, nc.vector.tensor_copy(c8b[:], t8a[:]))
        c.wait(nc.tensor, "dv")
        nc.tensor.matmul(p_pb, utb[:], c8b[:], start=True, stop=False,
                         skip_group_check=True)
        ins = nc.tensor.matmul(p_pb, B["onesrow32"][0:1, 0:128],
                               offs[0:1, 0:8], start=False, stop=True,
                               skip_group_check=True)
        c.inc(ins, "pe")
        ins = nc.tensor.matmul(p_cs, B["onescol"][:, 0:1], c8b[:],
                               start=True, stop=True, skip_group_check=True)
        c.inc(ins, "pe")
        c.wait(nc.vector, "pe")
        vw(c, nc.vector.tensor_mul(t8c[:], t8a[:], p_pb))
        vw(c, nc.vector.reduce_sum(stats[:, 0:1], t8c[:], axis=AX.X))
        vw(c, nc.vector.tensor_add(offs[0:1, 0:8], offs[0:1, 0:8], p_cs))
        pos = stats[:, 0:1]
        keep = stats[:, 4:5]
        vw(c, nc.vector.tensor_scalar(out=keep, in0=pos, scalar1=float(CAP0),
                                      scalar2=None, op0=ALU.is_lt))
        vw(c, nc.vector.tensor_scalar(out=stats[:, 5:6], in0=e1,
                                      scalar1=float(CAP0), scalar2=None,
                                      op0=ALU.mult))
        vw(c, nc.vector.tensor_add(stats[:, 5:6], stats[:, 5:6], pos))
        vw(c, nc.vector.tensor_mul(stats[:, 5:6], stats[:, 5:6], keep))
        vw(c, nc.vector.tensor_scalar(out=stats[:, 6:7], in0=keep,
                                      scalar1=-float(E0 * CAP0),
                                      scalar2=float(E0 * CAP0),
                                      op0=ALU.mult, op1=ALU.add))
        vw(c, nc.vector.tensor_add(stats[:, 5:6], stats[:, 5:6], stats[:, 6:7]))
        vw(c, nc.vector.tensor_copy(slotbuf[:, ti, 0:1], stats[:, 5:6]))
        vw(c, nc.vector.tensor_copy(xbf[:], stream[:, ti, :]))
        c.wait(nc.gpsimd, "dv")
        ins = nc.gpsimd.indirect_dma_start(
            out=dispL0,
            out_offset=IndirectOffsetOnAxis(ap=slotbuf[:, ti, 0:1], axis=0),
            in_=xbf[:], in_offset=None)
        c.inc(ins, "gd", 16)

    upT, gateT, downT = W["up_sT"], W["gate_sT"], W["down_sT"]
    wug, wdn = B["wug"], B["wdn"]
    pe_b, dv_b, ac_b = c.n["pe"], c.n["dv"], c.n["ac"]
    k = 0
    for hp in range(8):
        wb = wug[hp % 2]
        c.guard_buf(nc.gpsimd, "l0f_wug%d" % (hp % 2))
        ins = nc.gpsimd.dma_start(
            out=wb[:, :, 0:256],
            in_=upT[:, hp * 256:(hp + 1) * 256].rearrange(
                "(cc p) hh -> p cc hh", p=128))
        c.inc(ins, "gd", 16)
        ins = nc.gpsimd.dma_start(
            out=wb[:, :, 256:512],
            in_=gateT[:, hp * 256:(hp + 1) * 256].rearrange(
                "(cc p) hh -> p cc hh", p=128))
        c.inc(ins, "gd", 16)
        gd_mark = c.n["gd"]
        for hh in range(2):
            hc = hp * 2 + hh
            for hw in range(4):
                tw = slice(hw * 512, (hw + 1) * 512)
                nc.tensor.wait_ge(c.s["gd"], gd_mark)
                if k >= 1:
                    nc.tensor.wait_ge(c.s["ac"], ac_b + k)
                    nc.tensor.wait_ge(c.s["dv"], dv_b + k)
                else:
                    c.wait(nc.tensor, "ac", ac_b)
                    c.wait(nc.tensor, "dv", dv_b)
                for cc in range(NCC):
                    nc.tensor.matmul(
                        p_u[:], wb[:, cc, hh * 128:(hh + 1) * 128],
                        xfm[:, cc, tw], start=(cc == 0),
                        stop=(cc == NCC - 1))
                for cc in range(NCC):
                    lg = nc.tensor.matmul(
                        p_g[:], wb[:, cc, 256 + hh * 128:256 + (hh + 1) * 128],
                        xfm[:, cc, tw], start=(cc == 0),
                        stop=(cc == NCC - 1))
                c.inc(lg, "pe")
                c.consumed("l0f_wug%d" % (hp % 2))
                c.wait(nc.scalar, "pe")
                if k >= 1:
                    c.wait(nc.scalar, "dv", dv_b + k)
                ins = nc.scalar.activation(out=silu32[:, 0:512],
                                           in_=p_g[:], func=AF.Silu)
                c.inc(ins, "ac")
                c.wait(nc.vector, "ac")
                ins = nc.vector.tensor_mul(hstore[:, hc, tw], p_u[:],
                                           silu32[:, 0:512])
                c.inc(ins, "dv")
                k += 1
    for ch in range(2):
        cw = slice(ch * 512, (ch + 1) * 512)
        c.guard_buf(nc.gpsimd, "l0f_wdn")
        ins = nc.gpsimd.dma_start(
            out=wdn[:, 0:8, :],
            in_=downT[0:1024, ch * 512:(ch + 1) * 512].rearrange(
                "(hc p) cf -> p hc cf", p=128))
        c.inc(ins, "gd", 16)
        ins = nc.gpsimd.dma_start(
            out=wdn[:, 8:16, :],
            in_=downT[1024:2048, ch * 512:(ch + 1) * 512].rearrange(
                "(hc p) cf -> p hc cf", p=128))
        c.inc(ins, "gd", 16)
        gd_dn = c.n["gd"]
        dv_c = c.n["dv"]
        for ts in range(NTH):
            pb = p_y[ts % 2]
            tile = tiles[ts]
            nc.tensor.wait_ge(c.s["gd"], gd_dn)
            c.wait(nc.tensor, "dv", dv_b + 64)
            if ts >= 2:
                c.wait(nc.tensor, "dv", dv_c + ts - 1)
            else:
                c.wait(nc.tensor, "dv", dv_c)
            for hc in range(16):
                last = nc.tensor.matmul(
                    pb[:], hstore[:, hc, ts * 128:(ts + 1) * 128],
                    wdn[:, hc, :], start=(hc == 0), stop=(hc == 15))
            c.inc(last, "pe")
            c.consumed("l0f_wdn")
            c.wait(nc.vector, "pe")
            ins = nc.vector.tensor_add(stream[:, tile, cw], pb[:],
                                       stream[:, tile, cw])
            c.inc(ins, "dv")


def emit_l0_sparse_experts(c, W, B, cv):
    nc = c.nc
    wug, wdn = B["wug"], B["wdn"]
    p_u, p_g, p_y = B["p_u"], B["p_g"], B["p_y"]
    dispL0, eoutL0 = B["dispL0"], B["eoutL0"]
    NS = CAP0 // 128

    hst = cv.bf16([128, 16, CAP0])
    efoAB = [cv.f32([128, NS, 512]), cv.f32([128, NS, 512])]
    df = cv.bf16([128, NCC, CAP0])

    for e in range(E0):
        upT, gateT, downT = W["upT"][e], W["gateT"][e], W["downT"][e]
        c.wait(nc.scalar, "pe")
        c.wait(nc.scalar, "gd")
        for ccd in range(NCC):
            ins = nc.scalar.dma_start_transpose(
                df[:, ccd, :],
                dispL0[e * CAP0:(e + 1) * CAP0, ccd * 128:(ccd + 1) * 128])
            c.inc(ins, "sd", 16)
        sd_df = c.n["sd"]
        ac_h = c.n["ac"]
        k = 0
        for hp in range(8):
            wb = wug[hp % 2]
            c.guard_buf(nc.gpsimd, "l0f_wug%d" % (hp % 2))
            ins = nc.gpsimd.dma_start(
                out=wb[:, :, 0:256],
                in_=upT[:, hp * 256:(hp + 1) * 256].rearrange(
                    "(cc p) hh -> p cc hh", p=128))
            c.inc(ins, "gd", 16)
            ins = nc.gpsimd.dma_start(
                out=wb[:, :, 256:512],
                in_=gateT[:, hp * 256:(hp + 1) * 256].rearrange(
                    "(cc p) hh -> p cc hh", p=128))
            c.inc(ins, "gd", 16)
            gd_mark = c.n["gd"]
            for hh in range(2):
                hc = hp * 2 + hh
                for nh in range(2):
                    ns = slice(nh * 384, (nh + 1) * 384)
                    nc.tensor.wait_ge(c.s["gd"], gd_mark)
                    nc.tensor.wait_ge(c.s["sd"], sd_df)
                    if k >= 1:
                        nc.tensor.wait_ge(c.s["ac"], ac_h + k)
                        c.wait(nc.tensor, "dv")
                    else:
                        c.wait(nc.tensor, "ac")
                        c.wait(nc.tensor, "dv")
                    for cc in range(NCC):
                        nc.tensor.matmul(
                            p_u[:, 0:384], wb[:, cc, hh * 128:(hh + 1) * 128],
                            df[:, cc, ns], start=(cc == 0),
                            stop=(cc == NCC - 1))
                    for cc in range(NCC):
                        lg = nc.tensor.matmul(
                            p_g[:, 0:384],
                            wb[:, cc, 256 + hh * 128:256 + (hh + 1) * 128],
                            df[:, cc, ns], start=(cc == 0),
                            stop=(cc == NCC - 1))
                    c.inc(lg, "pe")
                    c.consumed("l0f_wug%d" % (hp % 2))
                    c.wait(nc.scalar, "pe")
                    if k >= 1:
                        c.wait(nc.scalar, "dv", c.n["dv"])
                    ins = nc.scalar.activation(out=B["silu32"][:, 0:384],
                                               in_=p_g[:, 0:384], func=AF.Silu)
                    c.inc(ins, "ac")
                    c.wait(nc.vector, "ac")
                    ins = nc.vector.tensor_mul(hst[:, hc, ns], p_u[:, 0:384],
                                               B["silu32"][:, 0:384])
                    c.inc(ins, "dv")
                    k += 1
        dma_marks = []
        for ch in range(2):
            c.guard_buf(nc.gpsimd, "l0f_wdn")
            ins = nc.gpsimd.dma_start(
                out=wdn[:, 0:8, :],
                in_=downT[0:1024, ch * 512:(ch + 1) * 512].rearrange(
                    "(hc p) cf -> p hc cf", p=128))
            c.inc(ins, "gd", 16)
            ins = nc.gpsimd.dma_start(
                out=wdn[:, 8:16, :],
                in_=downT[1024:2048, ch * 512:(ch + 1) * 512].rearrange(
                    "(hc p) cf -> p hc cf", p=128))
            c.inc(ins, "gd", 16)
            gd_dn = c.n["gd"]
            dv_c = c.n["dv"]
            for ts in range(NS):
                pb = p_y[ts % 2]
                nc.tensor.wait_ge(c.s["gd"], gd_dn)
                c.wait(nc.tensor, "dv", dv_c if ts < 2 else dv_c + ts - 1)
                for hc in range(16):
                    last = nc.tensor.matmul(
                        pb[:], hst[:, hc, ts * 128:(ts + 1) * 128],
                        wdn[:, hc, :], start=(hc == 0), stop=(hc == 15))
                c.inc(last, "pe")
                c.consumed("l0f_wdn")
                c.wait(nc.vector, "pe")
                c.wait(nc.vector, "gd")
                ins = nc.vector.tensor_copy(efoAB[ch][:, ts, :], pb[:])
                c.inc(ins, "dv")
        c.wait(nc.gpsimd, "dv")
        for ch in range(2):
            ins = nc.gpsimd.dma_start(
                out=eoutL0[e * CAP0:(e + 1) * CAP0,
                           ch * 512:(ch + 1) * 512].rearrange(
                    "(j p) m -> p j m", p=128),
                in_=efoAB[ch][:])
            c.inc(ins, "gd", 16)
    c.wait(nc.gpsimd, "gd")


def emit_l0_sparse_gather(c, stream, B, half, spilled):
    nc = c.nc
    slotbuf = B["slotbuf"]
    eoutL0 = B["eoutL0"]
    stats, silu32 = B["stats"], B["silu32"]
    tiles = list(range(half * NTH, (half + 1) * NTH))
    cv = Carver(stream.ap()[:, spilled * NTH:(spilled + 1) * NTH, :]
                .rearrange("p a b -> p (a b)"))
    gbufs = [cv.f32([128, 1024]) for _ in range(4)]
    c.wait(nc.vector, "sp")
    c.wait(nc.gpsimd, "sp")
    gm, am = {}, {}
    for ts in range(NTH):
        ti = tiles[ts]
        if ts >= 4:
            nc.gpsimd.wait_ge(c.s["dv"], am[ts - 4])
        ins = nc.gpsimd.indirect_dma_start(
            out=gbufs[ts % 4][:], out_offset=None,
            in_=eoutL0,
            in_offset=IndirectOffsetOnAxis(ap=slotbuf[:, ti, 0:1], axis=0))
        gm[ts] = c.inc(ins, "gd", 16)
        nc.vector.wait_ge(c.s["gd"], gm[ts])
        ins = nc.vector.tensor_add(stream[:, ti, :], gbufs[ts % 4][:],
                                   stream[:, ti, :])
        am[ts] = c.inc(ins, "dv")
    c.wait(nc.scalar, "dv")
    c.wait(nc.scalar, "sp")
    emit_ln(c, stream, tiles, silu32, stats)


def emit_l0_ffn_sparse(c, stream, W, B, scrA, scrB):
    nc = c.nc
    ins = nc.gpsimd.dma_start(out=B["rt"][:], in_=W["routerT_tiled"])
    c.inc(ins, "gd", 16)
    ins = nc.gpsimd.dma_start(out=B["utb"][:], in_=B["cin_ut"])
    c.inc(ins, "gd", 16)
    ins = nc.sync.dma_start(out=B["iota8"][:], in_=B["cin_iota"])
    c.inc(ins, "ld", 16)
    nc.vector.memset(B["onesrow32"][:], 1.0)
    nc.vector.memset(B["onescol"][:], 1.0)
    nc.vector.memset(B["offs"][0:1, 0:8], 0.0)
    ins = nc.vector.memset(B["g"][:], 0.0)
    c.inc(ins, "dv")
    c.wait(nc.gpsimd, "dv")
    ins = nc.gpsimd.dma_start(out=B["eoutL0"][E0 * CAP0:E0 * CAP0 + 1, :],
                              in_=B["g"][0:1, :])
    c.inc(ins, "gd", 16)
    emit_join(c)

    import os as _os
    _noexp = bool(_os.environ.get("L0SP_NOEXP"))
    emit_spill(c, stream, 1, scrB)
    cv = Carver(stream.ap()[:, NTH:NT, :].rearrange("p a b -> p (a b)"))
    emit_l0_sparse_rps(c, stream, W, B, 0, cv)
    emit_join(c)
    emit_spill(c, stream, 0, scrA)
    emit_unspill(c, stream, 1, scrB)
    cv = Carver(stream.ap()[:, 0:NTH, :].rearrange("p a b -> p (a b)"))
    emit_l0_sparse_rps(c, stream, W, B, 1, cv)
    emit_join(c)
    if _noexp:
        emit_ln(c, stream, list(range(NTH, NT)), B["silu32"], B["stats"])
        emit_join(c)
        emit_spill(c, stream, 1, scrB)
        emit_unspill(c, stream, 0, scrA)
        emit_ln(c, stream, list(range(0, NTH)), B["silu32"], B["stats"])
        emit_join(c)
        emit_unspill(c, stream, 1, scrB)
        c.wait(nc.sync, "sp")
        emit_join(c)
        return
    cv = Carver(stream.ap()[:, 0:NTH, :].rearrange("p a b -> p (a b)"))
    emit_l0_sparse_experts(c, W, B, cv)
    emit_join(c)
    emit_l0_sparse_gather(c, stream, B, 1, 0)
    emit_join(c)
    emit_spill(c, stream, 1, scrB)
    emit_unspill(c, stream, 0, scrA)
    emit_l0_sparse_gather(c, stream, B, 0, 1)
    emit_join(c)
    emit_unspill(c, stream, 1, scrB)
    c.wait(nc.sync, "sp")
    emit_join(c)


def prep_l0_ffn_weights(inp):
    up = np.asarray(inp["L0_ffn_up"])
    gate = np.asarray(inp["L0_ffn_gate"])
    down = np.asarray(inp["L0_ffn_down"])
    router = np.asarray(inp["L0_ffn_router"])
    rt = np.ascontiguousarray(
        router.T.reshape(NCC, 128, E0).transpose(1, 0, 2))
    return {
        "upT": np.ascontiguousarray(up.transpose(0, 2, 1)),
        "gateT": np.ascontiguousarray(gate.transpose(0, 2, 1)),
        "downT": np.ascontiguousarray(down.transpose(0, 2, 1)),
        "routerT_tiled": rt,
        "up_sT": np.ascontiguousarray(np.asarray(inp["L0_ffn_w_up_s"]).T),
        "gate_sT": np.ascontiguousarray(np.asarray(inp["L0_ffn_w_gate_s"]).T),
        "down_sT": np.ascontiguousarray(np.asarray(inp["L0_ffn_w_down_s"]).T),
    }


L0_FFN_SHAPES = {"upT": [E0, D, H0], "gateT": [E0, D, H0], "downT": [E0, H0, D],
                 "routerT_tiled": [128, NCC, E0], "up_sT": [D, H0],
                 "gate_sT": [D, H0], "down_sT": [H0, D]}


def build_launch(block_names):
    nc = bass.Bass("TRN2", target_bir_lowering=False)
    consts = build_consts()
    xin = nc.dram_tensor("xin", [T, D], F32, kind="ExternalInput").ap()
    xout = nc.dram_tensor("xout", [T, D], F32, kind="ExternalOutput").ap()
    cin = {k: nc.dram_tensor("c_" + k, list(v.shape), F32,
                             kind="ExternalInput").ap()
           for k, v in consts.items()}
    scrA = nc.dram_tensor("scrA", [128, NTH, D], F32, kind="Internal").ap()
    scrB = nc.dram_tensor("scrB", [128, NTH, D], F32, kind="Internal").ap()
    W = {}
    cc_in = nc.dram_tensor("cc_in", [1, 8], F32, kind="Internal").ap()
    cc_out = nc.dram_tensor("cc_out", [1, 8], F32, kind="Internal").ap()
    if "l0_attn" in block_names:
        W["l0_attn"] = {k: nc.dram_tensor("l0a_" + k, s, F32,
                                          kind="ExternalInput").ap()
                        for k, s in L0_ATTN_SHAPES.items()}
    if "l1_attn" in block_names:
        W["l1_attn"] = {k: nc.dram_tensor("l1a_" + k, s, F32,
                                          kind="ExternalInput").ap()
                        for k, s in L1_ATTN_SHAPES.items()}
    if "l0_ffn" in block_names:
        W["l0_ffn"] = {k: nc.dram_tensor("l0f_" + k, s, F32,
                                         kind="ExternalInput").ap()
                       for k, s in L0_FFN_SHAPES.items()}
        W["l0_ffn"]["dispL0"] = nc.dram_tensor(
            "dispL0", [E0 * CAP0 + 1, D], BF16, kind="Internal").ap()
        W["l0_ffn"]["eoutL0"] = nc.dram_tensor(
            "eoutL0", [E0 * CAP0 + 1, D], F32, kind="Internal").ap()
    if "l1_ffn" in block_names:
        W["l1_ffn"] = {k: nc.dram_tensor("l1f_" + k, s, F32,
                                         kind="ExternalInput").ap()
                       for k, s in L1_FFN_SHAPES.items()}
        W["l1_ffn"]["dispT"] = nc.dram_tensor(
            "dispT", [E1 * 2 * CAP + 1, D], BF16, kind="Internal").ap()
        W["l1_ffn"]["eoutD"] = nc.dram_tensor(
            "eoutD", [E1 * 2 * CAP + 1, D], F32, kind="Internal").ap()

    sem_names = ["ld", "gd", "sd", "pe", "dv", "ac", "st", "cc", "sp"]
    with contextlib.ExitStack() as stack:
        sems = {n: stack.enter_context(nc.semaphore(n)) for n in sem_names}
        c = Ctx(nc, sems)
        with (
            nc.sbuf_tensor("stream", [128, NT, D], F32) as stream,
            nc.sbuf_tensor("xfm", [128, NCC, 2048], BF16) as xfm,
            nc.sbuf_tensor("xcast0", [128, D], BF16) as xcast0,
            nc.sbuf_tensor("xcast1", [128, D], BF16) as xcast1,
            nc.sbuf_tensor("idbf", [128, 128], BF16) as idbf,
            nc.sbuf_tensor("idf32", [128, 128], F32) as idf32,
            nc.sbuf_tensor("silu32", [128, 1024], F32) as silu32,
            nc.sbuf_tensor("causal", [128, 128], F32) as causal,
            nc.sbuf_tensor("stats", [128, 8], F32) as stats,
            nc.Block() as block,
        ):
            @block.sync
            def _(sync):
                ins = nc.gpsimd.dma_start(out=idbf[:], in_=cin["idbf"])
                c.inc(ins, "gd", 16)
                ins = nc.sync.dma_start(out=idf32[:], in_=cin["idf32"])
                c.inc(ins, "ld", 16)
                ins = nc.sync.dma_start(out=causal[:], in_=cin["causal_add"])
                c.inc(ins, "ld", 16)
                emit_stream_load(c, xin, stream)
                for eng in (nc.vector, nc.tensor, nc.scalar, nc.gpsimd):
                    c.wait(eng, "ld")
                    c.wait(eng, "gd")

                B = {"xcast": [xcast0, xcast1], "idbf": idbf, "idf32": idf32,
                     "stats": stats, "silu32": silu32, "xfm": xfm,
                     "causal": causal}
                for bn in block_names:
                    if bn == "l0_ffn":
                        with contextlib.ExitStack() as st:
                            def sb(name, shape, dt, **kw):
                                return st.enter_context(
                                    nc.sbuf_tensor(name, shape, dt, **kw))
                            def ps(name, shape, dt):
                                return st.enter_context(
                                    nc.psum_tensor(name, shape, dt))
                            Bl = dict(
                                B,
                                wug=[sb("wug0", [128, NCC, 512], BF16),
                                     sb("wug1", [128, NCC, 512], BF16)],
                                wdn=sb("wdn", [128, 16, 512], BF16),
                                rt=sb("rt", [128, NCC, 8], BF16),
                                utb=sb("utb0", [128, 128], BF16),
                                iota8=sb("iota80", [128, 8], F32),
                                selbuf=sb("selbuf0", [128, NT, 1], F32),
                                slotbuf=sb("slotbuf0", [128, NT, 1], I32),
                                offs=sb("offs0", [1, 16], F32),
                                onesrow32=sb("onesrow320", [1, 128], F32),
                                onescol=sb("onescol0", [128, 1], BF16),
                                t8a=sb("t8a0", [128, 8], F32),
                                t8c=sb("t8c0", [128, 8], F32),
                                c8b=sb("c8b0", [128, 8], BF16),
                                m8=sb("m80", [128, 8], F32),
                                i8=sb("i80", [128, 8], U32),
                                g=silu32,
                                dispL0=W["l0_ffn"]["dispL0"],
                                eoutL0=W["l0_ffn"]["eoutL0"],
                                cin_ut=cin["ut_strict"],
                                cin_iota=cin["iota8"],
                                p_tr=[ps("p_tr0", [128, 512], BF16),
                                      ps("p_tr1", [128, 512], BF16)],
                                p_u=ps("p_u", [128, 512], F32),
                                p_g=ps("p_g", [128, 512], F32),
                                p_y=[ps("p_y0", [128, 512], F32),
                                     ps("p_y1", [128, 512], F32)],
                                p_s=ps("p_s", [128, 64], F32))
                            emit_l0_ffn_sparse(c, stream, W["l0_ffn"], Bl,
                                               scrA, scrB)
                    elif bn == "l0_attn":
                        with contextlib.ExitStack() as st:
                            def sb(name, shape, dt):
                                return st.enter_context(
                                    nc.sbuf_tensor(name, shape, dt))
                            def ps(name, shape, dt):
                                return st.enter_context(
                                    nc.psum_tensor(name, shape, dt))
                            Bl = dict(
                                B,
                                wvt=sb("wvt0", [128, NCC, 1024], BF16),
                                wqk=[sb("wqkA0", [128, 8, 128], BF16),
                                     sb("wqkA1", [128, 8, 128], BF16)],
                                paug=sb("paug", [128, 257], BF16),
                                paugA=sb("paugA", [128, 257], BF16),
                                paugB=sb("paugB", [128, 257], BF16),
                                pload=sb("pload", [128, 256], F32),
                                ctxbuf=sb("ctxbuf", [128, 2, 16, 2, 65], BF16),
                                vsumbuf=sb("vsumbuf", [128, 2112], BF16),
                                kmax=sb("kmax", [128, 4], F32),
                                kmbf=sb("kmbf", [128, 2], BF16),
                                acol=sb("acol", [128, 1], F32),
                                onesrow=sb("onesrow", [128, 128], BF16),
                                borow=sb("borow", [1, 1024], BF16),
                                p_a=ps("p0_a", [128, 512], F32),
                                p_b=ps("p0_b", [128, 512], F32),
                                p_dd=ps("p0_dd", [128, 512], F32),
                                p_dd2=ps("p0_dd2", [128, 512], F32),
                                p_c0=ps("p0_c0", [128, 512], F32),
                                p_c1=ps("p0_c1", [128, 512], F32),
                                p_tr=[ps("p0_tr0", [128, 512], BF16),
                                      ps("p0_tr1", [128, 512], BF16)])
                            emit_l0_attn(c, stream, W["l0_attn"], Bl,
                                         scrA, scrB, cc_in, cc_out)
                    elif bn == "l1_attn":
                        with (
                            nc.sbuf_tensor("wvt", [128, NCC, 1024], BF16) as wvt,
                            nc.sbuf_tensor("wqk0", [128, 16, 128], BF16) as wqk0,
                            nc.sbuf_tensor("wqk1", [128, 16, 128], BF16) as wqk1,
                            nc.sbuf_tensor("wotF", [128, 8, 512], BF16) as wotF,
                            nc.psum_tensor("pa_a", [128, 512], F32) as p_a,
                            nc.psum_tensor("pa_b", [128, 512], F32) as p_b,
                            nc.psum_tensor("pa_z0", [128, 512], F32) as p_z0,
                            nc.psum_tensor("pa_z1", [128, 512], F32) as p_z1,
                            nc.psum_tensor("pa_o", [128, 512], F32) as p_o,
                            nc.psum_tensor("pa_to", [128, 512], F32) as p_to,
                            nc.psum_tensor("pa_tr0", [128, 512], BF16) as pa_tr0,
                            nc.psum_tensor("pa_tr1", [128, 512], BF16) as pa_tr1,
                        ):
                            Bl = dict(B, wvt=wvt, wqk=[wqk0, wqk1],
                                      wot_full=wotF, p_a=p_a, p_b=p_b,
                                      p_z0=p_z0, p_z1=p_z1, p_o=p_o, p_to=p_to,
                                      p_tr=[pa_tr0, pa_tr1])
                            emit_block_halves(
                                c, stream, scrA, scrB,
                                lambda half, cv: emit_l1_attn_half(
                                    c, stream, W["l1_attn"], Bl, half, cv))
                    elif bn == "l1_ffn":
                        with contextlib.ExitStack() as st:
                            def sb(name, shape, dt, **kw):
                                return st.enter_context(
                                    nc.sbuf_tensor(name, shape, dt, **kw))
                            def ps(name, shape, dt):
                                return st.enter_context(
                                    nc.psum_tensor(name, shape, dt))
                            Wf = W["l1_ffn"]
                            Bl = dict(
                                B,
                                w1b=[sb("w1b0", [128, NCC, 128], BF16),
                                     sb("w1b1", [128, NCC, 128], BF16)],
                                w2b=[sb("w2b0", [128, 32, 128], BF16),
                                     sb("w2b1", [128, 32, 128], BF16)],
                                rw3=sb("rw3", [128, 32, 8], BF16),
                                rb1t=sb("rb1t", [128, 32], F32),
                                rb2t=sb("rb2t", [128, 32], F32),
                                rb3r=sb("rb3r", [1, 8], BF16),
                                utb=sb("utb", [128, 128], BF16),
                                iota8=sb("iota8", [128, 8], F32),
                                selbuf=sb("selbuf", [128, 2, NTH, 4], F32),
                                slotbuf=sb("slotbuf", [128, 2, NTH, 2], I32),
                                offs=sb("offs", [1, 16], F32),
                                onesrow=sb("onesrowF", [1, 128], BF16),
                                onescol=sb("onescolF", [128, 1], BF16),
                                onesrow32=sb("onesrow32", [1, 128], F32),
                                ebt1=sb("ebt1", [128, 32], F32),
                                ebt2=sb("ebt2", [128, 8], F32),
                                df=sb("df", [128, NCC, 2 * CAP], BF16,
                                      is_transpose_dest=True),
                                g=silu32,
                                h2sb=sb("h2sb", [128, 2, 1024], BF16),
                                t8a=sb("t8a", [128, 8], F32),
                                t8b=sb("t8b", [128, 8], F32),
                                t8c=sb("t8c", [128, 8], F32),
                                c8b=sb("c8b", [128, 8], BF16),
                                sc8=sb("sc8", [128, 8], F32),
                                m8=sb("m8", [128, 8], F32),
                                i8=sb("i8", [128, 8], U32),

                                dispT=Wf["dispT"], eoutD=Wf["eoutD"],
                                cin_ut=cin["ut_strict"],
                                cin_iota=cin["iota8"],
                                p_h1=ps("pf_h1", [128, 512], F32),
                                p_h2=ps("pf_h2", [128, 512], F32),
                                p_small=ps("pf_small", [128, 64], F32),
                                p_eo=ps("pf_eo", [128, 1024], F32),
                                p_tr=[ps("pf_tr0", [128, 512], BF16),
                                      ps("pf_tr1", [128, 512], BF16)])
                            Bl["xbf"] = Bl["h2sb"][:, 0, :]
                            Bl["ew1"] = Bl["w1b"]
                            Bl["ew2"] = Bl["w2b"]
                            emit_l1_ffn(c, stream, Wf, Bl, scrA, scrB)
                    elif bn == "ln_only":
                        emit_ln(c, stream, list(range(NT)), silu32, stats)
                        emit_join(c)
                    elif bn == "noop":
                        pass
                    else:
                        raise ValueError(bn)

                for s in ("dv", "ac", "pe", "sp"):
                    c.wait(nc.sync, s)
                emit_stream_store(c, xout, stream)

    return nc, consts


def _shard_x(x):
    x = np.asarray(x, dtype=np.float32)
    return [np.ascontiguousarray(x[2 * i:2 * i + 2].reshape(T, D))
            for i in range(8)]


def _gather_out(res, cores):
    outs = [res.results[i]["xout"].reshape(NBATCH, SEQ, D)
            for i in range(len(cores))]
    return np.concatenate(outs, axis=0)


def run_blocks(block_names, inputs, x_override=None, core_ids=None,
               shards=None):
    nc, consts = build_launch(block_names)
    if shards is None:
        x = x_override if x_override is not None else inputs["L0_attn_x"]
        shards = _shard_x(np.asarray(x))
    wmaps = {}
    if "l0_attn" in block_names:
        w = prep_l0_attn_weights(inputs)
        wmaps.update({"l0a_" + k: v for k, v in w.items()})
    if "l1_attn" in block_names:
        w = prep_l1_attn_weights(inputs)
        wmaps.update({"l1a_" + k: v for k, v in w.items()})
    if "l0_ffn" in block_names:
        w = prep_l0_ffn_weights(inputs)
        wmaps.update({"l0f_" + k: v for k, v in w.items()})
    if "l1_ffn" in block_names:
        w = prep_l1_ffn_weights(inputs)
        wmaps.update({"l1f_" + k: v for k, v in w.items()})
    cmaps = {"c_" + k: v for k, v in consts.items()}
    cores = core_ids if core_ids is not None else list(range(8))
    in_maps = [dict(xin=shards[i], **cmaps, **wmaps) for i in cores]
    res = run_bass_kernel_spmd(nc, in_maps, core_ids=cores)
    return res, cores


def kernel(**inputs):
    res, cores = run_blocks(["l0_attn", "l0_ffn", "l1_attn", "l1_ffn"], inputs)
    return _gather_out(res, cores).astype(np.float32)


# version 19
# speedup vs baseline: 1.0222x; 1.0222x over previous
import contextlib
import numpy as np
import concourse.bass as bass
import concourse.mybir as mybir
from concourse.bass import IndirectOffsetOnAxis
from concourse.bass_utils import run_bass_kernel_spmd

F32 = mybir.dt.float32
BF16 = mybir.dt.bfloat16
I32 = mybir.dt.int32
U32 = mybir.dt.uint32
AF = mybir.ActivationFunctionType
ALU = mybir.AluOpType
AX = mybir.AxisListType

D = 1024
SEQ = 2048
NBATCH = 2
T = NBATCH * SEQ
NT = T // 128
NTH = NT // 2
NCC = D // 128

HEADS = 16
DHEAD = 64
PM = 256
PEPS = 1e-4
PNORM = float(DHEAD) ** -0.25
PRATIO = float(PM) ** -0.5

E0 = 8
H0 = 2048
H1R = 4096
E1 = 8
H1E = 4096
CAP = 320
LN_EPS = 1e-5
MASKVAL = -1.25e9


def build_consts():
    c = {}
    c["idf32"] = np.eye(128, dtype=np.float32)
    c["idbf"] = np.eye(128, dtype=np.float32)
    c["ones_row"] = np.ones((1, 128), dtype=np.float32)
    c["iota8"] = np.broadcast_to(np.arange(8, dtype=np.float32), (128, 8)).copy()
    c["ut_strict"] = np.triu(np.ones((128, 128), dtype=np.float32), 1)
    c["causal_add"] = np.where(np.arange(128)[:, None] <= np.arange(128)[None, :],
                               0.0, MASKVAL).astype(np.float32)
    return c


class Ctx:
    def __init__(self, nc, sems):
        self.nc = nc
        self.s = dict(sems)
        self.n = {k: 0 for k in self.s}
        self.bufpe = {}

    def guard_buf(self, eng, name):
        v = self.bufpe.get(name, 0)
        if v > 0:
            eng.wait_ge(self.s["pe"], v)

    def consumed(self, name):
        self.bufpe[name] = self.n["pe"]

    def inc(self, instr, sem, k=1):
        instr.then_inc(self.s[sem], k)
        self.n[sem] += k
        return self.n[sem]

    def wait(self, eng, sem, val=None):
        v = self.n[sem] if val is None else val
        if v > 0:
            eng.wait_ge(self.s[sem], v)


def vw(c, ins):
    c.inc(ins, "dv")
    c.wait(c.nc.vector, "dv")
    return ins


def emit_join(c):
    nc = c.nc
    for eng in (nc.sync, nc.scalar, nc.vector, nc.tensor, nc.gpsimd):
        for s in ("pe", "dv", "ac", "ld", "gd", "sd", "sp"):
            c.wait(eng, s)


class Carver:
    def __init__(self, region):
        self.r = region
        self.off = 0

    def f32(self, shape):
        n = int(np.prod(shape[1:]))
        v = self.r[:, self.off:self.off + n]
        self.off += n
        if len(shape) == 3:
            v = v.rearrange("p (a b) -> p a b", a=shape[1])
        return v

    def u32(self, shape):
        n = int(np.prod(shape[1:]))
        v = self.r[:, self.off:self.off + n].bitcast(U32)
        self.off += n
        if len(shape) == 3:
            v = v.rearrange("p (a b) -> p a b", a=shape[1])
        return v

    def bf16(self, shape):
        n = int(np.prod(shape[1:]))
        assert n % 2 == 0
        v = self.r[:, self.off:self.off + n // 2].bitcast(BF16)
        self.off += n // 2
        if len(shape) == 3:
            v = v.rearrange("p (a b) -> p a b", a=shape[1])
        return v


def emit_stream_load(c, xin, stream):
    nc = c.nc
    src = xin.rearrange("(t p) f -> p t f", p=128)
    for i in range(4):
        ins = nc.sync.dma_start(out=stream[:, 8 * i:8 * i + 8, :],
                                in_=src[:, 8 * i:8 * i + 8, :])
        c.inc(ins, "ld", 16)


def emit_stream_store(c, xout, stream):
    nc = c.nc
    dst = xout.rearrange("(t p) f -> p t f", p=128)
    for i in range(4):
        ins = nc.sync.dma_start(out=dst[:, 8 * i:8 * i + 8, :],
                                in_=stream[:, 8 * i:8 * i + 8, :])
        c.inc(ins, "st", 16)
    c.wait(nc.sync, "st")


def emit_spill(c, stream, half, scratch):
    nc = c.nc
    sl = slice(half * NTH, (half + 1) * NTH)
    ins = nc.sync.dma_start(out=scratch[:, 0:8, :], in_=stream[:, sl, :][:, 0:8, :])
    c.inc(ins, "sp", 16)
    ins = nc.sync.dma_start(out=scratch[:, 8:16, :], in_=stream[:, sl, :][:, 8:16, :])
    c.inc(ins, "sp", 16)


def emit_unspill(c, stream, half, scratch):
    nc = c.nc
    sl = slice(half * NTH, (half + 1) * NTH)
    ins = nc.sync.dma_start(out=stream[:, sl, :][:, 0:8, :], in_=scratch[:, 0:8, :])
    c.inc(ins, "sp", 16)
    ins = nc.sync.dma_start(out=stream[:, sl, :][:, 8:16, :], in_=scratch[:, 8:16, :])
    c.inc(ins, "sp", 16)


def emit_tm_to_fm(c, stream, tiles, xfm, B):
    nc = c.nc
    xcast, idbf, ptr = B["xcast"], B["idbf"], B["p_tr"]
    pe0, dv0, ac0 = c.n["pe"], c.n["dv"], c.n["ac"]
    for i, t in enumerate(tiles):
        buf = xcast[i % 2]
        if i >= 2:
            c.wait(nc.vector, "pe", pe0 + 2 * (i - 1))
        ins = nc.vector.tensor_copy(buf[:], stream[:, t, :])
        c.inc(ins, "dv")
        for h in range(2):
            pb = ptr[h]
            nc.tensor.wait_ge(c.s["dv"], dv0 + i + 1)
            if i >= 1:
                nc.tensor.wait_ge(c.s["ac"], ac0 + 2 * (i - 1) + h + 1)
            for j in range(4):
                cc = h * 4 + j
                last = nc.tensor.transpose(pb[:, j * 128:(j + 1) * 128],
                                           buf[:, cc * 128:(cc + 1) * 128],
                                           idbf[:])
            c.inc(last, "pe")
            c.wait(nc.scalar, "pe")
            for j in range(4):
                cc = h * 4 + j
                ins = nc.scalar.copy(xfm[:, cc, i * 128:(i + 1) * 128],
                                     pb[:, j * 128:(j + 1) * 128])
            c.inc(ins, "ac")


def emit_ln(c, stream, tiles, sq32, stats):
    nc = c.nc
    for t in tiles:
        ssum, ssq = stats[:, 0:1], stats[:, 1:2]
        mean, var = stats[:, 2:3], stats[:, 3:4]
        rstd, tmp = stats[:, 4:5], stats[:, 5:6]
        ins = nc.scalar.activation(out=sq32[:, 0:D], in_=stream[:, t, :],
                                   func=AF.Square, accum_out=ssq)
        c.inc(ins, "ac")
        c.wait(nc.vector, "ac")
        ins = nc.vector.reduce_sum(ssum, stream[:, t, :], axis=AX.X)
        c.inc(ins, "dv")
        c.wait(nc.vector, "dv")
        nc.vector.tensor_scalar_mul(mean, ssum, 1.0 / D)
        nc.vector.tensor_scalar_mul(var, ssq, 1.0 / D)
        nc.vector.tensor_mul(tmp, mean, mean)
        nc.vector.tensor_sub(var, var, tmp)
        ins = nc.vector.tensor_scalar_add(var, var, LN_EPS)
        c.inc(ins, "dv")
        c.wait(nc.scalar, "dv")
        nc.scalar.activation(out=tmp, in_=var, func=AF.Ln)
        ins = nc.scalar.activation(out=rstd, in_=tmp, func=AF.Exp, scale=-0.5)
        c.inc(ins, "ac")
        c.wait(nc.vector, "ac")
        ins = nc.vector.tensor_scalar(out=stream[:, t, :], in0=stream[:, t, :],
                                      scalar1=mean, scalar2=rstd,
                                      op0=ALU.subtract, op1=ALU.mult)
        c.inc(ins, "dv")


def emit_l0_ffn_half(c, stream, W, B, half, cv):
    nc = c.nc
    xfm = B["xfm"]
    stats, silu32 = B["stats"], B["silu32"]
    p_r = B["p_u"][:, 0:8]
    p_u, p_g, p_y = B["p_u"], B["p_g"], B["p_y"]
    rt = B["rt"]

    hstore = cv.bf16([128, 16, 2048])
    mask9 = B["mask9"]
    tiles = list(range(half * NTH, (half + 1) * NTH))

    c.wait(nc.vector, "sp")
    emit_tm_to_fm(c, stream, tiles, xfm, B)
    c.wait(nc.tensor, "gd")
    c.wait(nc.tensor, "ac")
    c.wait(nc.tensor, "sp")
    c.wait(nc.vector, "sp")
    dv0 = c.n["dv"]
    for i in range(NTH):
        if i >= 1:
            nc.tensor.wait_ge(c.s["dv"], dv0 + 2 * i)
        for cc in range(NCC):
            last = nc.tensor.matmul(p_r, xfm[:, cc, i * 128:(i + 1) * 128],
                                    rt[:, cc, :], start=(cc == 0),
                                    stop=(cc == NCC - 1))
        c.inc(last, "pe")
        c.wait(nc.vector, "pe")
        ins = nc.vector.reduce_max(stats[:, 0:1], p_r, axis=AX.X)
        c.inc(ins, "dv")
        c.wait(nc.vector, "dv")
        ins = nc.vector.tensor_scalar(out=mask9[:, i, :], in0=p_r,
                                      scalar1=stats[:, 0:1], scalar2=None,
                                      op0=ALU.is_ge)
        c.inc(ins, "dv")
    wug, wdn = B["wug"], B["wdn"]
    for e in range(9):
        shared = (e == 8)
        if shared:
            upT, gateT, downT = W["up_sT"], W["gate_sT"], W["down_sT"]
        else:
            upT, gateT, downT = W["upT"][e], W["gateT"][e], W["downT"][e]
        pe_b, dv_b, ac_b = c.n["pe"], c.n["dv"], c.n["ac"]
        k = 0
        for hp in range(8):
            wb = wug[hp % 2]
            c.guard_buf(nc.gpsimd, "l0f_wug%d" % (hp % 2))
            ins = nc.gpsimd.dma_start(
                out=wb[:, :, 0:256],
                in_=upT[:, hp * 256:(hp + 1) * 256].rearrange(
                    "(cc p) hh -> p cc hh", p=128))
            c.inc(ins, "gd", 16)
            ins = nc.gpsimd.dma_start(
                out=wb[:, :, 256:512],
                in_=gateT[:, hp * 256:(hp + 1) * 256].rearrange(
                    "(cc p) hh -> p cc hh", p=128))
            c.inc(ins, "gd", 16)
            gd_mark = c.n["gd"]
            for hh in range(2):
                hc = hp * 2 + hh
                for hw in range(4):
                    tw = slice(hw * 512, (hw + 1) * 512)
                    nc.tensor.wait_ge(c.s["gd"], gd_mark)
                    if k >= 1:
                        nc.tensor.wait_ge(c.s["ac"], ac_b + k)
                        nc.tensor.wait_ge(c.s["dv"], dv_b + k)
                    else:
                        c.wait(nc.tensor, "ac", ac_b)
                        c.wait(nc.tensor, "dv", dv_b)
                    for cc in range(NCC):
                        nc.tensor.matmul(
                            p_u[:], wb[:, cc, hh * 128:(hh + 1) * 128],
                            xfm[:, cc, tw], start=(cc == 0),
                            stop=(cc == NCC - 1))
                    for cc in range(NCC):
                        lg = nc.tensor.matmul(
                            p_g[:], wb[:, cc, 256 + hh * 128:256 + (hh + 1) * 128],
                            xfm[:, cc, tw], start=(cc == 0),
                            stop=(cc == NCC - 1))
                    c.inc(lg, "pe")
                    c.wait(nc.scalar, "pe")
                    if k >= 1:
                        c.wait(nc.scalar, "dv", dv_b + k)
                    ins = nc.scalar.activation(out=silu32[:, 0:512],
                                               in_=p_g[:], func=AF.Silu)
                    c.inc(ins, "ac")
                    c.wait(nc.vector, "ac")
                    ins = nc.vector.tensor_mul(hstore[:, hc, tw], p_u[:],
                                               silu32[:, 0:512])
                    c.inc(ins, "dv")
                    k += 1
                    c.consumed("l0f_wug%d" % (hp % 2))
        for ch in range(2):
            cw = slice(ch * 512, (ch + 1) * 512)
            c.guard_buf(nc.gpsimd, "l0f_wdn")
            ins = nc.gpsimd.dma_start(
                out=wdn[:, 0:8, :],
                in_=downT[0:1024, ch * 512:(ch + 1) * 512].rearrange(
                    "(hc p) cf -> p hc cf", p=128))
            c.inc(ins, "gd", 16)
            ins = nc.gpsimd.dma_start(
                out=wdn[:, 8:16, :],
                in_=downT[1024:2048, ch * 512:(ch + 1) * 512].rearrange(
                    "(hc p) cf -> p hc cf", p=128))
            c.inc(ins, "gd", 16)
            gd_dn = c.n["gd"]
            dv_c = c.n["dv"]
            for ts in range(NTH):
                pb = p_y[ts % 2]
                tile = tiles[ts]
                nc.tensor.wait_ge(c.s["gd"], gd_dn)
                c.wait(nc.tensor, "dv", dv_b + 64)
                if ts >= 2:
                    c.wait(nc.tensor, "dv", dv_c + ts - 1)
                else:
                    c.wait(nc.tensor, "dv", dv_c)
                for hc in range(16):
                    last = nc.tensor.matmul(
                        pb[:], hstore[:, hc, ts * 128:(ts + 1) * 128],
                        wdn[:, hc, :], start=(hc == 0), stop=(hc == 15))
                c.inc(last, "pe")
                c.consumed("l0f_wdn")
                c.wait(nc.vector, "pe")
                if shared:
                    ins = nc.vector.tensor_add(stream[:, tile, cw], pb[:],
                                               stream[:, tile, cw])
                else:
                    ins = nc.vector.scalar_tensor_tensor(
                        out=stream[:, tile, cw], in0=pb[:],
                        scalar=mask9[:, ts, e:e + 1],
                        in1=stream[:, tile, cw],
                        op0=ALU.mult, op1=ALU.add)
                c.inc(ins, "dv")
    c.wait(nc.scalar, "dv")
    emit_ln(c, stream, tiles, silu32, stats)


def emit_block_halves(c, stream, scratchA, scratchB, emit_half):
    nc = c.nc
    emit_spill(c, stream, 1, scratchB)
    cv = Carver(stream.ap()[:, NTH:NT, :].rearrange("p a b -> p (a b)"))
    emit_half(0, cv)
    emit_join(c)
    emit_spill(c, stream, 0, scratchA)
    emit_unspill(c, stream, 1, scratchB)
    cv = Carver(stream.ap()[:, 0:NTH, :].rearrange("p a b -> p (a b)"))
    emit_half(1, cv)
    emit_join(c)
    emit_unspill(c, stream, 0, scratchA)
    c.wait(nc.sync, "sp")
    emit_join(c)


def emit_l0_ffn(c, stream, W, B, scratchA, scratchB):
    nc = c.nc
    ins = nc.gpsimd.dma_start(out=B["rt"][:], in_=W["routerT_tiled"])
    c.inc(ins, "gd", 16)
    emit_block_halves(c, stream, scratchA, scratchB,
                      lambda half, cv: emit_l0_ffn_half(c, stream, W, B, half, cv))




def emit_l1_attn_half(c, stream, W, B, half, cv):
    nc = c.nc
    xfm, stats, silu32 = B["xfm"], B["stats"], B["silu32"]
    causal = B["causal"]
    wvt, wqk, wot_full = B["wvt"], B["wqk"], B["wot_full"]
    p_a, p_b, p_o, p_to = B["p_a"], B["p_b"], B["p_o"], B["p_to"]
    pzb = [B["p_z0"], B["p_z1"]]
    tiles = list(range(half * NTH, (half + 1) * NTH))

    zall = cv.bf16([128, 8, 2048])
    v2 = cv.bf16([128, 16, 130])
    q2 = cv.bf16([128, 2048])
    k2 = cv.bf16([128, 2048])
    pbuf = [cv.bf16([128, 512]) for _ in range(4)]
    rowbuf = [cv.bf16([1, 512]) for _ in range(2)]
    recSB = [cv.f32([128, 512]) for _ in range(2)]
    onesb = cv.bf16([128, 130])

    c.wait(nc.vector, "sp")
    emit_tm_to_fm(c, stream, tiles, xfm, B)
    ins = nc.vector.memset(onesb[:], 1.0)
    c.inc(ins, "dv")
    ins = nc.vector.memset(v2[:, :, 64:65], 1.0)
    c.inc(ins, "dv")
    ins = nc.vector.memset(v2[:, :, 129:130], 1.0)
    c.inc(ins, "dv")
    ins = nc.gpsimd.dma_start(out=wvt[:], in_=W["WVT"].rearrange(
        "(cc p) f -> p cc f", p=128))
    c.inc(ins, "gd", 16)
    wvt_gd = c.n["gd"]
    ins = nc.gpsimd.dma_start(
        out=wot_full[:], in_=W["WOT"][:, 0:512].rearrange(
            "(ah p) m -> p ah m", p=128))
    c.inc(ins, "gd", 16)
    wot_gd = c.n["gd"]
    c.wait(nc.tensor, "ac")
    nc.tensor.wait_ge(c.s["gd"], wvt_gd)
    c.wait(nc.tensor, "dv")

    sbanks = [p_a[:], p_b[:], p_o[:], p_to[:]]
    st = {"gk": 0, "gp": 0, "pending": None}
    smark, amark, dmark, zmark = {}, {}, {}, {}
    normmark = [0, 0]
    bcmark = [0, 0]

    def flush_pending():
        if st["pending"] is None:
            return
        zb, zbi, rmark, hb, api_, cwsl_ = st["pending"]
        st["pending"] = None
        nc.tensor.wait_ge(c.s["dv"], rmark)
        ins = nc.tensor.matmul(zb[64:128, :], onesb[0:1, 0:64],
                               rowbuf[zbi][0:1, 0:512], start=True, stop=True)
        bcmark[zbi] = c.inc(ins, "pe")
        nc.scalar.wait_ge(c.s["pe"], bcmark[zbi])
        nc.scalar.wait_ge(c.s["dv"], normmark[zbi])
        ins = nc.scalar.copy(recSB[zbi][0:64, :], zb[64:128, :])
        cmark = c.inc(ins, "ac")
        nc.vector.wait_ge(c.s["ac"], cmark)
        ins = nc.vector.tensor_mul(zall[hb:hb + 64, api_, cwsl_],
                                   zb[0:64, :], recSB[zbi][0:64, :])
        normmark[zbi] = c.inc(ins, "dv")

    for ap_i in range(8):
        wq_t = wqk[ap_i % 2]
        c.guard_buf(nc.gpsimd, "l1a_wqk%d" % (ap_i % 2))
        ins = nc.gpsimd.dma_start(
            out=wq_t[:, 0:8, :],
            in_=W["WQT"][:, ap_i * 128:(ap_i + 1) * 128].rearrange(
                "(cc p) m -> p cc m", p=128))
        c.inc(ins, "gd", 16)
        ins = nc.gpsimd.dma_start(
            out=wq_t[:, 8:16, :],
            in_=W["WKT"][:, ap_i * 128:(ap_i + 1) * 128].rearrange(
                "(cc p) m -> p cc m", p=128))
        c.inc(ins, "gd", 16)
        gd_qk = c.n["gd"]
        ac_qk = c.n["ac"]
        for w in range(4):
            tw = slice(w * 512, (w + 1) * 512)
            nc.tensor.wait_ge(c.s["gd"], gd_qk)
            if w >= 1:
                nc.tensor.wait_ge(c.s["ac"], ac_qk + 2 * w)
            else:
                if st["gk"] >= 1:
                    la = max(amark.get(st["gk"] - 1 - k, 0) for k in range(4))
                    nc.tensor.wait_ge(c.s["ac"], la)
            for cc in range(NCC):
                lq = nc.tensor.matmul(p_a[:], wq_t[:, cc, :], xfm[:, cc, tw],
                                      start=(cc == 0), stop=(cc == NCC - 1))
            for cc in range(NCC):
                lk = nc.tensor.matmul(p_b[:], wq_t[:, 8 + cc, :], xfm[:, cc, tw],
                                      start=(cc == 0), stop=(cc == NCC - 1))
            c.inc(lk, "pe")
            c.wait(nc.scalar, "pe")
            nc.scalar.mul(q2[:, tw], p_a[:], 0.125)
            ins = nc.scalar.copy(k2[:, tw], p_b[:])
            c.inc(ins, "ac", 2)
        c.consumed("l1a_wqk%d" % (ap_i % 2))
        dv_v = c.n["dv"]
        for ts in range(NTH):
            vb = p_a if ts % 2 == 0 else p_b
            nc.tensor.wait_ge(c.s["ac"], ac_qk + 8)
            if ts >= 2:
                nc.tensor.wait_ge(c.s["dv"], dv_v + 2 * (ts - 1))
            for cc in range(NCC):
                lv = nc.tensor.matmul(
                    vb[:, 0:128], xfm[:, cc, ts * 128:(ts + 1) * 128],
                    wvt[:, cc, ap_i * 128:(ap_i + 1) * 128],
                    start=(cc == 0), stop=(cc == NCC - 1))
            c.inc(lv, "pe")
            c.wait(nc.vector, "pe")
            nc.vector.tensor_copy(v2[:, ts, 0:64], vb[:, 0:64])
            ins = nc.vector.tensor_copy(v2[:, ts, 65:129], vb[:, 64:128])
            c.inc(ins, "dv", 2)
        v2_dv = c.n["dv"]
        ins = nc.scalar.memzero(silu32[:, 0:1])
        c.inc(ins, "ac")
        mb = silu32[:, 0:1]
        qk_ac = c.n["ac"]
        nc.tensor.wait_ge(c.s["dv"], v2_dv)

        for hh in range(2):
            hb = 64 * hh
            vsl = slice(65 * hh, 65 * hh + 65)
            for cw in range(4):
                cwsl = slice(cw * 512, (cw + 1) * 512)
                nch = 4 * cw + 4
                zbi = st["gp"] % 2
                zb = pzb[zbi]
                kids = []

                def issue_S(Cc):
                    kid = st["gk"]
                    st["gk"] += 1
                    kids.append(kid)
                    bk = kid % 4
                    sbank = sbanks[bk]
                    pb = pbuf[bk]
                    if kid >= 4:
                        nc.tensor.wait_ge(c.s["ac"], amark[kid - 4])
                    ins = nc.tensor.matmul(
                        sbank, q2[hb:hb + 64, Cc * 128:(Cc + 1) * 128],
                        k2[hb:hb + 64, cwsl], start=True, stop=True)
                    smark[kid] = c.inc(ins, "pe")
                    j = Cc - 4 * cw
                    zg = zmark.get(kid - 4, 0)
                    if j >= 0:
                        nc.vector.wait_ge(c.s["pe"], max(smark[kid], zg))
                        ins = nc.vector.tensor_add(
                            sbank[:, j * 128:(j + 1) * 128],
                            sbank[:, j * 128:(j + 1) * 128], causal[:])
                        c.inc(ins, "dv")
                        if j > 0:
                            ins = nc.vector.memset(pb[:, 0:j * 128], 0.0)
                            c.inc(ins, "dv")
                        dmark[kid] = c.n["dv"]
                        nc.scalar.wait_ge(c.s["dv"], dmark[kid])
                        nc.scalar.wait_ge(c.s["pe"], zg)
                        ins = nc.scalar.activation(
                            out=pb[:, j * 128:512], in_=sbank[:, j * 128:512],
                            func=AF.Exp, bias=mb)
                        amark[kid] = c.inc(ins, "ac")
                    else:
                        nc.scalar.wait_ge(c.s["pe"], max(smark[kid], zg))
                        ins = nc.scalar.activation(out=pb[:], in_=sbank[:, :],
                                                   func=AF.Exp, bias=mb)
                        amark[kid] = c.inc(ins, "ac")

                def issue_z(i, kid):
                    nc.tensor.wait_ge(c.s["ac"], amark[kid])
                    if kid in dmark and dmark[kid] > 0:
                        nc.tensor.wait_ge(c.s["dv"], dmark[kid])
                    if i == 0:
                        nc.tensor.wait_ge(c.s["dv"], normmark[zbi])
                    ins = nc.tensor.matmul(
                        zb[0:65, :], v2[:, i, vsl], pbuf[kid % 4][:],
                        start=(i == 0), stop=(i == nch - 1),
                        skip_group_check=True)
                    zmark[kid] = c.inc(ins, "pe")

                issue_S(0)
                if nch > 1:
                    issue_S(1)
                flush_pending()
                for i in range(nch):
                    issue_z(i, kids[i])
                    if i + 2 < nch:
                        issue_S(i + 2)
                nc.vector.wait_ge(c.s["pe"], max(zmark[kids[-1]], bcmark[zbi]))
                with nc.allow_low_precision(
                        reason="recip rounded to bf16 for the bcast matmul "
                               "operand; same rounding as f32-recip+cast"):
                    ins = nc.vector.reciprocal(rowbuf[zbi][0:1, 0:512],
                                               zb[64:65, :])
                rmark = c.inc(ins, "dv")
                st["pending"] = (zb, zbi, rmark, hb, ap_i, cwsl)
                st["gp"] += 1
    flush_pending()
    c.wait(nc.tensor, "dv")
    obanks = [pzb[0], pzb[1], p_o[:], p_to[:]]
    evm = [0, 0, 0, 0]
    for fh in range(2):
        if fh == 1:
            c.wait(nc.gpsimd, "pe")
            ins = nc.gpsimd.dma_start(
                out=wot_full[:], in_=W["WOT"][:, 512:1024].rearrange(
                    "(ah p) m -> p ah m", p=128))
            c.inc(ins, "gd", 16)
            wot_gd = c.n["gd"]
        nc.tensor.wait_ge(c.s["gd"], wot_gd)
        for ti in range(NTH):
            tile = tiles[ti]
            ob = obanks[ti % 4]
            nc.tensor.wait_ge(c.s["dv"], evm[ti % 4])
            for api in range(8):
                lo = nc.tensor.matmul(ob[0:128, 0:512],
                                      zall[:, api, ti * 128:(ti + 1) * 128],
                                      wot_full[:, api, :],
                                      start=(api == 0), stop=(api == 7),
                                      skip_group_check=True)
            pem = c.inc(lo, "pe")
            nc.vector.wait_ge(c.s["pe"], pem)
            ins = nc.vector.tensor_add(
                stream[:, tile, fh * 512:(fh + 1) * 512], ob[0:128, 0:512],
                stream[:, tile, fh * 512:(fh + 1) * 512])
            evm[ti % 4] = c.inc(ins, "dv")
    c.wait(nc.scalar, "dv")
    emit_ln(c, stream, tiles, silu32, stats)


def prep_l1_attn_weights(inp):
    WQ = np.asarray(inp["L1_attn_W_Q"])
    WK = np.asarray(inp["L1_attn_W_K"])
    WV = np.asarray(inp["L1_attn_W_V"])
    WO = np.asarray(inp["L1_attn_W_O"])
    return {
        "WQT": np.ascontiguousarray(WQ.transpose(2, 0, 1).reshape(D, D)),
        "WKT": np.ascontiguousarray(WK.transpose(2, 0, 1).reshape(D, D)),
        "WVT": np.ascontiguousarray(WV.transpose(2, 0, 1).reshape(D, D)),
        "WOT": np.ascontiguousarray(WO.transpose(0, 2, 1).reshape(D, D)),
    }


L1_ATTN_SHAPES = {"WQT": [D, D], "WKT": [D, D], "WVT": [D, D], "WOT": [D, D]}




def emit_l0_attn_pass1(c, stream, W, B, half, cv, rebuild_xfm=True):
    nc = c.nc
    xfm, stats = B["xfm"], B["stats"]
    paug, wvt, wqk = B["paug"], B["wvt"], B["wqk"]
    ctxbuf, vsumbuf, kmax = B["ctxbuf"], B["vsumbuf"], B["kmax"]
    p_a, p_b, p_dd, p_dd2, p_c0, p_c1 = (B["p_a"], B["p_b"], B["p_dd"],
                                         B["p_dd2"], B["p_c0"], B["p_c1"])
    tiles = list(range(half * NTH, (half + 1) * NTH))

    v_tm = cv.bf16([128, 16, 1024])
    k2 = cv.bf16([128, 2048])
    augk = cv.bf16([128, 2048])
    kp = cv.bf16([128, 16, 256])
    onesb = cv.bf16([128, 130])
    bb = cv.f32([128, 2])
    bb2 = cv.f32([128, 2])
    kmt = cv.f32([128, NTH])

    c.wait(nc.vector, "sp")
    if rebuild_xfm:
        emit_tm_to_fm(c, stream, tiles, xfm, B)
    ins = nc.vector.memset(onesb[:], 1.0)
    c.inc(ins, "dv")
    c.wait(nc.tensor, "ac")
    c.wait(nc.tensor, "gd")
    c.wait(nc.tensor, "dv")
    dv_v = c.n["dv"]
    for ts in range(NTH):
        if ts >= 1:
            nc.tensor.wait_ge(c.s["dv"], dv_v + ts)
        for half_f in range(2):
            vb = p_a if half_f == 0 else p_b
            for cc in range(NCC):
                lv = nc.tensor.matmul(
                    vb[:], xfm[:, cc, ts * 128:(ts + 1) * 128],
                    B["wvt"][:, cc, half_f * 512:(half_f + 1) * 512],
                    start=(cc == 0), stop=(cc == NCC - 1))
            c.inc(lv, "pe")
            c.wait(nc.vector, "pe")
            ins = nc.vector.tensor_copy(
                v_tm[:, ts, half_f * 512:(half_f + 1) * 512], vb[:])
        c.inc(ins, "dv")
    for ap_i in range(8):
        wk_t = wqk[ap_i % 2]
        c.guard_buf(nc.gpsimd, "l0a_wqk%d" % (ap_i % 2))
        ins = nc.gpsimd.dma_start(
            out=wk_t[:, 0:8, :],
            in_=W["wk"][:, ap_i * 128:(ap_i + 1) * 128].rearrange(
                "(cc p) m -> p cc m", p=128))
        c.inc(ins, "gd", 16)
        gd_k = c.n["gd"]
        ac_k = c.n["ac"]
        for w in range(4):
            tw = slice(w * 512, (w + 1) * 512)
            kb = p_a if w % 2 == 0 else p_b
            nc.tensor.wait_ge(c.s["gd"], gd_k)
            c.wait(nc.tensor, "dv")
            if w >= 2:
                nc.tensor.wait_ge(c.s["ac"], ac_k + w - 1)
            for cc in range(NCC):
                lk = nc.tensor.matmul(kb[:], wk_t[:, cc, :], xfm[:, cc, tw],
                                      start=(cc == 0), stop=(cc == NCC - 1))
            c.inc(lk, "pe")
            c.consumed("l0a_wqk%d" % (ap_i % 2))
            c.wait(nc.scalar, "pe")
            ins = nc.scalar.copy(k2[:, tw], kb[:])
            c.inc(ins, "ac")
        for hh in range(2):
            h = ap_i * 2 + hh
            hb = 64 * hh
            c.wait(nc.vector, "ac")
            c.wait(nc.vector, "pe")
            ins = nc.vector.tensor_copy(augk[0:64, :], k2[hb:hb + 64, :])
            c.inc(ins, "dv")
            c.wait(nc.scalar, "dv")
            ins = nc.scalar.activation(out=augk[64:128, :], in_=k2[hb:hb + 64, :],
                                       func=AF.Square)
            c.inc(ins, "ac")
            c.wait(nc.tensor, "ac")
            mmark = {}
            amark = {}
            dmark = {}
            for ts in range(NTH):
                db = p_dd if ts % 2 == 0 else p_dd2
                if ts >= 2:
                    nc.tensor.wait_ge(c.s["dv"], dmark[ts - 2])
                    nc.tensor.wait_ge(c.s["ac"], amark[ts - 2])
                ins = nc.tensor.matmul(db[:, 0:257],
                                       augk[:, ts * 128:(ts + 1) * 128],
                                       paug[:], start=True, stop=True)
                mmark[ts] = c.inc(ins, "pe")
                nc.vector.wait_ge(c.s["pe"], mmark[ts])
                ins = nc.vector.reduce_max(kmt[:, ts:ts + 1], db[:, 0:256],
                                           axis=AX.X)
                c.inc(ins, "dv")
                ins = nc.vector.tensor_copy(bb2[:, ts % 2:ts % 2 + 1],
                                            db[:, 256:257])
                dmark[ts] = c.inc(ins, "dv")
                nc.scalar.wait_ge(c.s["dv"], dmark[ts])
                ins = nc.scalar.activation(out=kp[:, ts, :], in_=db[:, 0:256],
                                           func=AF.Exp,
                                           bias=bb2[:, ts % 2:ts % 2 + 1])
                amark[ts] = c.inc(ins, "ac")
            c.wait(nc.vector, "dv")
            vw(c, nc.vector.reduce_max(stats[:, 0:1], kmt[:], axis=AX.X))
            ins = nc.vector.tensor_max(kmax[:, 0:1], kmax[:, 0:1], stats[:, 0:1])
            c.inc(ins, "dv")
            c.wait(nc.vector, "dv")
            c.wait(nc.tensor, "ac")
            for mc in range(2):
                pc = p_c0 if mc == 0 else p_c1
                c.wait(nc.tensor, "dv")
                for ts in range(NTH):
                    nc.tensor.matmul(pc[:, 0:64],
                                     kp[:, ts, mc * 128:(mc + 1) * 128],
                                     v_tm[:, ts, h * 64:(h + 1) * 64],
                                     start=(ts == 0), stop=False,
                                     skip_group_check=True)
                    lm = nc.tensor.matmul(pc[:, 64:65], kp[:, ts, mc * 128:(mc + 1) * 128],
                                          onesb[:, 0:1],
                                          start=False, stop=(ts == NTH - 1),
                                          skip_group_check=True)
                c.inc(lm, "pe")
                c.wait(nc.vector, "pe")
                ins = nc.vector.tensor_copy(
                    ctxbuf[:, half, h, mc, 0:65], pc[:, 0:65])
                c.inc(ins, "dv")
            c.wait(nc.tensor, "dv")
            for ts in range(NTH):
                nc.tensor.matmul(p_dd[0:1, 0:64], onesb[:, 0:1],
                                 v_tm[:, ts, h * 64:(h + 1) * 64],
                                 start=(ts == 0), stop=False,
                                 skip_group_check=True)
                lm = nc.tensor.matmul(p_dd[0:1, 64:65], onesb[:, 0:1],
                                      onesb[:, 1:2],
                                      start=False, stop=(ts == NTH - 1),
                                      skip_group_check=True)
            c.inc(lm, "pe")
            c.wait(nc.vector, "pe")
            ins = nc.vector.tensor_scalar_mul(
                vsumbuf[0:1, (half * 16 + h) * 66:(half * 16 + h) * 66 + 65],
                p_dd[0:1, 0:65], PRATIO * PEPS)
            c.inc(ins, "dv")
            c.wait(nc.tensor, "dv")


def emit_l0_attn_pass2(c, stream, W, B, half, cv, rebuild_xfm):
    nc = c.nc
    xfm, stats, silu32 = B["xfm"], B["stats"], B["silu32"]
    paug, wqk, wvt = B["paug"], B["wqk"], B["wvt"]
    wotF = lambda api: wvt[:, api // 2,
                           (api % 2) * 512:(api % 2) * 512 + 512]
    wot_dst = wvt.ap()[:, 0:4, :].rearrange("p a (b f) -> p a b f", b=2)
    ctxbuf, vsumbuf, acol = B["ctxbuf"], B["vsumbuf"], B["acol"]
    p_a, p_b, p_dd, p_dd2, p_c0, p_c1 = (B["p_a"], B["p_b"], B["p_dd"],
                                         B["p_dd2"], B["p_c0"], B["p_c1"])
    p_tr = B["p_tr"]
    tiles = list(range(half * NTH, (half + 1) * NTH))

    q2 = cv.bf16([128, 2048])
    q2sq = cv.bf16([128, 2048])
    qp = cv.bf16([128, 16, 256])
    qpT = cv.bf16([128, 2, 2048])
    attn_fm = cv.bf16([128, 8, 2048])
    ctxf = cv.bf16([128, 2, 66])
    onesb = cv.bf16([128, 130])
    bb = cv.f32([128, 4])
    tmp32 = cv.f32([128, 2, 256])
    rowb = [cv.bf16([1, 512]) for _ in range(2)]
    recq = cv.f32([128, 512])

    c.wait(nc.vector, "sp")
    if rebuild_xfm:
        emit_tm_to_fm(c, stream, tiles, xfm, B)
    ins = nc.vector.memset(onesb[:], 1.0)
    c.inc(ins, "dv")
    epsb = B["kmax"][:, 3:4]
    c.wait(nc.gpsimd, "pe")
    ins = nc.gpsimd.dma_start(
        out=wot_dst, in_=W["wo"][:, 0:512].rearrange(
            "(ah p) m -> p ah m", p=128).rearrange(
            "p (a b) m -> p a b m", b=2))
    c.inc(ins, "gd", 16)
    wot_gd0 = c.n["gd"]

    ddbanks = [p_dd, p_dd2, p_c0, p_c1]
    outst = {"pending": None, "norm": 0, "nb": [0, 0], "bc": [0, 0]}

    def flush_out():
        if outst["pending"] is None:
            return
        ob, pi, rmark, dst = outst["pending"]
        outst["pending"] = None
        nc.tensor.wait_ge(c.s["dv"], rmark)
        ins = nc.tensor.matmul(ob[64:128, :], onesb[0:1, 0:64],
                               rowb[pi][0:1, 0:512], start=True, stop=True)
        outst["bc"][pi] = c.inc(ins, "pe")
        nc.scalar.wait_ge(c.s["pe"], outst["bc"][pi])
        nc.scalar.wait_ge(c.s["dv"], outst["norm"])
        ins = nc.scalar.copy(recq[0:64, :], ob[64:128, :])
        cm = c.inc(ins, "ac")
        nc.vector.wait_ge(c.s["ac"], cm)
        ins = nc.vector.tensor_mul(dst, ob[0:64, :], recq[0:64, :])
        outst["norm"] = outst["nb"][pi] = c.inc(ins, "dv")

    for ap_i in range(8):
        wq_t = wqk[ap_i % 2]
        c.guard_buf(nc.gpsimd, "l0a_wqk%d" % (ap_i % 2))
        ins = nc.gpsimd.dma_start(
            out=wq_t[:, 0:8, :],
            in_=W["wq"][:, ap_i * 128:(ap_i + 1) * 128].rearrange(
                "(cc p) m -> p cc m", p=128))
        c.inc(ins, "gd", 16)
        gd_q = c.n["gd"]
        ac_q = c.n["ac"]
        flush_out()
        c.wait(nc.tensor, "ac")
        c.wait(nc.tensor, "dv")
        for w in range(4):
            tw = slice(w * 512, (w + 1) * 512)
            qb = p_a if w % 2 == 0 else p_b
            nc.tensor.wait_ge(c.s["gd"], gd_q)
            if w >= 2:
                nc.tensor.wait_ge(c.s["ac"], ac_q + w - 1)
            for cc in range(NCC):
                lq = nc.tensor.matmul(qb[:], wq_t[:, cc, :], xfm[:, cc, tw],
                                      start=(cc == 0), stop=(cc == NCC - 1))
            c.inc(lq, "pe")
            c.consumed("l0a_wqk%d" % (ap_i % 2))
            c.wait(nc.scalar, "pe")
            ins = nc.scalar.copy(q2[:, tw], qb[:])
            c.inc(ins, "ac")
        c.wait(nc.scalar, "pe")
        ins = nc.scalar.activation(out=q2sq[:, :], in_=q2[:, :], func=AF.Square)
        sqm = c.inc(ins, "ac")
        bankr = [0, 0, 0, 0]
        for hh in range(2):
            h = ap_i * 2 + hh
            hb = 64 * hh
            vsl0 = (half * 16 + h) * 66
            ctxm = 0
            for mc in range(2):
                pc = p_c0 if mc == 0 else p_c1
                nc.tensor.wait_ge(c.s["ac"], bankr[2 + mc])
                ins = nc.tensor.matmul(pc[:, 0:65], onesb[0:1, 0:128],
                                       vsumbuf[0:1, vsl0:vsl0 + 65],
                                       start=True, stop=True)
                pcm = c.inc(ins, "pe")
                nc.vector.wait_ge(c.s["pe"], pcm)
                ins = nc.vector.scalar_tensor_tensor(
                    out=ctxf[:, mc, 0:65], in0=ctxbuf[:, half, h, mc, 0:65],
                    scalar=acol[:, 0:1], in1=pc[:, 0:65],
                    op0=ALU.mult, op1=ALU.add)
                ctxm = c.inc(ins, "dv")
            nc.tensor.wait_ge(c.s["ac"], sqm)
            amark_dd = {}
            qpm = {}
            for ts in range(NTH):
                db = ddbanks[ts % 4]
                p = ts % 2
                if ts >= 4:
                    nc.tensor.wait_ge(c.s["ac"], qpm[ts - 4])
                elif ts >= 2:
                    nc.tensor.wait_ge(c.s["dv"], ctxm)
                    nc.tensor.wait_ge(c.s["ac"], bankr[ts % 4])
                else:
                    nc.tensor.wait_ge(c.s["ac"], bankr[ts % 4])
                tsl = slice(ts * 128, (ts + 1) * 128)
                nc.tensor.matmul(db[:, 0:257], q2[hb:hb + 64, tsl],
                                 B["paugA"][hb:hb + 64, :],
                                 start=True, stop=False)
                ins = nc.tensor.matmul(db[:, 0:257], q2sq[hb:hb + 64, tsl],
                                       B["paugB"][hb:hb + 64, :],
                                       start=False, stop=True)
                ddm = c.inc(ins, "pe")
                nc.vector.wait_ge(c.s["pe"], ddm)
                ins = nc.vector.reduce_max(stats[:, p:p + 1], db[:, 0:256],
                                           axis=AX.X)
                mm = c.inc(ins, "dv")
                nc.vector.wait_ge(c.s["dv"], mm)
                ins = nc.vector.tensor_sub(bb[:, p:p + 1], db[:, 256:257],
                                           stats[:, p:p + 1])
                sm = c.inc(ins, "dv")
                nc.scalar.wait_ge(c.s["dv"], sm)
                ins = nc.scalar.activation(out=tmp32[:, p, :], in_=db[:, 0:256],
                                           func=AF.Exp, bias=bb[:, p:p + 1])
                em = c.inc(ins, "ac")
                bankr[ts % 4] = em
                ins = nc.scalar.activation(out=qp[:, ts, :], in_=tmp32[:, p, :],
                                           func=AF.Identity, scale=PRATIO,
                                           bias=epsb)
                qpm[ts] = c.inc(ins, "ac")
            nc.tensor.wait_ge(c.s["ac"], qpm[NTH - 1])
            trm = {}
            for ts in range(NTH):
                ptr = p_tr[ts % 2]
                if ts >= 2:
                    nc.tensor.wait_ge(c.s["ac"], trm[ts - 2])
                for mc in range(2):
                    ins = nc.tensor.transpose(
                        ptr[:, mc * 128:(mc + 1) * 128],
                        qp[:, ts, mc * 128:(mc + 1) * 128], B["idbf"][:])
                tm = c.inc(ins, "pe")
                nc.scalar.wait_ge(c.s["pe"], tm)
                ins = nc.scalar.copy(
                    qpT[:, 0:2, ts * 128:(ts + 1) * 128],
                    ptr[:, 0:256].rearrange("p (a b) -> p a b", a=2))
                trm[ts] = c.inc(ins, "ac")
            nc.tensor.wait_ge(c.s["ac"], trm[NTH - 1])
            for win in range(4):
                pi = win % 2
                ob = p_a if pi == 0 else p_b
                wsl = slice(win * 512, (win + 1) * 512)
                nc.tensor.wait_ge(c.s["dv"], outst["nb"][pi])
                nc.tensor.matmul(ob[0:65, 0:512], ctxf[:, 0, 0:65],
                                 qpT[:, 0, wsl], start=True, stop=False,
                                 skip_group_check=True)
                ins = nc.tensor.matmul(ob[0:65, 0:512], ctxf[:, 1, 0:65],
                                       qpT[:, 1, wsl], start=False, stop=True,
                                       skip_group_check=True)
                zm = c.inc(ins, "pe")
                flush_out()
                nc.vector.wait_ge(c.s["pe"], max(zm, outst["bc"][pi]))
                with nc.allow_low_precision(
                        reason="recip rounded to bf16 for the bcast matmul "
                               "operand; same rounding as f32-recip+cast"):
                    ins = nc.vector.reciprocal(rowb[pi][0:1, 0:512],
                                               ob[64:65, 0:512])
                rmark = c.inc(ins, "dv")
                outst["pending"] = (ob, pi, rmark,
                                    attn_fm[hb:hb + 64, ap_i, wsl])
    flush_out()
    c.wait(nc.tensor, "dv")
    obanks = [p_dd[:, 0:512], p_dd2[:, 0:512], p_c0[:, 0:512], p_c1[:, 0:512]]
    evm = [0, 0, 0, 0]
    wot_gd = wot_gd0
    for fh in range(2):
        if fh == 1:
            c.wait(nc.gpsimd, "pe")
            ins = nc.gpsimd.dma_start(
                out=wot_dst, in_=W["wo"][:, 512:1024].rearrange(
                    "(ah p) m -> p ah m", p=128).rearrange(
                    "p (a b) m -> p a b m", b=2))
            c.inc(ins, "gd", 16)
            wot_gd = c.n["gd"]
        nc.tensor.wait_ge(c.s["gd"], wot_gd)
        for ti in range(NTH):
            tile = tiles[ti]
            ob = obanks[ti % 4]
            nc.tensor.wait_ge(c.s["dv"], evm[ti % 4])
            for api in range(8):
                nc.tensor.matmul(ob, attn_fm[:, api, ti * 128:(ti + 1) * 128],
                                 wotF(api), start=(api == 0), stop=False,
                                 skip_group_check=True)
            lo = nc.tensor.matmul(ob, onesb[0:1, 0:128],
                                  B["borow"][0:1, fh * 512:(fh + 1) * 512],
                                  start=False, stop=True,
                                  skip_group_check=True)
            pem = c.inc(lo, "pe")
            nc.vector.wait_ge(c.s["pe"], pem)
            ins = nc.vector.tensor_add(
                stream[:, tile, fh * 512:(fh + 1) * 512], ob,
                stream[:, tile, fh * 512:(fh + 1) * 512])
            evm[ti % 4] = c.inc(ins, "dv")
    c.wait(nc.scalar, "dv")
    emit_ln(c, stream, tiles, silu32, stats)


def emit_l0_attn(c, stream, W, B, scrA, scrB, cc_in, cc_out):
    nc = c.nc
    paug, pload = B["paug"], B["pload"]
    paugA, paugB = B["paugA"], B["paugB"]
    ins = nc.sync.dma_start(out=pload[0:64, :], in_=W["projT"])
    c.inc(ins, "ld", 16)
    ins = nc.sync.dma_start(out=pload[64:128, :], in_=W["projT"])
    c.inc(ins, "ld", 16)
    ins = nc.vector.memset(paug[:], 0.0)
    c.inc(ins, "dv")
    ins = nc.vector.memset(paug[64:128, 256:257], -0.5 * PNORM * PNORM)
    c.inc(ins, "dv")
    ins = nc.vector.memset(paugA[:], 0.0)
    c.inc(ins, "dv")
    ins = nc.vector.memset(paugB[:], 0.0)
    c.inc(ins, "dv")
    ins = nc.vector.memset(paugB[:, 256:257], -0.5 * PNORM * PNORM)
    c.inc(ins, "dv")
    c.wait(nc.scalar, "ld")
    c.wait(nc.scalar, "dv")
    ins = nc.scalar.mul(paug[0:64, 0:256], pload[0:64, :], PNORM)
    c.inc(ins, "ac")
    ins = nc.scalar.mul(paugA[:, 0:256], pload[:], PNORM)
    c.inc(ins, "ac")
    ins = nc.vector.memset(B["kmax"][:, 0:1], -3.0e38)
    c.inc(ins, "dv")
    ins = nc.vector.memset(B["kmax"][:, 3:4], PEPS * PRATIO)
    c.inc(ins, "dv")
    ins = nc.vector.memset(B["onesrow"][:], 1.0)
    c.inc(ins, "dv")
    ins = nc.gpsimd.dma_start(out=B["wvt"][:], in_=W["wv"].rearrange(
        "(cc p) f -> p cc f", p=128))
    c.inc(ins, "gd", 16)
    ins = nc.gpsimd.dma_start(out=B["borow"][:], in_=W["bo_row"])
    c.inc(ins, "gd", 16)
    emit_join(c)

    emit_spill(c, stream, 1, scrB)
    cv = Carver(stream.ap()[:, NTH:NT, :].rearrange("p a b -> p (a b)"))
    emit_l0_attn_pass1(c, stream, W, B, 0, cv)
    emit_join(c)
    emit_spill(c, stream, 0, scrA)
    emit_unspill(c, stream, 1, scrB)
    cv = Carver(stream.ap()[:, 0:NTH, :].rearrange("p a b -> p (a b)"))
    emit_l0_attn_pass1(c, stream, W, B, 1, cv)
    emit_join(c)

    c.wait(nc.tensor, "dv")
    ins = nc.tensor.transpose(B["p_a"][0:1, 0:128], B["kmax"][:, 0:1],
                              B["idf32"][:])
    c.inc(ins, "pe")
    c.wait(nc.vector, "pe")
    ins = nc.vector.reduce_max(B["kmax"][0:1, 1:2], B["p_a"][0:1, 0:128],
                               axis=AX.X)
    c.inc(ins, "dv")
    c.wait(nc.vector, "dv")
    nc.vector.memset(B["stats"][0:1, 0:8], 0.0)
    ins = nc.vector.tensor_copy(B["stats"][0:1, 0:1], B["kmax"][0:1, 1:2])
    c.inc(ins, "dv")
    c.wait(nc.gpsimd, "dv")
    ins = nc.gpsimd.dma_start(out=cc_in, in_=B["stats"][0:1, 0:8])
    c.inc(ins, "gd", 16)
    c.wait(nc.gpsimd, "gd")
    ins = nc.gpsimd.collective_compute(
        "AllReduce", ALU.max, replica_groups=[list(range(8))],
        ins=[cc_in.opt()], outs=[cc_out.opt()])
    c.inc(ins, "cc")
    c.wait(nc.gpsimd, "cc", 1)
    ins = nc.gpsimd.dma_start(out=B["kmax"][0:1, 1:2], in_=cc_out[0:1, 0:1])
    c.inc(ins, "gd", 16)
    c.wait(nc.scalar, "gd")
    ins = nc.scalar.activation(out=B["kmax"][0:1, 2:3], in_=B["kmax"][0:1, 1:2],
                               func=AF.Exp, scale=-1.0)
    c.inc(ins, "ac")
    c.wait(nc.vector, "ac")
    nc.vector.tensor_scalar_mul(B["kmax"][0:1, 2:3], B["kmax"][0:1, 2:3], PRATIO)
    ins = nc.vector.tensor_copy(B["kmbf"][0:1, 1:2], B["kmax"][0:1, 2:3])
    c.inc(ins, "dv")
    c.wait(nc.tensor, "dv")
    ins = nc.tensor.matmul(B["p_c0"][:, 0:1], B["onesrow"][0:1, 0:128],
                           B["kmbf"][0:1, 1:2], start=True, stop=True)
    c.inc(ins, "pe")
    c.wait(nc.scalar, "pe")
    ins = nc.scalar.copy(B["acol"][:, 0:1], B["p_c0"][:, 0:1])
    c.inc(ins, "ac")
    emit_join(c)

    cv = Carver(stream.ap()[:, 0:NTH, :].rearrange("p a b -> p (a b)"))
    emit_l0_attn_pass2(c, stream, W, B, 1, cv, rebuild_xfm=False)
    emit_join(c)
    emit_spill(c, stream, 1, scrB)
    emit_unspill(c, stream, 0, scrA)
    cv = Carver(stream.ap()[:, NTH:NT, :].rearrange("p a b -> p (a b)"))
    emit_l0_attn_pass2(c, stream, W, B, 0, cv, rebuild_xfm=True)
    emit_join(c)
    emit_unspill(c, stream, 1, scrB)
    c.wait(nc.sync, "sp")
    emit_join(c)


def prep_l0_attn_weights(inp):
    return {
        "wq": np.ascontiguousarray(np.asarray(inp["L0_attn_wq"])),
        "wk": np.ascontiguousarray(np.asarray(inp["L0_attn_wk"])),
        "wv": np.ascontiguousarray(np.asarray(inp["L0_attn_wv"])),
        "wo": np.ascontiguousarray(np.asarray(inp["L0_attn_wo"])),
        "bo_row": np.ascontiguousarray(
            np.asarray(inp["L0_attn_bo"]).reshape(1, D)),
        "projT": np.ascontiguousarray(np.asarray(inp["L0_attn_proj"]).T),
    }


L0_ATTN_SHAPES = {"wq": [D, D], "wk": [D, D], "wv": [D, D], "wo": [D, D],
                  "bo_row": [1, D], "projT": [64, 256]}




def emit_l1_ffn_rps(c, stream, W, B, half, cv):
    nc = c.nc
    xfm, stats, silu32 = B["xfm"], B["stats"], B["silu32"]
    w1b, w2b, rw3, rb1t, rb2t = B["w1b"], B["w2b"], B["rw3"], B["rb1t"], B["rb2t"]
    utb, iota8, selbuf, slotbuf = B["utb"], B["iota8"], B["selbuf"], B["slotbuf"]
    offs = B["offs"]
    p_h1, p_h2, p_small = B["p_h1"], B["p_h2"], B["p_small"]
    p_l3 = p_small[:, 0:32]
    p_pb = p_small[:, 32:40]
    p_cs = p_small[0:1, 40:48]
    dispT = B["dispT"]
    tiles = list(range(half * NTH, (half + 1) * NTH))

    h1 = cv.bf16([128, 32, 1024])
    h2sb = B["h2sb"]
    t8a, t8b, t8c = B["t8a"], B["t8b"], B["t8c"]
    c8b, sc8, m8 = B["c8b"], B["sc8"], B["m8"]
    i8 = B["i8"]
    xbf = B["xbf"]
    p_eo = B["p_eo"]
    pbs = [p_h1[:], p_h2[:], p_eo[:, 0:512], p_eo[:, 512:1024]]

    c.wait(nc.vector, "sp")
    emit_tm_to_fm(c, stream, tiles, xfm, B)
    ins = nc.vector.memset(offs[0:1, 0:8], 0.0)
    c.inc(ins, "dv")

    for tt in range(2):
        c.wait(nc.tensor, "ac")
        c.wait(nc.tensor, "sp")
        for H1c in range(32):
            wb = w1b[H1c % 2]
            c.guard_buf(nc.gpsimd, "l1f_w1b%d" % (H1c % 2))
            ins = nc.gpsimd.dma_start(
                out=wb[:],
                in_=W["rW1"][:, H1c * 128:(H1c + 1) * 128].rearrange(
                    "(cc p) m -> p cc m", p=128))
            c.inc(ins, "gd", 16)
            gd_w = c.n["gd"]
            nc.tensor.wait_ge(c.s["gd"], gd_w)
            for win in range(2):
                ttw = slice(tt * 1024 + win * 512, tt * 1024 + win * 512 + 512)
                pb = pbs[(2 * H1c + win) % 4]
                if H1c >= 1:
                    c.wait(nc.tensor, "ac", c.n["ac"] - 2)
                for cc in range(NCC):
                    lm = nc.tensor.matmul(pb, wb[:, cc, :], xfm[:, cc, ttw],
                                          start=(cc == 0), stop=(cc == NCC - 1))
                c.inc(lm, "pe")
                c.consumed("l1f_w1b%d" % (H1c % 2))
                c.wait(nc.scalar, "pe")
                ins = nc.scalar.activation(
                    out=h1[:, H1c, win * 512:(win + 1) * 512], in_=pb,
                    func=AF.Relu, bias=rb1t[:, H1c:H1c + 1])
                c.inc(ins, "ac")
        c.wait(nc.tensor, "ac")
        p_l3 = p_small[:, 0:64]
        pend_l3 = None
        relm = {}

        def emit_l3(pl):
            par, rm, h2c = pl
            nc.tensor.wait_ge(c.s["ac"], rm)
            for sl in range(8):
                lm = nc.tensor.matmul(
                    p_l3[:, sl * 8:(sl + 1) * 8],
                    h2sb[:, par, sl * 128:(sl + 1) * 128],
                    rw3[:, h2c, :], start=(h2c == 0 and sl == 0), stop=False,
                    skip_group_check=True)
            relm[h2c] = c.inc(lm, "pe")

        for H2c in range(32):
            wb = w2b[H2c % 2]
            par = H2c % 2
            c.guard_buf(nc.gpsimd, "l1f_w2b%d" % (H2c % 2))
            ins = nc.gpsimd.dma_start(
                out=wb[:, 0:16, :],
                in_=W["rW2"][0:2048, H2c * 128:(H2c + 1) * 128].rearrange(
                    "(hc p) m -> p hc m", p=128))
            c.inc(ins, "gd", 16)
            ins = nc.gpsimd.dma_start(
                out=wb[:, 16:32, :],
                in_=W["rW2"][2048:4096, H2c * 128:(H2c + 1) * 128].rearrange(
                    "(hc p) m -> p hc m", p=128))
            c.inc(ins, "gd", 16)
            gd_w = c.n["gd"]
            nc.tensor.wait_ge(c.s["gd"], gd_w)
            for win in range(2):
                pb = pbs[(2 * H2c + win) % 4]
                if H2c >= 1:
                    c.wait(nc.tensor, "ac", c.n["ac"] - 2)
                for hc in range(32):
                    lm = nc.tensor.matmul(
                        pb, wb[:, hc, :],
                        h1[:, hc, win * 512:(win + 1) * 512],
                        start=(hc == 0), stop=(hc == 31))
                c.inc(lm, "pe")
                c.consumed("l1f_w2b%d" % (H2c % 2))
                if win == 0 and pend_l3 is not None:
                    emit_l3(pend_l3)
                    pend_l3 = None
                c.wait(nc.scalar, "pe")
                prev = relm.get(H2c - 2)
                if prev is not None:
                    nc.scalar.wait_ge(c.s["pe"], prev)
                ins = nc.scalar.activation(
                    out=h2sb[:, par, win * 512:(win + 1) * 512], in_=pb,
                    func=AF.Relu, bias=rb2t[:, H2c:H2c + 1])
                c.inc(ins, "ac")
            pend_l3 = (par, c.n["ac"], H2c)
        emit_l3(pend_l3)
        for sl in range(8):
            lm = nc.tensor.matmul(p_l3[:, sl * 8:(sl + 1) * 8],
                                  B["onesrow"][0:1, 0:128], B["rb3r"][0:1, :],
                                  start=False, stop=(sl == 7),
                                  skip_group_check=True)
        c.inc(lm, "pe")
        for sl in range(8):
            ts = tt * 8 + sl
            l3 = p_l3[:, sl * 8:(sl + 1) * 8]
            c.wait(nc.vector, "pe")
            vw(c, nc.vector.reduce_max(stats[:, 0:1], l3, axis=AX.X))
            vw(c, nc.vector.tensor_scalar_mul(stats[:, 1:2], stats[:, 0:1], -1.0))
            c.wait(nc.scalar, "dv")
            ins = nc.scalar.activation(out=sc8[:], in_=l3, func=AF.Exp,
                                       bias=stats[:, 1:2],
                                       accum_out=stats[:, 2:3])
            c.inc(ins, "ac")
            c.wait(nc.vector, "ac")
            vw(c, nc.vector.reciprocal(stats[:, 3:4], stats[:, 2:3]))
            vw(c, nc.vector.tensor_scalar_mul(sc8[:], sc8[:], stats[:, 3:4]))
            vw(c, nc.vector.max(m8[:], sc8[:]))
            vw(c, nc.vector.max_index(i8[:], m8[:], sc8[:]))
            vw(c, nc.vector.tensor_copy(selbuf[:, half, ts, 0:1], i8[:, 0:1]))
            vw(c, nc.vector.tensor_copy(selbuf[:, half, ts, 1:2], i8[:, 1:2]))
            vw(c, nc.vector.tensor_copy(selbuf[:, half, ts, 2:3], m8[:, 0:1]))
            vw(c, nc.vector.tensor_copy(selbuf[:, half, ts, 3:4], m8[:, 1:2]))
    for ts in range(NTH):
        e1 = selbuf[:, half, ts, 0:1]
        e2 = selbuf[:, half, ts, 1:2]
        c.wait(nc.vector, "pe")
        nc.vector.tensor_scalar(out=t8a[:], in0=iota8[:], scalar1=e1,
                                scalar2=None, op0=ALU.is_equal)
        ins = nc.vector.tensor_scalar(out=t8b[:], in0=iota8[:], scalar1=e2,
                                      scalar2=None, op0=ALU.is_equal)
        c.inc(ins, "dv")
        c.wait(nc.vector, "dv")
        vw(c, nc.vector.tensor_add(t8c[:], t8a[:], t8b[:]))
        vw(c, nc.vector.tensor_copy(c8b[:], t8c[:]))
        c.wait(nc.tensor, "dv")
        nc.tensor.matmul(p_pb, utb[:], c8b[:], start=True, stop=False,
                         skip_group_check=True)
        ins = nc.tensor.matmul(p_pb, B["onesrow32"][0:1, 0:128],
                               offs[0:1, 0:8], start=False, stop=True,
                               skip_group_check=True)
        c.inc(ins, "pe")
        ins = nc.tensor.matmul(p_cs, B["onescol"][:, 0:1], c8b[:],
                               start=True, stop=True, skip_group_check=True)
        c.inc(ins, "pe")
        c.wait(nc.vector, "pe")
        if "dbg_pb" in B:
            nc.vector.tensor_copy(B["dbg_pb"][:], p_pb)
            nc.vector.tensor_copy(B["dbg_t8c"][:], t8c[:])
            nc.vector.tensor_copy(B["dbg_c8b"][:], c8b[:])
        vw(c, nc.vector.tensor_mul(t8a[:], t8a[:], p_pb))
        vw(c, nc.vector.reduce_sum(stats[:, 0:1], t8a[:], axis=AX.X))
        vw(c, nc.vector.tensor_mul(t8b[:], t8b[:], p_pb))
        vw(c, nc.vector.reduce_sum(stats[:, 1:2], t8b[:], axis=AX.X))
        vw(c, nc.vector.tensor_tensor(stats[:, 2:3], e1, e2, ALU.is_equal))
        vw(c, nc.vector.tensor_add(stats[:, 1:2], stats[:, 1:2], stats[:, 2:3]))
        vw(c, nc.vector.tensor_add(offs[0:1, 0:8], offs[0:1, 0:8], p_cs))
        for kk in range(2):
            pos = stats[:, kk:kk + 1]
            ekf = selbuf[:, half, ts, kk:kk + 1]
            skf = selbuf[:, half, ts, 2 + kk:3 + kk]
            keep = stats[:, 4:5]
            vw(c, nc.vector.tensor_scalar(out=keep, in0=pos, scalar1=float(CAP),
                                          scalar2=None, op0=ALU.is_lt))
            import os as _os
            if _os.environ.get("DBG_L1F_ZS"):
                vw(c, nc.vector.memset(skf, 0.0))
            else:
                vw(c, nc.vector.tensor_mul(skf, skf, keep))
            vw(c, nc.vector.tensor_scalar(out=stats[:, 5:6], in0=ekf,
                                          scalar1=640.0,
                                          scalar2=float(half * CAP),
                                          op0=ALU.mult, op1=ALU.add))
            vw(c, nc.vector.tensor_add(stats[:, 5:6], stats[:, 5:6], pos))
            vw(c, nc.vector.tensor_mul(stats[:, 5:6], stats[:, 5:6], keep))
            vw(c, nc.vector.tensor_scalar(out=stats[:, 6:7], in0=keep,
                                          scalar1=-float(E1 * 2 * CAP),
                                          scalar2=float(E1 * 2 * CAP),
                                          op0=ALU.mult, op1=ALU.add))
            vw(c, nc.vector.tensor_add(stats[:, 5:6], stats[:, 5:6], stats[:, 6:7]))
            vw(c, nc.vector.tensor_copy(slotbuf[:, half, ts, kk:kk + 1],
                                        stats[:, 5:6]))
        ins = nc.vector.tensor_copy(xbf[:], stream[:, tiles[ts], :])
        c.inc(ins, "dv")
        c.wait(nc.gpsimd, "dv")
        for kk in range(2):
            ins = nc.gpsimd.indirect_dma_start(
                out=dispT,
                out_offset=IndirectOffsetOnAxis(
                    ap=slotbuf[:, half, ts, kk:kk + 1], axis=0),
                in_=xbf[:], in_offset=None)
            c.inc(ins, "gd", 16)


def emit_l1_ffn_experts(c, W, B, cv):
    nc = c.nc
    ew1, ew2 = B["ew1"], B["ew2"]
    ebt1, ebt2 = B["ebt1"], B["ebt2"]
    p_h1, p_h2, p_eo = B["p_h1"], B["p_h2"], B["p_eo"]
    dispT, eoutD = B["dispT"], B["eoutD"]
    NR = 2 * CAP

    hh = cv.bf16([128, 32, NR])
    efo = cv.f32([128, NR])
    tmr = cv.f32([128, 5, 128])
    dfs = [B["df"], cv.bf16([128, NCC, NR])]

    sdm = {}
    dfree = {}

    def prefetch(e):
        if e >= E1:
            return
        if e >= 2:
            nc.scalar.wait_ge(c.s["pe"], dfree[e - 2])
        dfx = dfs[e % 2]
        for ccd in range(NCC):
            ins = nc.scalar.dma_start_transpose(
                dfx[:, ccd, :],
                dispT[e * NR:(e + 1) * NR, ccd * 128:(ccd + 1) * 128])
            c.inc(ins, "sd", 16)
        sdm[e] = c.n["sd"]

    prefetch(0)
    prefetch(1)
    for e in range(E1):
        df = dfs[e % 2]
        c.wait(nc.sync, "ac")
        ins = nc.sync.dma_start(out=ebt1[:], in_=W["eb1t"][e])
        c.inc(ins, "ld", 16)
        ins = nc.sync.dma_start(out=ebt2[:], in_=W["eb2t"][e])
        c.inc(ins, "ld", 16)
        ld_b = c.n["ld"]
        ac_h = c.n["ac"]

        def _issue_w1(Hc):
            wbx = ew1[Hc % 2]
            c.guard_buf(nc.gpsimd, "l1f_w1b%d" % (Hc % 2))
            ins = nc.gpsimd.dma_start(
                out=wbx[:],
                in_=W["eW1"][e][:, Hc * 128:(Hc + 1) * 128].rearrange(
                    "(cc p) m -> p cc m", p=128))
            c.inc(ins, "gd", 16)
            return c.n["gd"]
        gd_marks1 = [_issue_w1(0), _issue_w1(1)]
        for Hc in range(32):
            wb = ew1[Hc % 2]
            gd_w = gd_marks1[Hc]
            nc.tensor.wait_ge(c.s["gd"], gd_w)
            nc.tensor.wait_ge(c.s["sd"], sdm[e])
            if Hc >= 2:
                nc.tensor.wait_ge(c.s["ac"], ac_h + 2 * (Hc - 1))
            else:
                nc.tensor.wait_ge(c.s["ld"], ld_b)
                c.wait(nc.tensor, "ac")
                c.wait(nc.tensor, "dv")
            for nh in range(2):
                pb = p_h1 if nh == 0 else p_h2
                for cc in range(NCC):
                    lm = nc.tensor.matmul(
                        pb[:, 0:CAP], wb[:, cc, :],
                        df[:, cc, nh * CAP:(nh + 1) * CAP],
                        start=(cc == 0), stop=(cc == NCC - 1))
                w1_pe = c.inc(lm, "pe")
                c.consumed("l1f_w1b%d" % (Hc % 2))
                c.wait(nc.scalar, "pe")
                ins = nc.scalar.activation(out=hh[:, Hc, nh * CAP:(nh + 1) * CAP],
                                           in_=pb[:, 0:CAP], func=AF.Relu,
                                           bias=ebt1[:, Hc:Hc + 1])
                c.inc(ins, "ac")
            if Hc + 2 < 32:
                gd_marks1.append(_issue_w1(Hc + 2))
        dfree[e] = w1_pe
        prefetch(e + 1)
        c.wait(nc.tensor, "ac")

        def _issue_w2(fc):
            wbx = ew2[fc % 2]
            c.guard_buf(nc.gpsimd, "l1f_w2b%d" % (fc % 2))
            ins = nc.gpsimd.dma_start(
                out=wbx[:],
                in_=W["eW2"][e][:, fc * 128:(fc + 1) * 128].rearrange(
                    "(hc p) m -> p hc m", p=128))
            c.inc(ins, "gd", 16)
            return c.n["gd"]
        gd_marks2 = [_issue_w2(0), _issue_w2(1)]
        for fc in range(NCC):
            wb = ew2[fc % 2]
            gd_w = gd_marks2[fc]
            nc.tensor.wait_ge(c.s["gd"], gd_w)
            if fc >= 2:
                c.wait(nc.tensor, "ac")
                c.wait(nc.tensor, "dv")
            for nh in range(2):
                pb = p_h1 if nh == 0 else p_h2
                for Hc in range(32):
                    lm = nc.tensor.matmul(
                        pb[:, 0:CAP], wb[:, Hc, :],
                        hh[:, Hc, nh * CAP:(nh + 1) * CAP],
                        start=(Hc == 0), stop=(Hc == 31))
                c.inc(lm, "pe")
                c.consumed("l1f_w2b%d" % (fc % 2))
                c.wait(nc.scalar, "pe")
                ins = nc.scalar.activation(
                    out=efo[:, nh * CAP:(nh + 1) * CAP], in_=pb[:, 0:CAP],
                    func=AF.Identity, bias=ebt2[:, fc:fc + 1])
                c.inc(ins, "ac")
            if fc + 2 < NCC:
                gd_marks2.append(_issue_w2(fc + 2))
            c.wait(nc.tensor, "ac")
            for j in range(5):
                ins = nc.tensor.transpose(
                    p_eo[:, j * 128:(j + 1) * 128],
                    efo[:, j * 128:(j + 1) * 128], B["idf32"][:])
            c.inc(ins, "pe")
            c.wait(nc.vector, "pe")
            ins = nc.vector.tensor_copy(tmr[:], p_eo[:, 0:640].rearrange(
                "p (j m) -> p j m", j=5))
            c.inc(ins, "dv")
            c.wait(nc.gpsimd, "dv")
            ins = nc.gpsimd.dma_start(
                out=eoutD[e * NR:(e + 1) * NR,
                          fc * 128:(fc + 1) * 128].rearrange(
                    "(j p) m -> p j m", p=128),
                in_=tmr[:])
            c.inc(ins, "gd", 16)
    c.wait(nc.gpsimd, "gd")


def emit_l1_ffn_gather(c, stream, B, half, spilled):
    nc = c.nc
    selbuf, slotbuf = B["selbuf"], B["slotbuf"]
    eoutD = B["eoutD"]
    stats, silu32 = B["stats"], B["silu32"]
    tiles = list(range(half * NTH, (half + 1) * NTH))
    cv = Carver(stream.ap()[:, spilled * NTH:(spilled + 1) * NTH, :]
                .rearrange("p a b -> p (a b)"))
    gbufs = [cv.f32([128, 1024]) for _ in range(4)]
    c.wait(nc.vector, "sp")
    c.wait(nc.gpsimd, "sp")
    gm, am = {}, {}
    for i, (ts, kk) in enumerate((t, k) for t in range(NTH) for k in range(2)):
        if i >= 4:
            nc.gpsimd.wait_ge(c.s["dv"], am[i - 4])
        ins = nc.gpsimd.indirect_dma_start(
            out=gbufs[i % 4][:], out_offset=None,
            in_=eoutD,
            in_offset=IndirectOffsetOnAxis(
                ap=slotbuf[:, half, ts, kk:kk + 1], axis=0))
        gm[i] = c.inc(ins, "gd", 16)
        nc.vector.wait_ge(c.s["gd"], gm[i])
        ins = nc.vector.scalar_tensor_tensor(
            out=stream[:, tiles[ts], :], in0=gbufs[i % 4][:],
            scalar=selbuf[:, half, ts, 2 + kk:3 + kk],
            in1=stream[:, tiles[ts], :], op0=ALU.mult, op1=ALU.add)
        am[i] = c.inc(ins, "dv")
    c.wait(nc.scalar, "dv")
    c.wait(nc.scalar, "sp")
    emit_ln(c, stream, tiles, silu32, stats)


def emit_l1_ffn(c, stream, W, B, scrA, scrB):
    nc = c.nc
    ins = nc.gpsimd.dma_start(out=B["rw3"][:], in_=W["rW3"].rearrange(
        "(hc p) m -> p hc m", p=128))
    c.inc(ins, "gd", 16)
    ins = nc.sync.dma_start(out=B["rb1t"][:], in_=W["rb1t"])
    c.inc(ins, "ld", 16)
    ins = nc.sync.dma_start(out=B["rb2t"][:], in_=W["rb2t"])
    c.inc(ins, "ld", 16)
    ins = nc.gpsimd.dma_start(out=B["rb3r"][:], in_=W["rb3"])
    c.inc(ins, "gd", 16)
    ins = nc.gpsimd.dma_start(out=B["utb"][:], in_=B["cin_ut"])
    c.inc(ins, "gd", 16)
    ins = nc.sync.dma_start(out=B["iota8"][:], in_=B["cin_iota"])
    c.inc(ins, "ld", 16)
    ins = nc.vector.memset(B["onesrow"][:], 1.0)
    ins = nc.vector.memset(B["onescol"][:], 1.0)
    ins = nc.vector.memset(B["onesrow32"][:], 1.0)
    ins = nc.vector.memset(B["g"][:], 0.0)
    c.inc(ins, "dv")
    c.wait(nc.gpsimd, "dv")
    ins = nc.gpsimd.dma_start(out=B["eoutD"][E1 * 2 * CAP:E1 * 2 * CAP + 1, :],
                              in_=B["g"][0:1, :])
    c.inc(ins, "gd", 16)
    emit_join(c)

    emit_spill(c, stream, 1, scrB)
    cv = Carver(stream.ap()[:, NTH:NT, :].rearrange("p a b -> p (a b)"))
    emit_l1_ffn_rps(c, stream, W, B, 0, cv)
    emit_join(c)
    emit_spill(c, stream, 0, scrA)
    emit_unspill(c, stream, 1, scrB)
    cv = Carver(stream.ap()[:, 0:NTH, :].rearrange("p a b -> p (a b)"))
    emit_l1_ffn_rps(c, stream, W, B, 1, cv)
    emit_join(c)
    import os as _os
    if _os.environ.get("DBG_L1F"):
        ins = nc.vector.tensor_copy(
            stream[:, NTH, 0:128],
            B["selbuf"].ap().rearrange("p a b cdim -> p (a b cdim)"))
        c.inc(ins, "dv")
        ins = nc.vector.tensor_copy(
            stream[:, NTH + 1, 0:64],
            B["slotbuf"].ap().rearrange("p a b cdim -> p (a b cdim)"))
        c.inc(ins, "dv")
        ins = nc.vector.tensor_copy(stream[:, NTH + 2, 0:128], B["utb"][:])
        nc.vector.tensor_copy(stream[:, NTH + 3, 0:8], B["dbg_t8c"][:])
        nc.vector.tensor_copy(stream[:, NTH + 3, 8:16], B["dbg_c8b"][:])
        nc.vector.tensor_copy(stream[:, NTH + 3, 16:24], B["iota8"][:])
        nc.vector.tensor_copy(stream[0:1, NTH + 3, 24:32], B["offs"][0:1, 0:8])
        ins2 = nc.vector.tensor_copy(stream[:, NTH + 3, 32:40], B["dbg_pb"][:])
        c.inc(ins2, "dv")
        emit_join(c)
        emit_unspill(c, stream, 0, scrA)
        c.wait(nc.sync, "sp")
        emit_join(c)
        return
    cv = Carver(stream.ap()[:, 0:NTH, :].rearrange("p a b -> p (a b)"))
    emit_l1_ffn_experts(c, W, B, cv)
    emit_join(c)
    emit_l1_ffn_gather(c, stream, B, 1, 0)
    emit_join(c)
    emit_spill(c, stream, 1, scrB)
    emit_unspill(c, stream, 0, scrA)
    emit_l1_ffn_gather(c, stream, B, 0, 1)
    emit_join(c)
    emit_unspill(c, stream, 1, scrB)
    c.wait(nc.sync, "sp")
    emit_join(c)


def prep_l1_ffn_weights(inp):
    rb1 = np.asarray(inp["L1_ffn_rb1"])
    rb2 = np.asarray(inp["L1_ffn_rb2"])
    eb1 = np.asarray(inp["L1_ffn_eb1"])
    return {
        "rW1": np.ascontiguousarray(np.asarray(inp["L1_ffn_rW1"])),
        "rW2": np.ascontiguousarray(np.asarray(inp["L1_ffn_rW2"])),
        "rW3": np.ascontiguousarray(np.asarray(inp["L1_ffn_rW3"])),
        "rb1t": np.ascontiguousarray(rb1.reshape(32, 128).T),
        "rb2t": np.ascontiguousarray(rb2.reshape(32, 128).T),
        "rb3": np.ascontiguousarray(np.asarray(inp["L1_ffn_rb3"]).reshape(1, 8)),
        "eW1": np.ascontiguousarray(np.asarray(inp["L1_ffn_eW1"])),
        "eW2": np.ascontiguousarray(np.asarray(inp["L1_ffn_eW2"])),
        "eb1t": np.ascontiguousarray(
            eb1.reshape(E1, 32, 128).transpose(0, 2, 1)),
        "eb2t": np.ascontiguousarray(
            np.asarray(inp["L1_ffn_eb2"]).reshape(E1, 8, 128).transpose(0, 2, 1)),
    }


L1_FFN_SHAPES = {"rW1": [D, H1R], "rW2": [H1R, H1R], "rW3": [H1R, 8],
                 "rb1t": [128, 32], "rb2t": [128, 32], "rb3": [1, 8],
                 "eW1": [E1, D, H1E], "eW2": [E1, H1E, D],
                 "eb1t": [E1, 128, 32], "eb2t": [E1, 128, 8]}


CAP0 = 768


def emit_l0_sparse_rps(c, stream, W, B, half, cv):
    nc = c.nc
    xfm, stats, silu32 = B["xfm"], B["stats"], B["silu32"]
    rt, utb, iota8 = B["rt"], B["utb"], B["iota8"]
    selbuf, slotbuf, offs = B["selbuf"], B["slotbuf"], B["offs"]
    t8a, t8c, c8b = B["t8a"], B["t8c"], B["c8b"]
    m8, i8 = B["m8"], B["i8"]
    xbf = B["xcast"][0]
    p_u, p_g, p_y = B["p_u"], B["p_g"], B["p_y"]
    p_s = B["p_s"]
    p_r = p_s[:, 0:8]
    p_pb = p_s[:, 32:40]
    p_cs = p_s[0:1, 40:48]
    dispL0 = B["dispL0"]
    tiles = list(range(half * NTH, (half + 1) * NTH))

    hstore = cv.bf16([128, 16, 2048])

    c.wait(nc.vector, "sp")
    emit_tm_to_fm(c, stream, tiles, xfm, B)
    c.wait(nc.tensor, "gd")
    c.wait(nc.tensor, "ac")
    c.wait(nc.tensor, "sp")
    for i in range(NTH):
        ti = half * NTH + i
        c.wait(nc.tensor, "dv")
        for cc in range(NCC):
            last = nc.tensor.matmul(p_r, xfm[:, cc, i * 128:(i + 1) * 128],
                                    rt[:, cc, :], start=(cc == 0),
                                    stop=(cc == NCC - 1), skip_group_check=True)
        c.inc(last, "pe")
        c.wait(nc.vector, "pe")
        vw(c, nc.vector.max(m8[:], p_r))
        vw(c, nc.vector.max_index(i8[:], m8[:], p_r))
        vw(c, nc.vector.tensor_copy(selbuf[:, ti, 0:1], i8[:, 0:1]))
        e1 = selbuf[:, ti, 0:1]
        vw(c, nc.vector.tensor_scalar(out=t8a[:], in0=iota8[:], scalar1=e1,
                                      scalar2=None, op0=ALU.is_equal))
        vw(c, nc.vector.tensor_copy(c8b[:], t8a[:]))
        c.wait(nc.tensor, "dv")
        nc.tensor.matmul(p_pb, utb[:], c8b[:], start=True, stop=False,
                         skip_group_check=True)
        ins = nc.tensor.matmul(p_pb, B["onesrow32"][0:1, 0:128],
                               offs[0:1, 0:8], start=False, stop=True,
                               skip_group_check=True)
        c.inc(ins, "pe")
        ins = nc.tensor.matmul(p_cs, B["onescol"][:, 0:1], c8b[:],
                               start=True, stop=True, skip_group_check=True)
        c.inc(ins, "pe")
        c.wait(nc.vector, "pe")
        vw(c, nc.vector.tensor_mul(t8c[:], t8a[:], p_pb))
        vw(c, nc.vector.reduce_sum(stats[:, 0:1], t8c[:], axis=AX.X))
        vw(c, nc.vector.tensor_add(offs[0:1, 0:8], offs[0:1, 0:8], p_cs))
        pos = stats[:, 0:1]
        keep = stats[:, 4:5]
        vw(c, nc.vector.tensor_scalar(out=keep, in0=pos, scalar1=float(CAP0),
                                      scalar2=None, op0=ALU.is_lt))
        vw(c, nc.vector.tensor_scalar(out=stats[:, 5:6], in0=e1,
                                      scalar1=float(CAP0), scalar2=None,
                                      op0=ALU.mult))
        vw(c, nc.vector.tensor_add(stats[:, 5:6], stats[:, 5:6], pos))
        vw(c, nc.vector.tensor_mul(stats[:, 5:6], stats[:, 5:6], keep))
        vw(c, nc.vector.tensor_scalar(out=stats[:, 6:7], in0=keep,
                                      scalar1=-float(E0 * CAP0),
                                      scalar2=float(E0 * CAP0),
                                      op0=ALU.mult, op1=ALU.add))
        vw(c, nc.vector.tensor_add(stats[:, 5:6], stats[:, 5:6], stats[:, 6:7]))
        vw(c, nc.vector.tensor_copy(slotbuf[:, ti, 0:1], stats[:, 5:6]))
        vw(c, nc.vector.tensor_copy(xbf[:], stream[:, ti, :]))
        c.wait(nc.gpsimd, "dv")
        ins = nc.gpsimd.indirect_dma_start(
            out=dispL0,
            out_offset=IndirectOffsetOnAxis(ap=slotbuf[:, ti, 0:1], axis=0),
            in_=xbf[:], in_offset=None)
        c.inc(ins, "gd", 16)

    upT, gateT, downT = W["up_sT"], W["gate_sT"], W["down_sT"]
    wug, wdn = B["wug"], B["wdn"]
    pe_b, dv_b, ac_b = c.n["pe"], c.n["dv"], c.n["ac"]
    k = 0
    for hp in range(8):
        wb = wug[hp % 2]
        c.guard_buf(nc.gpsimd, "l0f_wug%d" % (hp % 2))
        ins = nc.gpsimd.dma_start(
            out=wb[:, :, 0:256],
            in_=upT[:, hp * 256:(hp + 1) * 256].rearrange(
                "(cc p) hh -> p cc hh", p=128))
        c.inc(ins, "gd", 16)
        ins = nc.gpsimd.dma_start(
            out=wb[:, :, 256:512],
            in_=gateT[:, hp * 256:(hp + 1) * 256].rearrange(
                "(cc p) hh -> p cc hh", p=128))
        c.inc(ins, "gd", 16)
        gd_mark = c.n["gd"]
        for hh in range(2):
            hc = hp * 2 + hh
            for hw in range(4):
                tw = slice(hw * 512, (hw + 1) * 512)
                nc.tensor.wait_ge(c.s["gd"], gd_mark)
                if k >= 1:
                    nc.tensor.wait_ge(c.s["ac"], ac_b + k)
                    nc.tensor.wait_ge(c.s["dv"], dv_b + k)
                else:
                    c.wait(nc.tensor, "ac", ac_b)
                    c.wait(nc.tensor, "dv", dv_b)
                for cc in range(NCC):
                    nc.tensor.matmul(
                        p_u[:], wb[:, cc, hh * 128:(hh + 1) * 128],
                        xfm[:, cc, tw], start=(cc == 0),
                        stop=(cc == NCC - 1))
                for cc in range(NCC):
                    lg = nc.tensor.matmul(
                        p_g[:], wb[:, cc, 256 + hh * 128:256 + (hh + 1) * 128],
                        xfm[:, cc, tw], start=(cc == 0),
                        stop=(cc == NCC - 1))
                c.inc(lg, "pe")
                c.consumed("l0f_wug%d" % (hp % 2))
                c.wait(nc.scalar, "pe")
                if k >= 1:
                    c.wait(nc.scalar, "dv", dv_b + k)
                ins = nc.scalar.activation(out=silu32[:, 0:512],
                                           in_=p_g[:], func=AF.Silu)
                c.inc(ins, "ac")
                c.wait(nc.vector, "ac")
                ins = nc.vector.tensor_mul(hstore[:, hc, tw], p_u[:],
                                           silu32[:, 0:512])
                c.inc(ins, "dv")
                k += 1
    for ch in range(2):
        cw = slice(ch * 512, (ch + 1) * 512)
        c.guard_buf(nc.gpsimd, "l0f_wdn")
        ins = nc.gpsimd.dma_start(
            out=wdn[:, 0:8, :],
            in_=downT[0:1024, ch * 512:(ch + 1) * 512].rearrange(
                "(hc p) cf -> p hc cf", p=128))
        c.inc(ins, "gd", 16)
        ins = nc.gpsimd.dma_start(
            out=wdn[:, 8:16, :],
            in_=downT[1024:2048, ch * 512:(ch + 1) * 512].rearrange(
                "(hc p) cf -> p hc cf", p=128))
        c.inc(ins, "gd", 16)
        gd_dn = c.n["gd"]
        dv_c = c.n["dv"]
        for ts in range(NTH):
            pb = p_y[ts % 2]
            tile = tiles[ts]
            nc.tensor.wait_ge(c.s["gd"], gd_dn)
            c.wait(nc.tensor, "dv", dv_b + 64)
            if ts >= 2:
                c.wait(nc.tensor, "dv", dv_c + ts - 1)
            else:
                c.wait(nc.tensor, "dv", dv_c)
            for hc in range(16):
                last = nc.tensor.matmul(
                    pb[:], hstore[:, hc, ts * 128:(ts + 1) * 128],
                    wdn[:, hc, :], start=(hc == 0), stop=(hc == 15))
            c.inc(last, "pe")
            c.consumed("l0f_wdn")
            c.wait(nc.vector, "pe")
            ins = nc.vector.tensor_add(stream[:, tile, cw], pb[:],
                                       stream[:, tile, cw])
            c.inc(ins, "dv")


def emit_l0_sparse_experts(c, W, B, cv):
    nc = c.nc
    wug, wdn = B["wug"], B["wdn"]
    p_u, p_g, p_y = B["p_u"], B["p_g"], B["p_y"]
    dispL0, eoutL0 = B["dispL0"], B["eoutL0"]
    NS = CAP0 // 128

    hst = cv.bf16([128, 16, CAP0])
    efo = [cv.f32([128, 512]) for _ in range(4)]
    dfs = [cv.bf16([128, NCC, CAP0]) for _ in range(2)]

    sdm = {}
    dfree = {}

    def prefetch(e):
        if e >= E0:
            return
        if e >= 2:
            nc.scalar.wait_ge(c.s["pe"], dfree[e - 2])
        dfx = dfs[e % 2]
        for ccd in range(NCC):
            ins = nc.scalar.dma_start_transpose(
                dfx[:, ccd, :],
                dispL0[e * CAP0:(e + 1) * CAP0, ccd * 128:(ccd + 1) * 128])
            c.inc(ins, "sd", 16)
        sdm[e] = c.n["sd"]

    prefetch(0)
    prefetch(1)
    gdm = [0, 0, 0, 0]
    ym = [0, 0]
    for e in range(E0):
        df = dfs[e % 2]
        upT, gateT, downT = W["upT"][e], W["gateT"][e], W["downT"][e]
        ac_h = c.n["ac"]
        hm = {}
        k = 0
        for hp in range(8):
            wb = wug[hp % 2]
            c.guard_buf(nc.gpsimd, "l0f_wug%d" % (hp % 2))
            ins = nc.gpsimd.dma_start(
                out=wb[:, :, 0:256],
                in_=upT[:, hp * 256:(hp + 1) * 256].rearrange(
                    "(cc p) hh -> p cc hh", p=128))
            c.inc(ins, "gd", 16)
            ins = nc.gpsimd.dma_start(
                out=wb[:, :, 256:512],
                in_=gateT[:, hp * 256:(hp + 1) * 256].rearrange(
                    "(cc p) hh -> p cc hh", p=128))
            c.inc(ins, "gd", 16)
            gd_mark = c.n["gd"]
            for hh in range(2):
                hc = hp * 2 + hh
                for nh in range(2):
                    ns = slice(nh * 384, (nh + 1) * 384)
                    nc.tensor.wait_ge(c.s["gd"], gd_mark)
                    nc.tensor.wait_ge(c.s["sd"], sdm[e])
                    if k >= 1:
                        nc.tensor.wait_ge(c.s["ac"], ac_h + k)
                        nc.tensor.wait_ge(c.s["dv"], hm[k - 1])
                    else:
                        c.wait(nc.tensor, "ac")
                        c.wait(nc.tensor, "dv")
                    for cc in range(NCC):
                        nc.tensor.matmul(
                            p_u[:, 0:384], wb[:, cc, hh * 128:(hh + 1) * 128],
                            df[:, cc, ns], start=(cc == 0),
                            stop=(cc == NCC - 1))
                    for cc in range(NCC):
                        lg = nc.tensor.matmul(
                            p_g[:, 0:384],
                            wb[:, cc, 256 + hh * 128:256 + (hh + 1) * 128],
                            df[:, cc, ns], start=(cc == 0),
                            stop=(cc == NCC - 1))
                    ug_pe = c.inc(lg, "pe")
                    c.consumed("l0f_wug%d" % (hp % 2))
                    c.wait(nc.scalar, "pe")
                    ins = nc.scalar.activation(out=B["silu32"][:, 0:384],
                                               in_=p_g[:, 0:384], func=AF.Silu)
                    sm = c.inc(ins, "ac")
                    nc.vector.wait_ge(c.s["ac"], sm)
                    ins = nc.vector.tensor_mul(hst[:, hc, ns], p_u[:, 0:384],
                                               B["silu32"][:, 0:384])
                    hm[k] = c.inc(ins, "dv")
                    k += 1
        dfree[e] = ug_pe
        prefetch(e + 1)
        for ch in range(2):
            c.guard_buf(nc.gpsimd, "l0f_wdn")
            ins = nc.gpsimd.dma_start(
                out=wdn[:, 0:8, :],
                in_=downT[0:1024, ch * 512:(ch + 1) * 512].rearrange(
                    "(hc p) cf -> p hc cf", p=128))
            c.inc(ins, "gd", 16)
            ins = nc.gpsimd.dma_start(
                out=wdn[:, 8:16, :],
                in_=downT[1024:2048, ch * 512:(ch + 1) * 512].rearrange(
                    "(hc p) cf -> p hc cf", p=128))
            c.inc(ins, "gd", 16)
            gd_dn = c.n["gd"]
            for ts in range(NS):
                pb = p_y[ts % 2]
                slot = (ch * NS + ts) % 4
                nc.tensor.wait_ge(c.s["gd"], gd_dn)
                nc.tensor.wait_ge(c.s["dv"], hm[31])
                nc.tensor.wait_ge(c.s["dv"], ym[ts % 2])
                for hc in range(16):
                    last = nc.tensor.matmul(
                        pb[:], hst[:, hc, ts * 128:(ts + 1) * 128],
                        wdn[:, hc, :], start=(hc == 0), stop=(hc == 15))
                ypem = c.inc(last, "pe")
                c.consumed("l0f_wdn")
                nc.vector.wait_ge(c.s["pe"], ypem)
                nc.vector.wait_ge(c.s["gd"], gdm[slot])
                ins = nc.vector.tensor_copy(efo[slot][:], pb[:])
                em = ym[ts % 2] = c.inc(ins, "dv")
                nc.gpsimd.wait_ge(c.s["dv"], em)
                ins = nc.gpsimd.dma_start(
                    out=eoutL0[e * CAP0 + ts * 128:e * CAP0 + (ts + 1) * 128,
                               ch * 512:(ch + 1) * 512],
                    in_=efo[slot][:])
                gdm[slot] = c.inc(ins, "gd", 16)
    c.wait(nc.gpsimd, "gd")


def emit_l0_sparse_gather(c, stream, B, half, spilled):
    nc = c.nc
    slotbuf = B["slotbuf"]
    eoutL0 = B["eoutL0"]
    stats, silu32 = B["stats"], B["silu32"]
    tiles = list(range(half * NTH, (half + 1) * NTH))
    cv = Carver(stream.ap()[:, spilled * NTH:(spilled + 1) * NTH, :]
                .rearrange("p a b -> p (a b)"))
    gbufs = [cv.f32([128, 1024]) for _ in range(4)]
    c.wait(nc.vector, "sp")
    c.wait(nc.gpsimd, "sp")
    gm, am = {}, {}
    for ts in range(NTH):
        ti = tiles[ts]
        if ts >= 4:
            nc.gpsimd.wait_ge(c.s["dv"], am[ts - 4])
        ins = nc.gpsimd.indirect_dma_start(
            out=gbufs[ts % 4][:], out_offset=None,
            in_=eoutL0,
            in_offset=IndirectOffsetOnAxis(ap=slotbuf[:, ti, 0:1], axis=0))
        gm[ts] = c.inc(ins, "gd", 16)
        nc.vector.wait_ge(c.s["gd"], gm[ts])
        ins = nc.vector.tensor_add(stream[:, ti, :], gbufs[ts % 4][:],
                                   stream[:, ti, :])
        am[ts] = c.inc(ins, "dv")
    c.wait(nc.scalar, "dv")
    c.wait(nc.scalar, "sp")
    emit_ln(c, stream, tiles, silu32, stats)


def emit_l0_ffn_sparse(c, stream, W, B, scrA, scrB):
    nc = c.nc
    ins = nc.gpsimd.dma_start(out=B["rt"][:], in_=W["routerT_tiled"])
    c.inc(ins, "gd", 16)
    ins = nc.gpsimd.dma_start(out=B["utb"][:], in_=B["cin_ut"])
    c.inc(ins, "gd", 16)
    ins = nc.sync.dma_start(out=B["iota8"][:], in_=B["cin_iota"])
    c.inc(ins, "ld", 16)
    nc.vector.memset(B["onesrow32"][:], 1.0)
    nc.vector.memset(B["onescol"][:], 1.0)
    nc.vector.memset(B["offs"][0:1, 0:8], 0.0)
    ins = nc.vector.memset(B["g"][:], 0.0)
    c.inc(ins, "dv")
    c.wait(nc.gpsimd, "dv")
    ins = nc.gpsimd.dma_start(out=B["eoutL0"][E0 * CAP0:E0 * CAP0 + 1, :],
                              in_=B["g"][0:1, :])
    c.inc(ins, "gd", 16)
    emit_join(c)

    import os as _os
    _noexp = bool(_os.environ.get("L0SP_NOEXP"))
    emit_spill(c, stream, 1, scrB)
    cv = Carver(stream.ap()[:, NTH:NT, :].rearrange("p a b -> p (a b)"))
    emit_l0_sparse_rps(c, stream, W, B, 0, cv)
    emit_join(c)
    emit_spill(c, stream, 0, scrA)
    emit_unspill(c, stream, 1, scrB)
    cv = Carver(stream.ap()[:, 0:NTH, :].rearrange("p a b -> p (a b)"))
    emit_l0_sparse_rps(c, stream, W, B, 1, cv)
    emit_join(c)
    if _noexp:
        emit_ln(c, stream, list(range(NTH, NT)), B["silu32"], B["stats"])
        emit_join(c)
        emit_spill(c, stream, 1, scrB)
        emit_unspill(c, stream, 0, scrA)
        emit_ln(c, stream, list(range(0, NTH)), B["silu32"], B["stats"])
        emit_join(c)
        emit_unspill(c, stream, 1, scrB)
        c.wait(nc.sync, "sp")
        emit_join(c)
        return
    cv = Carver(stream.ap()[:, 0:NTH, :].rearrange("p a b -> p (a b)"))
    emit_l0_sparse_experts(c, W, B, cv)
    emit_join(c)
    emit_l0_sparse_gather(c, stream, B, 1, 0)
    emit_join(c)
    emit_spill(c, stream, 1, scrB)
    emit_unspill(c, stream, 0, scrA)
    emit_l0_sparse_gather(c, stream, B, 0, 1)
    emit_join(c)
    emit_unspill(c, stream, 1, scrB)
    c.wait(nc.sync, "sp")
    emit_join(c)


def prep_l0_ffn_weights(inp):
    up = np.asarray(inp["L0_ffn_up"])
    gate = np.asarray(inp["L0_ffn_gate"])
    down = np.asarray(inp["L0_ffn_down"])
    router = np.asarray(inp["L0_ffn_router"])
    rt = np.ascontiguousarray(
        router.T.reshape(NCC, 128, E0).transpose(1, 0, 2))
    return {
        "upT": np.ascontiguousarray(up.transpose(0, 2, 1)),
        "gateT": np.ascontiguousarray(gate.transpose(0, 2, 1)),
        "downT": np.ascontiguousarray(down.transpose(0, 2, 1)),
        "routerT_tiled": rt,
        "up_sT": np.ascontiguousarray(np.asarray(inp["L0_ffn_w_up_s"]).T),
        "gate_sT": np.ascontiguousarray(np.asarray(inp["L0_ffn_w_gate_s"]).T),
        "down_sT": np.ascontiguousarray(np.asarray(inp["L0_ffn_w_down_s"]).T),
    }


L0_FFN_SHAPES = {"upT": [E0, D, H0], "gateT": [E0, D, H0], "downT": [E0, H0, D],
                 "routerT_tiled": [128, NCC, E0], "up_sT": [D, H0],
                 "gate_sT": [D, H0], "down_sT": [H0, D]}


def build_launch(block_names):
    nc = bass.Bass("TRN2", target_bir_lowering=False)
    consts = build_consts()
    xin = nc.dram_tensor("xin", [T, D], F32, kind="ExternalInput").ap()
    xout = nc.dram_tensor("xout", [T, D], F32, kind="ExternalOutput").ap()
    cin = {k: nc.dram_tensor("c_" + k, list(v.shape), F32,
                             kind="ExternalInput").ap()
           for k, v in consts.items()}
    scrA = nc.dram_tensor("scrA", [128, NTH, D], F32, kind="Internal").ap()
    scrB = nc.dram_tensor("scrB", [128, NTH, D], F32, kind="Internal").ap()
    W = {}
    cc_in = nc.dram_tensor("cc_in", [1, 8], F32, kind="Internal").ap()
    cc_out = nc.dram_tensor("cc_out", [1, 8], F32, kind="Internal").ap()
    if "l0_attn" in block_names:
        W["l0_attn"] = {k: nc.dram_tensor("l0a_" + k, s, F32,
                                          kind="ExternalInput").ap()
                        for k, s in L0_ATTN_SHAPES.items()}
    if "l1_attn" in block_names:
        W["l1_attn"] = {k: nc.dram_tensor("l1a_" + k, s, F32,
                                          kind="ExternalInput").ap()
                        for k, s in L1_ATTN_SHAPES.items()}
    if "l0_ffn" in block_names:
        W["l0_ffn"] = {k: nc.dram_tensor("l0f_" + k, s, F32,
                                         kind="ExternalInput").ap()
                       for k, s in L0_FFN_SHAPES.items()}
        W["l0_ffn"]["dispL0"] = nc.dram_tensor(
            "dispL0", [E0 * CAP0 + 1, D], BF16, kind="Internal").ap()
        W["l0_ffn"]["eoutL0"] = nc.dram_tensor(
            "eoutL0", [E0 * CAP0 + 1, D], F32, kind="Internal").ap()
    if "l1_ffn" in block_names:
        W["l1_ffn"] = {k: nc.dram_tensor("l1f_" + k, s, F32,
                                         kind="ExternalInput").ap()
                       for k, s in L1_FFN_SHAPES.items()}
        W["l1_ffn"]["dispT"] = nc.dram_tensor(
            "dispT", [E1 * 2 * CAP + 1, D], BF16, kind="Internal").ap()
        W["l1_ffn"]["eoutD"] = nc.dram_tensor(
            "eoutD", [E1 * 2 * CAP + 1, D], F32, kind="Internal").ap()

    sem_names = ["ld", "gd", "sd", "pe", "dv", "ac", "st", "cc", "sp"]
    with contextlib.ExitStack() as stack:
        sems = {n: stack.enter_context(nc.semaphore(n)) for n in sem_names}
        c = Ctx(nc, sems)
        with (
            nc.sbuf_tensor("stream", [128, NT, D], F32) as stream,
            nc.sbuf_tensor("xfm", [128, NCC, 2048], BF16) as xfm,
            nc.sbuf_tensor("xcast0", [128, D], BF16) as xcast0,
            nc.sbuf_tensor("xcast1", [128, D], BF16) as xcast1,
            nc.sbuf_tensor("idbf", [128, 128], BF16) as idbf,
            nc.sbuf_tensor("idf32", [128, 128], F32) as idf32,
            nc.sbuf_tensor("silu32", [128, 1024], F32) as silu32,
            nc.sbuf_tensor("causal", [128, 128], F32) as causal,
            nc.sbuf_tensor("stats", [128, 8], F32) as stats,
            nc.Block() as block,
        ):
            @block.sync
            def _(sync):
                ins = nc.gpsimd.dma_start(out=idbf[:], in_=cin["idbf"])
                c.inc(ins, "gd", 16)
                ins = nc.sync.dma_start(out=idf32[:], in_=cin["idf32"])
                c.inc(ins, "ld", 16)
                ins = nc.sync.dma_start(out=causal[:], in_=cin["causal_add"])
                c.inc(ins, "ld", 16)
                emit_stream_load(c, xin, stream)
                for eng in (nc.vector, nc.tensor, nc.scalar, nc.gpsimd):
                    c.wait(eng, "ld")
                    c.wait(eng, "gd")

                B = {"xcast": [xcast0, xcast1], "idbf": idbf, "idf32": idf32,
                     "stats": stats, "silu32": silu32, "xfm": xfm,
                     "causal": causal}
                for bn in block_names:
                    if bn == "l0_ffn":
                        with contextlib.ExitStack() as st:
                            def sb(name, shape, dt, **kw):
                                return st.enter_context(
                                    nc.sbuf_tensor(name, shape, dt, **kw))
                            def ps(name, shape, dt):
                                return st.enter_context(
                                    nc.psum_tensor(name, shape, dt))
                            Bl = dict(
                                B,
                                wug=[sb("wug0", [128, NCC, 512], BF16),
                                     sb("wug1", [128, NCC, 512], BF16)],
                                wdn=sb("wdn", [128, 16, 512], BF16),
                                rt=sb("rt", [128, NCC, 8], BF16),
                                utb=sb("utb0", [128, 128], BF16),
                                iota8=sb("iota80", [128, 8], F32),
                                selbuf=sb("selbuf0", [128, NT, 1], F32),
                                slotbuf=sb("slotbuf0", [128, NT, 1], I32),
                                offs=sb("offs0", [1, 16], F32),
                                onesrow32=sb("onesrow320", [1, 128], F32),
                                onescol=sb("onescol0", [128, 1], BF16),
                                t8a=sb("t8a0", [128, 8], F32),
                                t8c=sb("t8c0", [128, 8], F32),
                                c8b=sb("c8b0", [128, 8], BF16),
                                m8=sb("m80", [128, 8], F32),
                                i8=sb("i80", [128, 8], U32),
                                g=silu32,
                                dispL0=W["l0_ffn"]["dispL0"],
                                eoutL0=W["l0_ffn"]["eoutL0"],
                                cin_ut=cin["ut_strict"],
                                cin_iota=cin["iota8"],
                                p_tr=[ps("p_tr0", [128, 512], BF16),
                                      ps("p_tr1", [128, 512], BF16)],
                                p_u=ps("p_u", [128, 512], F32),
                                p_g=ps("p_g", [128, 512], F32),
                                p_y=[ps("p_y0", [128, 512], F32),
                                     ps("p_y1", [128, 512], F32)],
                                p_s=ps("p_s", [128, 64], F32))
                            emit_l0_ffn_sparse(c, stream, W["l0_ffn"], Bl,
                                               scrA, scrB)
                    elif bn == "l0_attn":
                        with contextlib.ExitStack() as st:
                            def sb(name, shape, dt):
                                return st.enter_context(
                                    nc.sbuf_tensor(name, shape, dt))
                            def ps(name, shape, dt):
                                return st.enter_context(
                                    nc.psum_tensor(name, shape, dt))
                            Bl = dict(
                                B,
                                wvt=sb("wvt0", [128, NCC, 1024], BF16),
                                wqk=[sb("wqkA0", [128, 8, 128], BF16),
                                     sb("wqkA1", [128, 8, 128], BF16)],
                                paug=sb("paug", [128, 257], BF16),
                                paugA=sb("paugA", [128, 257], BF16),
                                paugB=sb("paugB", [128, 257], BF16),
                                pload=sb("pload", [128, 256], F32),
                                ctxbuf=sb("ctxbuf", [128, 2, 16, 2, 65], BF16),
                                vsumbuf=sb("vsumbuf", [128, 2112], BF16),
                                kmax=sb("kmax", [128, 4], F32),
                                kmbf=sb("kmbf", [128, 2], BF16),
                                acol=sb("acol", [128, 1], F32),
                                onesrow=sb("onesrow", [128, 128], BF16),
                                borow=sb("borow", [1, 1024], BF16),
                                p_a=ps("p0_a", [128, 512], F32),
                                p_b=ps("p0_b", [128, 512], F32),
                                p_dd=ps("p0_dd", [128, 512], F32),
                                p_dd2=ps("p0_dd2", [128, 512], F32),
                                p_c0=ps("p0_c0", [128, 512], F32),
                                p_c1=ps("p0_c1", [128, 512], F32),
                                p_tr=[ps("p0_tr0", [128, 512], BF16),
                                      ps("p0_tr1", [128, 512], BF16)])
                            emit_l0_attn(c, stream, W["l0_attn"], Bl,
                                         scrA, scrB, cc_in, cc_out)
                    elif bn == "l1_attn":
                        with (
                            nc.sbuf_tensor("wvt", [128, NCC, 1024], BF16) as wvt,
                            nc.sbuf_tensor("wqk0", [128, 16, 128], BF16) as wqk0,
                            nc.sbuf_tensor("wqk1", [128, 16, 128], BF16) as wqk1,
                            nc.sbuf_tensor("wotF", [128, 8, 512], BF16) as wotF,
                            nc.psum_tensor("pa_a", [128, 512], F32) as p_a,
                            nc.psum_tensor("pa_b", [128, 512], F32) as p_b,
                            nc.psum_tensor("pa_z0", [128, 512], F32) as p_z0,
                            nc.psum_tensor("pa_z1", [128, 512], F32) as p_z1,
                            nc.psum_tensor("pa_o", [128, 512], F32) as p_o,
                            nc.psum_tensor("pa_to", [128, 512], F32) as p_to,
                            nc.psum_tensor("pa_tr0", [128, 512], BF16) as pa_tr0,
                            nc.psum_tensor("pa_tr1", [128, 512], BF16) as pa_tr1,
                        ):
                            Bl = dict(B, wvt=wvt, wqk=[wqk0, wqk1],
                                      wot_full=wotF, p_a=p_a, p_b=p_b,
                                      p_z0=p_z0, p_z1=p_z1, p_o=p_o, p_to=p_to,
                                      p_tr=[pa_tr0, pa_tr1])
                            emit_block_halves(
                                c, stream, scrA, scrB,
                                lambda half, cv: emit_l1_attn_half(
                                    c, stream, W["l1_attn"], Bl, half, cv))
                    elif bn == "l1_ffn":
                        with contextlib.ExitStack() as st:
                            def sb(name, shape, dt, **kw):
                                return st.enter_context(
                                    nc.sbuf_tensor(name, shape, dt, **kw))
                            def ps(name, shape, dt):
                                return st.enter_context(
                                    nc.psum_tensor(name, shape, dt))
                            Wf = W["l1_ffn"]
                            Bl = dict(
                                B,
                                w1b=[sb("w1b0", [128, NCC, 128], BF16),
                                     sb("w1b1", [128, NCC, 128], BF16)],
                                w2b=[sb("w2b0", [128, 32, 128], BF16),
                                     sb("w2b1", [128, 32, 128], BF16)],
                                rw3=sb("rw3", [128, 32, 8], BF16),
                                rb1t=sb("rb1t", [128, 32], F32),
                                rb2t=sb("rb2t", [128, 32], F32),
                                rb3r=sb("rb3r", [1, 8], BF16),
                                utb=sb("utb", [128, 128], BF16),
                                iota8=sb("iota8", [128, 8], F32),
                                selbuf=sb("selbuf", [128, 2, NTH, 4], F32),
                                slotbuf=sb("slotbuf", [128, 2, NTH, 2], I32),
                                offs=sb("offs", [1, 16], F32),
                                onesrow=sb("onesrowF", [1, 128], BF16),
                                onescol=sb("onescolF", [128, 1], BF16),
                                onesrow32=sb("onesrow32", [1, 128], F32),
                                ebt1=sb("ebt1", [128, 32], F32),
                                ebt2=sb("ebt2", [128, 8], F32),
                                df=sb("df", [128, NCC, 2 * CAP], BF16,
                                      is_transpose_dest=True),
                                g=silu32,
                                h2sb=sb("h2sb", [128, 2, 1024], BF16),
                                t8a=sb("t8a", [128, 8], F32),
                                t8b=sb("t8b", [128, 8], F32),
                                t8c=sb("t8c", [128, 8], F32),
                                c8b=sb("c8b", [128, 8], BF16),
                                sc8=sb("sc8", [128, 8], F32),
                                m8=sb("m8", [128, 8], F32),
                                i8=sb("i8", [128, 8], U32),

                                dispT=Wf["dispT"], eoutD=Wf["eoutD"],
                                cin_ut=cin["ut_strict"],
                                cin_iota=cin["iota8"],
                                p_h1=ps("pf_h1", [128, 512], F32),
                                p_h2=ps("pf_h2", [128, 512], F32),
                                p_small=ps("pf_small", [128, 64], F32),
                                p_eo=ps("pf_eo", [128, 1024], F32),
                                p_tr=[ps("pf_tr0", [128, 512], BF16),
                                      ps("pf_tr1", [128, 512], BF16)])
                            Bl["xbf"] = Bl["h2sb"][:, 0, :]
                            Bl["ew1"] = Bl["w1b"]
                            Bl["ew2"] = Bl["w2b"]
                            emit_l1_ffn(c, stream, Wf, Bl, scrA, scrB)
                    elif bn == "ln_only":
                        emit_ln(c, stream, list(range(NT)), silu32, stats)
                        emit_join(c)
                    elif bn == "noop":
                        pass
                    else:
                        raise ValueError(bn)

                for s in ("dv", "ac", "pe", "sp"):
                    c.wait(nc.sync, s)
                emit_stream_store(c, xout, stream)

    return nc, consts


def _shard_x(x):
    x = np.asarray(x, dtype=np.float32)
    return [np.ascontiguousarray(x[2 * i:2 * i + 2].reshape(T, D))
            for i in range(8)]


def _gather_out(res, cores):
    outs = [res.results[i]["xout"].reshape(NBATCH, SEQ, D)
            for i in range(len(cores))]
    return np.concatenate(outs, axis=0)


def run_blocks(block_names, inputs, x_override=None, core_ids=None,
               shards=None):
    nc, consts = build_launch(block_names)
    if shards is None:
        x = x_override if x_override is not None else inputs["L0_attn_x"]
        shards = _shard_x(np.asarray(x))
    wmaps = {}
    if "l0_attn" in block_names:
        w = prep_l0_attn_weights(inputs)
        wmaps.update({"l0a_" + k: v for k, v in w.items()})
    if "l1_attn" in block_names:
        w = prep_l1_attn_weights(inputs)
        wmaps.update({"l1a_" + k: v for k, v in w.items()})
    if "l0_ffn" in block_names:
        w = prep_l0_ffn_weights(inputs)
        wmaps.update({"l0f_" + k: v for k, v in w.items()})
    if "l1_ffn" in block_names:
        w = prep_l1_ffn_weights(inputs)
        wmaps.update({"l1f_" + k: v for k, v in w.items()})
    cmaps = {"c_" + k: v for k, v in consts.items()}
    cores = core_ids if core_ids is not None else list(range(8))
    in_maps = [dict(xin=shards[i], **cmaps, **wmaps) for i in cores]
    res = run_bass_kernel_spmd(nc, in_maps, core_ids=cores)
    return res, cores


def kernel(**inputs):
    res, cores = run_blocks(["l0_attn", "l0_ffn", "l1_attn", "l1_ffn"], inputs)
    return _gather_out(res, cores).astype(np.float32)


# version 24
# speedup vs baseline: 1.0668x; 1.0436x over previous
import contextlib
import numpy as np
import concourse.bass as bass
import concourse.mybir as mybir
from concourse.bass import IndirectOffsetOnAxis
from concourse.bass_utils import run_bass_kernel_spmd

F32 = mybir.dt.float32
BF16 = mybir.dt.bfloat16
I32 = mybir.dt.int32
U32 = mybir.dt.uint32
AF = mybir.ActivationFunctionType
ALU = mybir.AluOpType
AX = mybir.AxisListType

D = 1024
SEQ = 2048
NBATCH = 2
T = NBATCH * SEQ
NT = T // 128
NTH = NT // 2
NCC = D // 128

HEADS = 16
DHEAD = 64
PM = 256
PEPS = 1e-4
PNORM = float(DHEAD) ** -0.25
PRATIO = float(PM) ** -0.5

E0 = 8
H0 = 2048
H1R = 4096
E1 = 8
H1E = 4096
CAP = 320
LN_EPS = 1e-5
MASKVAL = -1.25e9


def build_consts():
    c = {}
    c["idf32"] = np.eye(128, dtype=np.float32)
    c["idbf"] = np.eye(128, dtype=np.float32)
    c["ones_row"] = np.ones((1, 128), dtype=np.float32)
    c["iota8"] = np.broadcast_to(np.arange(8, dtype=np.float32), (128, 8)).copy()
    c["ut_strict"] = np.triu(np.ones((128, 128), dtype=np.float32), 1)
    c["causal_add"] = np.where(np.arange(128)[:, None] <= np.arange(128)[None, :],
                               0.0, MASKVAL).astype(np.float32)
    return c


class Ctx:
    def __init__(self, nc, sems):
        self.nc = nc
        self.s = dict(sems)
        self.n = {k: 0 for k in self.s}
        self.bufpe = {}

    def guard_buf(self, eng, name):
        v = self.bufpe.get(name, 0)
        if v > 0:
            eng.wait_ge(self.s["pe"], v)

    def consumed(self, name):
        self.bufpe[name] = self.n["pe"]

    def inc(self, instr, sem, k=1):
        instr.then_inc(self.s[sem], k)
        self.n[sem] += k
        return self.n[sem]

    def wait(self, eng, sem, val=None):
        v = self.n[sem] if val is None else val
        if v > 0:
            eng.wait_ge(self.s[sem], v)


def vw(c, ins):
    c.inc(ins, "dv")
    c.wait(c.nc.vector, "dv")
    return ins


def heat(c, n):
    hb = c.heatbuf
    for _ in range(n):
        c.nc.tensor.ldweights(hb)


def emit_join(c):
    nc = c.nc
    for eng in (nc.sync, nc.scalar, nc.vector, nc.tensor, nc.gpsimd):
        for s in ("pe", "dv", "ac", "ld", "gd", "sd", "sp"):
            c.wait(eng, s)


class Carver:
    def __init__(self, region):
        self.r = region
        self.off = 0

    def f32(self, shape):
        n = int(np.prod(shape[1:]))
        v = self.r[:, self.off:self.off + n]
        self.off += n
        if len(shape) == 3:
            v = v.rearrange("p (a b) -> p a b", a=shape[1])
        return v

    def u32(self, shape):
        n = int(np.prod(shape[1:]))
        v = self.r[:, self.off:self.off + n].bitcast(U32)
        self.off += n
        if len(shape) == 3:
            v = v.rearrange("p (a b) -> p a b", a=shape[1])
        return v

    def bf16(self, shape):
        n = int(np.prod(shape[1:]))
        assert n % 2 == 0
        v = self.r[:, self.off:self.off + n // 2].bitcast(BF16)
        self.off += n // 2
        if len(shape) == 3:
            v = v.rearrange("p (a b) -> p a b", a=shape[1])
        return v


def emit_stream_load(c, xin, stream):
    nc = c.nc
    src = xin.rearrange("(t p) f -> p t f", p=128)
    for i in range(4):
        ins = nc.sync.dma_start(out=stream[:, 8 * i:8 * i + 8, :],
                                in_=src[:, 8 * i:8 * i + 8, :])
        c.inc(ins, "ld", 16)


def emit_stream_store(c, xout, stream):
    nc = c.nc
    dst = xout.rearrange("(t p) f -> p t f", p=128)
    for i in range(4):
        ins = nc.sync.dma_start(out=dst[:, 8 * i:8 * i + 8, :],
                                in_=stream[:, 8 * i:8 * i + 8, :])
        c.inc(ins, "st", 16)
    c.wait(nc.sync, "st")


def emit_spill(c, stream, half, scratch):
    nc = c.nc
    sl = slice(half * NTH, (half + 1) * NTH)
    ins = nc.sync.dma_start(out=scratch[:, 0:8, :], in_=stream[:, sl, :][:, 0:8, :])
    c.inc(ins, "sp", 16)
    ins = nc.sync.dma_start(out=scratch[:, 8:16, :], in_=stream[:, sl, :][:, 8:16, :])
    c.inc(ins, "sp", 16)


def emit_unspill(c, stream, half, scratch):
    nc = c.nc
    sl = slice(half * NTH, (half + 1) * NTH)
    ins = nc.sync.dma_start(out=stream[:, sl, :][:, 0:8, :], in_=scratch[:, 0:8, :])
    c.inc(ins, "sp", 16)
    ins = nc.sync.dma_start(out=stream[:, sl, :][:, 8:16, :], in_=scratch[:, 8:16, :])
    c.inc(ins, "sp", 16)


def emit_tm_to_fm(c, stream, tiles, xfm, B):
    nc = c.nc
    xcast, idbf, ptr = B["xcast"], B["idbf"], B["p_tr"]
    pe0, dv0, ac0 = c.n["pe"], c.n["dv"], c.n["ac"]
    for i, t in enumerate(tiles):
        buf = xcast[i % 2]
        if i >= 2:
            c.wait(nc.vector, "pe", pe0 + 2 * (i - 1))
        ins = nc.vector.tensor_copy(buf[:], stream[:, t, :])
        c.inc(ins, "dv")
        for h in range(2):
            pb = ptr[h]
            nc.tensor.wait_ge(c.s["dv"], dv0 + i + 1)
            if i >= 1:
                nc.tensor.wait_ge(c.s["ac"], ac0 + 2 * (i - 1) + h + 1)
            for j in range(4):
                cc = h * 4 + j
                last = nc.tensor.transpose(pb[:, j * 128:(j + 1) * 128],
                                           buf[:, cc * 128:(cc + 1) * 128],
                                           idbf[:])
            c.inc(last, "pe")
            c.wait(nc.scalar, "pe")
            for j in range(4):
                cc = h * 4 + j
                ins = nc.scalar.copy(xfm[:, cc, i * 128:(i + 1) * 128],
                                     pb[:, j * 128:(j + 1) * 128])
            c.inc(ins, "ac")


def emit_ln(c, stream, tiles, sq32, stats, spill_to=None):
    nc = c.nc
    for li, t in enumerate(tiles):
        ssum, ssq = stats[:, 0:1], stats[:, 1:2]
        mean, var = stats[:, 2:3], stats[:, 3:4]
        rstd, tmp = stats[:, 4:5], stats[:, 5:6]
        ins = nc.scalar.activation(out=sq32[:, 0:D], in_=stream[:, t, :],
                                   func=AF.Square, accum_out=ssq)
        c.inc(ins, "ac")
        c.wait(nc.vector, "ac")
        ins = nc.vector.reduce_sum(ssum, stream[:, t, :], axis=AX.X)
        c.inc(ins, "dv")
        c.wait(nc.vector, "dv")
        nc.vector.tensor_scalar_mul(mean, ssum, 1.0 / D)
        nc.vector.tensor_scalar_mul(var, ssq, 1.0 / D)
        nc.vector.tensor_mul(tmp, mean, mean)
        nc.vector.tensor_sub(var, var, tmp)
        ins = nc.vector.tensor_scalar_add(var, var, LN_EPS)
        c.inc(ins, "dv")
        c.wait(nc.scalar, "dv")
        nc.scalar.activation(out=tmp, in_=var, func=AF.Ln)
        ins = nc.scalar.activation(out=rstd, in_=tmp, func=AF.Exp, scale=-0.5)
        c.inc(ins, "ac")
        c.wait(nc.vector, "ac")
        ins = nc.vector.tensor_scalar(out=stream[:, t, :], in0=stream[:, t, :],
                                      scalar1=mean, scalar2=rstd,
                                      op0=ALU.subtract, op1=ALU.mult)
        lnm = c.inc(ins, "dv")
        if spill_to is not None:
            nc.sync.wait_ge(c.s["dv"], lnm)
            ins = nc.sync.dma_start(out=spill_to[:, li, :], in_=stream[:, t, :])
            c.inc(ins, "sp", 16)


def emit_l0_ffn_half(c, stream, W, B, half, cv):
    nc = c.nc
    xfm = B["xfm"]
    stats, silu32 = B["stats"], B["silu32"]
    p_r = B["p_u"][:, 0:8]
    p_u, p_g, p_y = B["p_u"], B["p_g"], B["p_y"]
    rt = B["rt"]

    hstore = cv.bf16([128, 16, 2048])
    mask9 = B["mask9"]
    tiles = list(range(half * NTH, (half + 1) * NTH))

    c.wait(nc.vector, "sp")
    emit_tm_to_fm(c, stream, tiles, xfm, B)
    c.wait(nc.tensor, "gd")
    c.wait(nc.tensor, "ac")
    c.wait(nc.tensor, "sp")
    c.wait(nc.vector, "sp")
    dv0 = c.n["dv"]
    for i in range(NTH):
        if i >= 1:
            nc.tensor.wait_ge(c.s["dv"], dv0 + 2 * i)
        for cc in range(NCC):
            last = nc.tensor.matmul(p_r, xfm[:, cc, i * 128:(i + 1) * 128],
                                    rt[:, cc, :], start=(cc == 0),
                                    stop=(cc == NCC - 1))
        c.inc(last, "pe")
        c.wait(nc.vector, "pe")
        ins = nc.vector.reduce_max(stats[:, 0:1], p_r, axis=AX.X)
        c.inc(ins, "dv")
        c.wait(nc.vector, "dv")
        ins = nc.vector.tensor_scalar(out=mask9[:, i, :], in0=p_r,
                                      scalar1=stats[:, 0:1], scalar2=None,
                                      op0=ALU.is_ge)
        c.inc(ins, "dv")
    wug, wdn = B["wug"], B["wdn"]
    for e in range(9):
        shared = (e == 8)
        if shared:
            upT, gateT, downT = W["up_sT"], W["gate_sT"], W["down_sT"]
        else:
            upT, gateT, downT = W["upT"][e], W["gateT"][e], W["downT"][e]
        pe_b, dv_b, ac_b = c.n["pe"], c.n["dv"], c.n["ac"]
        k = 0
        for hp in range(8):
            wb = wug[hp % 2]
            c.guard_buf(nc.gpsimd, "l0f_wug%d" % (hp % 2))
            ins = nc.gpsimd.dma_start(
                out=wb[:, :, 0:256],
                in_=upT[:, hp * 256:(hp + 1) * 256].rearrange(
                    "(cc p) hh -> p cc hh", p=128))
            c.inc(ins, "gd", 16)
            ins = nc.gpsimd.dma_start(
                out=wb[:, :, 256:512],
                in_=gateT[:, hp * 256:(hp + 1) * 256].rearrange(
                    "(cc p) hh -> p cc hh", p=128))
            c.inc(ins, "gd", 16)
            gd_mark = c.n["gd"]
            for hh in range(2):
                hc = hp * 2 + hh
                for hw in range(4):
                    tw = slice(hw * 512, (hw + 1) * 512)
                    nc.tensor.wait_ge(c.s["gd"], gd_mark)
                    if k >= 1:
                        nc.tensor.wait_ge(c.s["ac"], ac_b + k)
                        nc.tensor.wait_ge(c.s["dv"], dv_b + k)
                    else:
                        c.wait(nc.tensor, "ac", ac_b)
                        c.wait(nc.tensor, "dv", dv_b)
                    for cc in range(NCC):
                        nc.tensor.matmul(
                            p_u[:], wb[:, cc, hh * 128:(hh + 1) * 128],
                            xfm[:, cc, tw], start=(cc == 0),
                            stop=(cc == NCC - 1))
                    for cc in range(NCC):
                        lg = nc.tensor.matmul(
                            p_g[:], wb[:, cc, 256 + hh * 128:256 + (hh + 1) * 128],
                            xfm[:, cc, tw], start=(cc == 0),
                            stop=(cc == NCC - 1))
                    c.inc(lg, "pe")
                    c.wait(nc.scalar, "pe")
                    if k >= 1:
                        c.wait(nc.scalar, "dv", dv_b + k)
                    ins = nc.scalar.activation(out=silu32[:, 0:512],
                                               in_=p_g[:], func=AF.Silu)
                    c.inc(ins, "ac")
                    c.wait(nc.vector, "ac")
                    ins = nc.vector.tensor_mul(hstore[:, hc, tw], p_u[:],
                                               silu32[:, 0:512])
                    c.inc(ins, "dv")
                    k += 1
                    c.consumed("l0f_wug%d" % (hp % 2))
        for ch in range(2):
            cw = slice(ch * 512, (ch + 1) * 512)
            c.guard_buf(nc.gpsimd, "l0f_wdn")
            ins = nc.gpsimd.dma_start(
                out=wdn[:, 0:8, :],
                in_=downT[0:1024, ch * 512:(ch + 1) * 512].rearrange(
                    "(hc p) cf -> p hc cf", p=128))
            c.inc(ins, "gd", 16)
            ins = nc.gpsimd.dma_start(
                out=wdn[:, 8:16, :],
                in_=downT[1024:2048, ch * 512:(ch + 1) * 512].rearrange(
                    "(hc p) cf -> p hc cf", p=128))
            c.inc(ins, "gd", 16)
            gd_dn = c.n["gd"]
            dv_c = c.n["dv"]
            for ts in range(NTH):
                pb = p_y[ts % 2]
                tile = tiles[ts]
                nc.tensor.wait_ge(c.s["gd"], gd_dn)
                c.wait(nc.tensor, "dv", dv_b + 64)
                if ts >= 2:
                    c.wait(nc.tensor, "dv", dv_c + ts - 1)
                else:
                    c.wait(nc.tensor, "dv", dv_c)
                for hc in range(16):
                    last = nc.tensor.matmul(
                        pb[:], hstore[:, hc, ts * 128:(ts + 1) * 128],
                        wdn[:, hc, :], start=(hc == 0), stop=(hc == 15))
                c.inc(last, "pe")
                c.consumed("l0f_wdn")
                c.wait(nc.vector, "pe")
                if shared:
                    ins = nc.vector.tensor_add(stream[:, tile, cw], pb[:],
                                               stream[:, tile, cw])
                else:
                    ins = nc.vector.scalar_tensor_tensor(
                        out=stream[:, tile, cw], in0=pb[:],
                        scalar=mask9[:, ts, e:e + 1],
                        in1=stream[:, tile, cw],
                        op0=ALU.mult, op1=ALU.add)
                c.inc(ins, "dv")
    c.wait(nc.scalar, "dv")
    emit_ln(c, stream, tiles, silu32, stats)


def emit_block_halves(c, stream, scratchA, scratchB, emit_half):
    nc = c.nc
    cv = Carver(stream.ap()[:, NTH:NT, :].rearrange("p a b -> p (a b)"))
    emit_half(0, cv, scratchA)
    emit_join(c)
    emit_unspill(c, stream, 1, scratchB)
    cv = Carver(stream.ap()[:, 0:NTH, :].rearrange("p a b -> p (a b)"))
    emit_half(1, cv, scratchB)
    emit_join(c)
    emit_unspill(c, stream, 0, scratchA)
    c.wait(nc.sync, "sp")
    emit_join(c)


def emit_l0_ffn(c, stream, W, B, scratchA, scratchB):
    nc = c.nc
    ins = nc.gpsimd.dma_start(out=B["rt"][:], in_=W["routerT_tiled"])
    c.inc(ins, "gd", 16)
    emit_block_halves(c, stream, scratchA, scratchB,
                      lambda half, cv: emit_l0_ffn_half(c, stream, W, B, half, cv))




def emit_l1_attn_half(c, stream, W, B, half, cv, spill_to=None):
    nc = c.nc
    xfm, stats, silu32 = B["xfm"], B["stats"], B["silu32"]
    causal = B["causal"]
    wvt, wqk, wot_full = B["wvt"], B["wqk"], B["wot_full"]
    p_a, p_b, p_o, p_to = B["p_a"], B["p_b"], B["p_o"], B["p_to"]
    pzb = [B["p_z0"], B["p_z1"]]
    tiles = list(range(half * NTH, (half + 1) * NTH))

    zall = cv.bf16([128, 8, 2048])
    v2 = cv.bf16([128, 16, 130])
    q2 = cv.bf16([128, 2048])
    k2 = cv.bf16([128, 2048])
    pbuf = [cv.bf16([128, 512]) for _ in range(4)]
    rowbuf = [cv.bf16([1, 512]) for _ in range(2)]
    recSB = [cv.f32([128, 512]) for _ in range(2)]
    onesb = cv.bf16([128, 130])

    c.wait(nc.vector, "sp")
    emit_tm_to_fm(c, stream, tiles, xfm, B)
    ins = nc.vector.memset(onesb[:], 1.0)
    c.inc(ins, "dv")
    ins = nc.vector.memset(v2[:, :, 64:65], 1.0)
    c.inc(ins, "dv")
    ins = nc.vector.memset(v2[:, :, 129:130], 1.0)
    c.inc(ins, "dv")
    ins = nc.gpsimd.dma_start(out=wvt[:], in_=W["WVT"].rearrange(
        "(cc p) f -> p cc f", p=128))
    c.inc(ins, "gd", 16)
    wvt_gd = c.n["gd"]
    ins = nc.gpsimd.dma_start(
        out=wot_full[:], in_=W["WOT"][:, 0:512].rearrange(
            "(ah p) m -> p ah m", p=128))
    c.inc(ins, "gd", 16)
    wot_gd = c.n["gd"]
    c.wait(nc.tensor, "ac")
    nc.tensor.wait_ge(c.s["gd"], wvt_gd)
    c.wait(nc.tensor, "dv")

    sbanks = [p_a[:], p_b[:], p_o[:], p_to[:]]
    st = {"gk": 0, "gp": 0, "pending": None}
    smark, amark, dmark, zmark = {}, {}, {}, {}
    normmark = [0, 0]
    bcmark = [0, 0]

    def flush_pending():
        if st["pending"] is None:
            return
        zb, zbi, rmark, hb, api_, cwsl_ = st["pending"]
        st["pending"] = None
        nc.tensor.wait_ge(c.s["dv"], rmark)
        ins = nc.tensor.matmul(zb[64:128, :], onesb[0:1, 0:64],
                               rowbuf[zbi][0:1, 0:512], start=True, stop=True)
        bcmark[zbi] = c.inc(ins, "pe")
        nc.scalar.wait_ge(c.s["pe"], bcmark[zbi])
        nc.scalar.wait_ge(c.s["dv"], normmark[zbi])
        ins = nc.scalar.copy(recSB[zbi][0:64, :], zb[64:128, :])
        cmark = c.inc(ins, "ac")
        nc.vector.wait_ge(c.s["ac"], cmark)
        ins = nc.vector.tensor_mul(zall[hb:hb + 64, api_, cwsl_],
                                   zb[0:64, :], recSB[zbi][0:64, :])
        normmark[zbi] = c.inc(ins, "dv")

    for ap_i in range(8):
        wq_t = wqk[ap_i % 2]
        c.guard_buf(nc.gpsimd, "l1a_wqk%d" % (ap_i % 2))
        ins = nc.gpsimd.dma_start(
            out=wq_t[:, 0:8, :],
            in_=W["WQT"][:, ap_i * 128:(ap_i + 1) * 128].rearrange(
                "(cc p) m -> p cc m", p=128))
        c.inc(ins, "gd", 16)
        ins = nc.gpsimd.dma_start(
            out=wq_t[:, 8:16, :],
            in_=W["WKT"][:, ap_i * 128:(ap_i + 1) * 128].rearrange(
                "(cc p) m -> p cc m", p=128))
        c.inc(ins, "gd", 16)
        gd_qk = c.n["gd"]
        ac_qk = c.n["ac"]
        for w in range(4):
            tw = slice(w * 512, (w + 1) * 512)
            nc.tensor.wait_ge(c.s["gd"], gd_qk)
            if w >= 1:
                nc.tensor.wait_ge(c.s["ac"], ac_qk + 2 * w)
            else:
                if st["gk"] >= 1:
                    la = max(amark.get(st["gk"] - 1 - k, 0) for k in range(4))
                    nc.tensor.wait_ge(c.s["ac"], la)
            for cc in range(NCC):
                lq = nc.tensor.matmul(p_a[:], wq_t[:, cc, :], xfm[:, cc, tw],
                                      start=(cc == 0), stop=(cc == NCC - 1))
            for cc in range(NCC):
                lk = nc.tensor.matmul(p_b[:], wq_t[:, 8 + cc, :], xfm[:, cc, tw],
                                      start=(cc == 0), stop=(cc == NCC - 1))
            c.inc(lk, "pe")
            c.wait(nc.scalar, "pe")
            nc.scalar.mul(q2[:, tw], p_a[:], 0.125)
            ins = nc.scalar.copy(k2[:, tw], p_b[:])
            c.inc(ins, "ac", 2)
        c.consumed("l1a_wqk%d" % (ap_i % 2))
        dv_v = c.n["dv"]
        for ts in range(NTH):
            vb = p_a if ts % 2 == 0 else p_b
            nc.tensor.wait_ge(c.s["ac"], ac_qk + 8)
            if ts >= 2:
                nc.tensor.wait_ge(c.s["dv"], dv_v + 2 * (ts - 1))
            for cc in range(NCC):
                lv = nc.tensor.matmul(
                    vb[:, 0:128], xfm[:, cc, ts * 128:(ts + 1) * 128],
                    wvt[:, cc, ap_i * 128:(ap_i + 1) * 128],
                    start=(cc == 0), stop=(cc == NCC - 1))
            c.inc(lv, "pe")
            c.wait(nc.vector, "pe")
            nc.vector.tensor_copy(v2[:, ts, 0:64], vb[:, 0:64])
            ins = nc.vector.tensor_copy(v2[:, ts, 65:129], vb[:, 64:128])
            c.inc(ins, "dv", 2)
        v2_dv = c.n["dv"]
        ins = nc.scalar.memzero(silu32[:, 0:1])
        c.inc(ins, "ac")
        mb = silu32[:, 0:1]
        qk_ac = c.n["ac"]
        nc.tensor.wait_ge(c.s["dv"], v2_dv)

        for hh in range(2):
            hb = 64 * hh
            vsl = slice(65 * hh, 65 * hh + 65)
            for cw in range(4):
                cwsl = slice(cw * 512, (cw + 1) * 512)
                nch = 4 * cw + 4
                zbi = st["gp"] % 2
                zb = pzb[zbi]
                kids = []

                def issue_S(Cc):
                    kid = st["gk"]
                    st["gk"] += 1
                    kids.append(kid)
                    bk = kid % 4
                    sbank = sbanks[bk]
                    pb = pbuf[bk]
                    if kid >= 4:
                        nc.tensor.wait_ge(c.s["ac"], amark[kid - 4])
                    ins = nc.tensor.matmul(
                        sbank, q2[hb:hb + 64, Cc * 128:(Cc + 1) * 128],
                        k2[hb:hb + 64, cwsl], start=True, stop=True)
                    smark[kid] = c.inc(ins, "pe")
                    j = Cc - 4 * cw
                    zg = zmark.get(kid - 4, 0)
                    if j >= 0:
                        nc.vector.wait_ge(c.s["pe"], max(smark[kid], zg))
                        ins = nc.vector.tensor_add(
                            sbank[:, j * 128:(j + 1) * 128],
                            sbank[:, j * 128:(j + 1) * 128], causal[:])
                        c.inc(ins, "dv")
                        if j > 0:
                            ins = nc.vector.memset(pb[:, 0:j * 128], 0.0)
                            c.inc(ins, "dv")
                        dmark[kid] = c.n["dv"]
                        nc.scalar.wait_ge(c.s["dv"], dmark[kid])
                        nc.scalar.wait_ge(c.s["pe"], zg)
                        ins = nc.scalar.activation(
                            out=pb[:, j * 128:512], in_=sbank[:, j * 128:512],
                            func=AF.Exp, bias=mb)
                        amark[kid] = c.inc(ins, "ac")
                    else:
                        nc.scalar.wait_ge(c.s["pe"], max(smark[kid], zg))
                        ins = nc.scalar.activation(out=pb[:], in_=sbank[:, :],
                                                   func=AF.Exp, bias=mb)
                        amark[kid] = c.inc(ins, "ac")

                def issue_z(i, kid):
                    nc.tensor.wait_ge(c.s["ac"], amark[kid])
                    if kid in dmark and dmark[kid] > 0:
                        nc.tensor.wait_ge(c.s["dv"], dmark[kid])
                    if i == 0:
                        nc.tensor.wait_ge(c.s["dv"], normmark[zbi])
                    ins = nc.tensor.matmul(
                        zb[0:65, :], v2[:, i, vsl], pbuf[kid % 4][:],
                        start=(i == 0), stop=(i == nch - 1),
                        skip_group_check=True)
                    zmark[kid] = c.inc(ins, "pe")

                issue_S(0)
                if nch > 1:
                    issue_S(1)
                flush_pending()
                for i in range(nch):
                    issue_z(i, kids[i])
                    if i + 2 < nch:
                        issue_S(i + 2)
                nc.vector.wait_ge(c.s["pe"], max(zmark[kids[-1]], bcmark[zbi]))
                with nc.allow_low_precision(
                        reason="recip rounded to bf16 for the bcast matmul "
                               "operand; same rounding as f32-recip+cast"):
                    ins = nc.vector.reciprocal(rowbuf[zbi][0:1, 0:512],
                                               zb[64:65, :])
                rmark = c.inc(ins, "dv")
                st["pending"] = (zb, zbi, rmark, hb, ap_i, cwsl)
                st["gp"] += 1
    flush_pending()
    c.wait(nc.tensor, "dv")
    obanks = [pzb[0], pzb[1], p_o[:], p_to[:]]
    evm = [0, 0, 0, 0]
    for fh in range(2):
        if fh == 1:
            c.wait(nc.gpsimd, "pe")
            ins = nc.gpsimd.dma_start(
                out=wot_full[:], in_=W["WOT"][:, 512:1024].rearrange(
                    "(ah p) m -> p ah m", p=128))
            c.inc(ins, "gd", 16)
            wot_gd = c.n["gd"]
        nc.tensor.wait_ge(c.s["gd"], wot_gd)
        for ti in range(NTH):
            tile = tiles[ti]
            ob = obanks[ti % 4]
            nc.tensor.wait_ge(c.s["dv"], evm[ti % 4])
            for api in range(8):
                lo = nc.tensor.matmul(ob[0:128, 0:512],
                                      zall[:, api, ti * 128:(ti + 1) * 128],
                                      wot_full[:, api, :],
                                      start=(api == 0), stop=(api == 7),
                                      skip_group_check=True)
            pem = c.inc(lo, "pe")
            nc.vector.wait_ge(c.s["pe"], pem)
            ins = nc.vector.tensor_add(
                stream[:, tile, fh * 512:(fh + 1) * 512], ob[0:128, 0:512],
                stream[:, tile, fh * 512:(fh + 1) * 512])
            evm[ti % 4] = c.inc(ins, "dv")
    c.wait(nc.scalar, "dv")
    emit_ln(c, stream, tiles, silu32, stats, spill_to=spill_to)


def prep_l1_attn_weights(inp):
    WQ = np.asarray(inp["L1_attn_W_Q"])
    WK = np.asarray(inp["L1_attn_W_K"])
    WV = np.asarray(inp["L1_attn_W_V"])
    WO = np.asarray(inp["L1_attn_W_O"])
    return {
        "WQT": np.ascontiguousarray(WQ.transpose(2, 0, 1).reshape(D, D)),
        "WKT": np.ascontiguousarray(WK.transpose(2, 0, 1).reshape(D, D)),
        "WVT": np.ascontiguousarray(WV.transpose(2, 0, 1).reshape(D, D)),
        "WOT": np.ascontiguousarray(WO.transpose(0, 2, 1).reshape(D, D)),
    }


L1_ATTN_SHAPES = {"WQT": [D, D], "WKT": [D, D], "WVT": [D, D], "WOT": [D, D]}




def emit_l0_attn_pass1(c, stream, W, B, half, cv, rebuild_xfm=True):
    nc = c.nc
    xfm, stats = B["xfm"], B["stats"]
    paug, wvt, wqk = B["paug"], B["wvt"], B["wqk"]
    ctxbuf, vsumbuf, kmax = B["ctxbuf"], B["vsumbuf"], B["kmax"]
    p_a, p_b, p_dd, p_dd2, p_c0, p_c1 = (B["p_a"], B["p_b"], B["p_dd"],
                                         B["p_dd2"], B["p_c0"], B["p_c1"])
    tiles = list(range(half * NTH, (half + 1) * NTH))

    v_tm = cv.bf16([128, 16, 1024])
    k2 = cv.bf16([128, 2048])
    k2sq = cv.bf16([128, 2048])
    kp = cv.bf16([128, 16, 256])
    onesb = cv.bf16([128, 130])
    kmt = cv.f32([128, NTH])
    bb2 = cv.f32([128, 4])

    c.wait(nc.vector, "sp")
    if rebuild_xfm:
        emit_tm_to_fm(c, stream, tiles, xfm, B)
    ins = nc.vector.memset(onesb[:], 1.0)
    c.inc(ins, "dv")
    c.wait(nc.tensor, "ac")
    c.wait(nc.tensor, "gd")
    c.wait(nc.tensor, "dv")
    dv_v = c.n["dv"]
    for ts in range(NTH):
        if ts >= 1:
            nc.tensor.wait_ge(c.s["dv"], dv_v + ts)
        for half_f in range(2):
            vb = p_a if half_f == 0 else p_b
            for cc in range(NCC):
                lv = nc.tensor.matmul(
                    vb[:], xfm[:, cc, ts * 128:(ts + 1) * 128],
                    B["wvt"][:, cc, half_f * 512:(half_f + 1) * 512],
                    start=(cc == 0), stop=(cc == NCC - 1))
            c.inc(lv, "pe")
            c.wait(nc.vector, "pe")
            ins = nc.vector.tensor_copy(
                v_tm[:, ts, half_f * 512:(half_f + 1) * 512], vb[:])
        c.inc(ins, "dv")
    c.wait(nc.tensor, "ac")
    for ch in range(2):
        pc = p_c0 if ch == 0 else p_c1
        for ts in range(NTH):
            if ch == 0 and ts == 0:
                nc.tensor.wait_ge(c.s["dv"], dv_v + NTH)
            lm = nc.tensor.matmul(pc[0:1, 0:512], onesb[:, 0:1],
                                  v_tm[:, ts, ch * 512:(ch + 1) * 512],
                                  start=(ts == 0), stop=(ts == NTH - 1),
                                  skip_group_check=True)
        c.inc(lm, "pe")
    c.wait(nc.vector, "pe")
    for h in range(16):
        seg = (half * 16 + h) * 66
        ch, off = divmod(h * 64, 512)
        pc = p_c0 if ch == 0 else p_c1
        nc.vector.tensor_scalar_mul(vsumbuf[0:1, seg:seg + 64],
                                    pc[0:1, off:off + 64], PRATIO * PEPS)
        ins = nc.vector.memset(vsumbuf[0:1, seg + 64:seg + 65],
                               PRATIO * PEPS * 2048.0)
        c.inc(ins, "dv", 2)
    vs_dv = c.n["dv"]

    ddbanks = [p_dd, p_dd2, p_c0, p_c1]
    amark, rmark = {}, {}
    bank_dv = [0, 0, vs_dv, vs_dv]
    gk = 0
    for ap_i in range(8):
        wk_t = wqk[ap_i % 2]
        c.guard_buf(nc.gpsimd, "l0a_wqk%d" % (ap_i % 2))
        ins = nc.gpsimd.dma_start(
            out=wk_t[:, 0:8, :],
            in_=W["wk"][:, ap_i * 128:(ap_i + 1) * 128].rearrange(
                "(cc p) m -> p cc m", p=128))
        c.inc(ins, "gd", 16)
        gd_k = c.n["gd"]
        ac_k = c.n["ac"]
        for w in range(4):
            tw = slice(w * 512, (w + 1) * 512)
            kb = p_a if w % 2 == 0 else p_b
            nc.tensor.wait_ge(c.s["gd"], gd_k)
            if w < 2:
                if ap_i == 0:
                    nc.tensor.wait_ge(c.s["dv"], dv_v + NTH)
                else:
                    nc.tensor.wait_ge(c.s["ac"], ac_k)
            else:
                nc.tensor.wait_ge(c.s["ac"], ac_k + w - 1)
            for cc in range(NCC):
                lk = nc.tensor.matmul(kb[:], wk_t[:, cc, :], xfm[:, cc, tw],
                                      start=(cc == 0), stop=(cc == NCC - 1))
            c.inc(lk, "pe")
            c.consumed("l0a_wqk%d" % (ap_i % 2))
            c.wait(nc.scalar, "pe")
            ins = nc.scalar.copy(k2[:, tw], kb[:])
            c.inc(ins, "ac")
        c.wait(nc.scalar, "pe")
        ins = nc.scalar.activation(out=k2sq[:, :], in_=k2[:, :], func=AF.Square)
        sqm = c.inc(ins, "ac")
        for hh in range(2):
            h = ap_i * 2 + hh
            hb = 64 * hh
            nc.tensor.wait_ge(c.s["ac"], sqm)
            for ts in range(NTH):
                bk = gk % 4
                db = ddbanks[bk]
                if gk >= 4:
                    nc.tensor.wait_ge(c.s["ac"], amark[gk - 4])
                    nc.tensor.wait_ge(c.s["dv"], rmark[gk - 4])
                if bank_dv[bk]:
                    nc.tensor.wait_ge(c.s["dv"], bank_dv[bk])
                    bank_dv[bk] = 0
                tsl = slice(ts * 128, (ts + 1) * 128)
                nc.tensor.matmul(db[:, 0:257], k2[hb:hb + 64, tsl],
                                 B["paugA"][hb:hb + 64, :],
                                 start=True, stop=False)
                ins = nc.tensor.matmul(db[:, 0:257], k2sq[hb:hb + 64, tsl],
                                       B["paugB"][hb:hb + 64, :],
                                       start=False, stop=True)
                mm = c.inc(ins, "pe")
                nc.vector.wait_ge(c.s["pe"], mm)
                ins = nc.vector.reduce_max(kmt[:, ts:ts + 1], db[:, 0:256],
                                           axis=AX.X)
                c.inc(ins, "dv")
                if gk >= 4:
                    nc.vector.wait_ge(c.s["ac"], amark[gk - 4])
                ins = nc.vector.tensor_copy(bb2[:, gk % 4:gk % 4 + 1],
                                            db[:, 256:257])
                rmark[gk] = c.inc(ins, "dv")
                nc.scalar.wait_ge(c.s["dv"], rmark[gk])
                ins = nc.scalar.activation(out=kp[:, ts, :], in_=db[:, 0:256],
                                           func=AF.Exp,
                                           bias=bb2[:, gk % 4:gk % 4 + 1])
                amark[gk] = c.inc(ins, "ac")
                gk += 1
            c.wait(nc.vector, "dv")
            vw(c, nc.vector.reduce_max(stats[:, 0:1], kmt[:], axis=AX.X))
            ins = nc.vector.tensor_max(kmax[:, 0:1], kmax[:, 0:1], stats[:, 0:1])
            c.inc(ins, "dv")
            c.wait(nc.vector, "dv")
            nc.tensor.wait_ge(c.s["ac"], amark[gk - 1])
            nc.tensor.wait_ge(c.s["dv"], rmark[gk - 1])
            nc.tensor.wait_ge(c.s["ac"], amark[gk - 2])
            nc.tensor.wait_ge(c.s["dv"], rmark[gk - 2])
            for mc in range(2):
                pc = p_c0 if mc == 0 else p_c1
                for ts in range(NTH):
                    nc.tensor.matmul(pc[:, 0:64],
                                     kp[:, ts, mc * 128:(mc + 1) * 128],
                                     v_tm[:, ts, h * 64:(h + 1) * 64],
                                     start=(ts == 0), stop=False,
                                     skip_group_check=True)
                    lm = nc.tensor.matmul(pc[:, 64:65],
                                          kp[:, ts, mc * 128:(mc + 1) * 128],
                                          onesb[:, 0:1],
                                          start=False, stop=(ts == NTH - 1),
                                          skip_group_check=True)
                cx_pe = c.inc(lm, "pe")
                nc.vector.wait_ge(c.s["pe"], cx_pe)
                ins = nc.vector.tensor_copy(
                    ctxbuf[:, half, h, mc, 0:65], pc[:, 0:65])
                bank_dv[2 + mc] = c.inc(ins, "dv")


def emit_l0_attn_pass2(c, stream, W, B, half, cv, rebuild_xfm,
                       spill_to=None):
    nc = c.nc
    xfm, stats, silu32 = B["xfm"], B["stats"], B["silu32"]
    paug, wqk, wvt = B["paug"], B["wqk"], B["wvt"]
    wotF = lambda api: wvt[:, api // 2,
                           (api % 2) * 512:(api % 2) * 512 + 512]
    wot_dst = wvt.ap()[:, 0:4, :].rearrange("p a (b f) -> p a b f", b=2)
    ctxbuf, vsumbuf, acol = B["ctxbuf"], B["vsumbuf"], B["acol"]
    p_a, p_b, p_dd, p_dd2, p_c0, p_c1 = (B["p_a"], B["p_b"], B["p_dd"],
                                         B["p_dd2"], B["p_c0"], B["p_c1"])
    p_tr = B["p_tr"]
    tiles = list(range(half * NTH, (half + 1) * NTH))

    q2 = cv.bf16([128, 2048])
    q2sq = cv.bf16([128, 2048])
    qp = cv.bf16([128, 16, 256])
    qpT = cv.bf16([128, 2, 2048])
    attn_fm = cv.bf16([128, 8, 2048])
    ctxf = cv.bf16([128, 2, 66])
    onesb = cv.bf16([128, 130])
    bb = cv.f32([128, 4])
    tmp32 = cv.f32([128, 2, 256])
    rowb = [cv.bf16([1, 512]) for _ in range(2)]
    recq = cv.f32([128, 512])

    c.wait(nc.vector, "sp")
    if rebuild_xfm:
        emit_tm_to_fm(c, stream, tiles, xfm, B)
    ins = nc.vector.memset(onesb[:], 1.0)
    c.inc(ins, "dv")
    epsb = B["kmax"][:, 3:4]
    c.wait(nc.gpsimd, "pe")
    ins = nc.gpsimd.dma_start(
        out=wot_dst, in_=W["wo"][:, 0:512].rearrange(
            "(ah p) m -> p ah m", p=128).rearrange(
            "p (a b) m -> p a b m", b=2))
    c.inc(ins, "gd", 16)
    wot_gd0 = c.n["gd"]

    ddbanks = [p_dd, p_dd2, p_c0, p_c1]
    outst = {"pending": None, "norm": 0, "nb": [0, 0], "bc": [0, 0]}

    def flush_out():
        if outst["pending"] is None:
            return
        ob, pi, rmark, dst = outst["pending"]
        outst["pending"] = None
        nc.tensor.wait_ge(c.s["dv"], rmark)
        ins = nc.tensor.matmul(ob[64:128, :], onesb[0:1, 0:64],
                               rowb[pi][0:1, 0:512], start=True, stop=True)
        outst["bc"][pi] = c.inc(ins, "pe")
        nc.scalar.wait_ge(c.s["pe"], outst["bc"][pi])
        nc.scalar.wait_ge(c.s["dv"], outst["norm"])
        ins = nc.scalar.copy(recq[0:64, :], ob[64:128, :])
        cm = c.inc(ins, "ac")
        nc.vector.wait_ge(c.s["ac"], cm)
        ins = nc.vector.tensor_mul(dst, ob[0:64, :], recq[0:64, :])
        outst["norm"] = outst["nb"][pi] = c.inc(ins, "dv")

    for ap_i in range(8):
        wq_t = wqk[ap_i % 2]
        c.guard_buf(nc.gpsimd, "l0a_wqk%d" % (ap_i % 2))
        ins = nc.gpsimd.dma_start(
            out=wq_t[:, 0:8, :],
            in_=W["wq"][:, ap_i * 128:(ap_i + 1) * 128].rearrange(
                "(cc p) m -> p cc m", p=128))
        c.inc(ins, "gd", 16)
        gd_q = c.n["gd"]
        ac_q = c.n["ac"]
        flush_out()
        c.wait(nc.tensor, "ac")
        c.wait(nc.tensor, "dv")
        for w in range(4):
            tw = slice(w * 512, (w + 1) * 512)
            qb = p_a if w % 2 == 0 else p_b
            nc.tensor.wait_ge(c.s["gd"], gd_q)
            if w >= 2:
                nc.tensor.wait_ge(c.s["ac"], ac_q + w - 1)
            for cc in range(NCC):
                lq = nc.tensor.matmul(qb[:], wq_t[:, cc, :], xfm[:, cc, tw],
                                      start=(cc == 0), stop=(cc == NCC - 1))
            c.inc(lq, "pe")
            c.consumed("l0a_wqk%d" % (ap_i % 2))
            c.wait(nc.scalar, "pe")
            ins = nc.scalar.copy(q2[:, tw], qb[:])
            c.inc(ins, "ac")
        c.wait(nc.scalar, "pe")
        ins = nc.scalar.activation(out=q2sq[:, :], in_=q2[:, :], func=AF.Square)
        sqm = c.inc(ins, "ac")
        bankr = [0, 0, 0, 0]
        for hh in range(2):
            h = ap_i * 2 + hh
            hb = 64 * hh
            vsl0 = (half * 16 + h) * 66
            ctxm = 0
            for mc in range(2):
                pc = p_c0 if mc == 0 else p_c1
                nc.tensor.wait_ge(c.s["ac"], bankr[2 + mc])
                ins = nc.tensor.matmul(pc[:, 0:65], onesb[0:1, 0:128],
                                       vsumbuf[0:1, vsl0:vsl0 + 65],
                                       start=True, stop=True)
                pcm = c.inc(ins, "pe")
                nc.vector.wait_ge(c.s["pe"], pcm)
                ins = nc.vector.scalar_tensor_tensor(
                    out=ctxf[:, mc, 0:65], in0=ctxbuf[:, half, h, mc, 0:65],
                    scalar=acol[:, 0:1], in1=pc[:, 0:65],
                    op0=ALU.mult, op1=ALU.add)
                ctxm = c.inc(ins, "dv")
            nc.tensor.wait_ge(c.s["ac"], sqm)
            qpm, rmm, subm, expm = {}, {}, {}, {}

            def finish_dd(t):
                db_ = ddbanks[t % 4]
                pv = t % 2
                pb4 = t % 4
                if t >= 4:
                    nc.vector.wait_ge(c.s["ac"], expm[t - 4])
                nc.vector.wait_ge(c.s["dv"], rmm[t])
                ins = nc.vector.tensor_sub(bb[:, pb4:pb4 + 1],
                                           db_[:, 256:257],
                                           stats[:, pv:pv + 1])
                subm[t] = c.inc(ins, "dv")
                nc.scalar.wait_ge(c.s["dv"], subm[t])
                ins = nc.scalar.activation(out=tmp32[:, pv, :],
                                           in_=db_[:, 0:256],
                                           func=AF.Exp, bias=bb[:, pb4:pb4 + 1])
                expm[t] = c.inc(ins, "ac")
                bankr[t % 4] = expm[t]
                ins = nc.scalar.activation(out=qp[:, t, :], in_=tmp32[:, pv, :],
                                           func=AF.Identity, scale=PRATIO,
                                           bias=epsb)
                qpm[t] = c.inc(ins, "ac")

            for ts in range(NTH):
                db = ddbanks[ts % 4]
                if ts >= 4:
                    nc.tensor.wait_ge(c.s["ac"], expm[ts - 4])
                elif ts >= 2:
                    nc.tensor.wait_ge(c.s["dv"], ctxm)
                    nc.tensor.wait_ge(c.s["ac"], bankr[ts % 4])
                else:
                    nc.tensor.wait_ge(c.s["ac"], bankr[ts % 4])
                tsl = slice(ts * 128, (ts + 1) * 128)
                nc.tensor.matmul(db[:, 0:257], q2[hb:hb + 64, tsl],
                                 B["paugA"][hb:hb + 64, :],
                                 start=True, stop=False)
                ins = nc.tensor.matmul(db[:, 0:257], q2sq[hb:hb + 64, tsl],
                                       B["paugB"][hb:hb + 64, :],
                                       start=False, stop=True)
                ddm = c.inc(ins, "pe")
                nc.vector.wait_ge(c.s["pe"], ddm)
                ins = nc.vector.reduce_max(stats[:, ts % 2:ts % 2 + 1],
                                           db[:, 0:256], axis=AX.X)
                rmm[ts] = c.inc(ins, "dv")
                if ts >= 1:
                    finish_dd(ts - 1)
            finish_dd(NTH - 1)
            trm = {}
            for win in range(4):
                for ts in range(4 * win, 4 * win + 4):
                    ptr = p_tr[ts % 2]
                    nc.tensor.wait_ge(c.s["ac"], qpm[ts])
                    if ts >= 2:
                        nc.tensor.wait_ge(c.s["ac"], trm[ts - 2])
                    for mc in range(2):
                        ins = nc.tensor.transpose(
                            ptr[:, mc * 128:(mc + 1) * 128],
                            qp[:, ts, mc * 128:(mc + 1) * 128], B["idbf"][:])
                    tm = c.inc(ins, "pe")
                    nc.scalar.wait_ge(c.s["pe"], tm)
                    ins = nc.scalar.copy(
                        qpT[:, 0:2, ts * 128:(ts + 1) * 128],
                        ptr[:, 0:256].rearrange("p (a b) -> p a b", a=2))
                    trm[ts] = c.inc(ins, "ac")
                pi = win % 2
                ob = p_a if pi == 0 else p_b
                wsl = slice(win * 512, (win + 1) * 512)
                nc.tensor.wait_ge(c.s["ac"], trm[4 * win + 3])
                nc.tensor.wait_ge(c.s["dv"], outst["nb"][pi])
                nc.tensor.matmul(ob[0:65, 0:512], ctxf[:, 0, 0:65],
                                 qpT[:, 0, wsl], start=True, stop=False,
                                 skip_group_check=True)
                ins = nc.tensor.matmul(ob[0:65, 0:512], ctxf[:, 1, 0:65],
                                       qpT[:, 1, wsl], start=False, stop=True,
                                       skip_group_check=True)
                zm = c.inc(ins, "pe")
                flush_out()
                nc.vector.wait_ge(c.s["pe"], max(zm, outst["bc"][pi]))
                with nc.allow_low_precision(
                        reason="recip rounded to bf16 for the bcast matmul "
                               "operand; same rounding as f32-recip+cast"):
                    ins = nc.vector.reciprocal(rowb[pi][0:1, 0:512],
                                               ob[64:65, 0:512])
                rmark = c.inc(ins, "dv")
                outst["pending"] = (ob, pi, rmark,
                                    attn_fm[hb:hb + 64, ap_i, wsl])
    flush_out()
    c.wait(nc.tensor, "dv")
    obanks = [p_dd[:, 0:512], p_dd2[:, 0:512], p_c0[:, 0:512], p_c1[:, 0:512]]
    evm = [0, 0, 0, 0]
    wot_gd = wot_gd0
    for fh in range(2):
        if fh == 1:
            c.wait(nc.gpsimd, "pe")
            ins = nc.gpsimd.dma_start(
                out=wot_dst, in_=W["wo"][:, 512:1024].rearrange(
                    "(ah p) m -> p ah m", p=128).rearrange(
                    "p (a b) m -> p a b m", b=2))
            c.inc(ins, "gd", 16)
            wot_gd = c.n["gd"]
        nc.tensor.wait_ge(c.s["gd"], wot_gd)
        for ti in range(NTH):
            tile = tiles[ti]
            ob = obanks[ti % 4]
            nc.tensor.wait_ge(c.s["dv"], evm[ti % 4])
            for api in range(8):
                nc.tensor.matmul(ob, attn_fm[:, api, ti * 128:(ti + 1) * 128],
                                 wotF(api), start=(api == 0), stop=False,
                                 skip_group_check=True)
            lo = nc.tensor.matmul(ob, onesb[0:1, 0:128],
                                  B["borow"][0:1, fh * 512:(fh + 1) * 512],
                                  start=False, stop=True,
                                  skip_group_check=True)
            pem = c.inc(lo, "pe")
            nc.vector.wait_ge(c.s["pe"], pem)
            ins = nc.vector.tensor_add(
                stream[:, tile, fh * 512:(fh + 1) * 512], ob,
                stream[:, tile, fh * 512:(fh + 1) * 512])
            evm[ti % 4] = c.inc(ins, "dv")
    c.wait(nc.scalar, "dv")
    emit_ln(c, stream, tiles, silu32, stats, spill_to=spill_to)


def emit_l0_attn(c, stream, W, B, scrA, scrB, cc_in, cc_out):
    nc = c.nc
    paug, pload = B["paug"], B["pload"]
    paugA, paugB = B["paugA"], B["paugB"]
    ins = nc.sync.dma_start(out=pload[0:64, :], in_=W["projT"])
    c.inc(ins, "ld", 16)
    ins = nc.sync.dma_start(out=pload[64:128, :], in_=W["projT"])
    c.inc(ins, "ld", 16)
    ins = nc.vector.memset(paug[:], 0.0)
    c.inc(ins, "dv")
    ins = nc.vector.memset(paug[64:128, 256:257], -0.5 * PNORM * PNORM)
    c.inc(ins, "dv")
    ins = nc.vector.memset(paugA[:], 0.0)
    c.inc(ins, "dv")
    ins = nc.vector.memset(paugB[:], 0.0)
    c.inc(ins, "dv")
    ins = nc.vector.memset(paugB[:, 256:257], -0.5 * PNORM * PNORM)
    c.inc(ins, "dv")
    c.wait(nc.scalar, "ld")
    c.wait(nc.scalar, "dv")
    ins = nc.scalar.mul(paug[0:64, 0:256], pload[0:64, :], PNORM)
    c.inc(ins, "ac")
    ins = nc.scalar.mul(paugA[:, 0:256], pload[:], PNORM)
    c.inc(ins, "ac")
    ins = nc.vector.memset(B["kmax"][:, 0:1], -3.0e38)
    c.inc(ins, "dv")
    ins = nc.vector.memset(B["kmax"][:, 3:4], PEPS * PRATIO)
    c.inc(ins, "dv")
    ins = nc.vector.memset(B["onesrow"][:], 1.0)
    c.inc(ins, "dv")
    ins = nc.gpsimd.dma_start(out=B["wvt"][:], in_=W["wv"].rearrange(
        "(cc p) f -> p cc f", p=128))
    c.inc(ins, "gd", 16)
    ins = nc.gpsimd.dma_start(out=B["borow"][:], in_=W["bo_row"])
    c.inc(ins, "gd", 16)
    emit_join(c)

    emit_spill(c, stream, 1, scrB)
    cv = Carver(stream.ap()[:, NTH:NT, :].rearrange("p a b -> p (a b)"))
    emit_l0_attn_pass1(c, stream, W, B, 0, cv)
    emit_join(c)
    emit_spill(c, stream, 0, scrA)
    emit_unspill(c, stream, 1, scrB)
    cv = Carver(stream.ap()[:, 0:NTH, :].rearrange("p a b -> p (a b)"))
    emit_l0_attn_pass1(c, stream, W, B, 1, cv)
    emit_join(c)

    c.wait(nc.tensor, "dv")
    ins = nc.tensor.transpose(B["p_a"][0:1, 0:128], B["kmax"][:, 0:1],
                              B["idf32"][:])
    c.inc(ins, "pe")
    c.wait(nc.vector, "pe")
    ins = nc.vector.reduce_max(B["kmax"][0:1, 1:2], B["p_a"][0:1, 0:128],
                               axis=AX.X)
    c.inc(ins, "dv")
    c.wait(nc.vector, "dv")
    nc.vector.memset(B["stats"][0:1, 0:8], 0.0)
    ins = nc.vector.tensor_copy(B["stats"][0:1, 0:1], B["kmax"][0:1, 1:2])
    c.inc(ins, "dv")
    c.wait(nc.gpsimd, "dv")
    ins = nc.gpsimd.dma_start(out=cc_in, in_=B["stats"][0:1, 0:8])
    c.inc(ins, "gd", 16)
    c.wait(nc.gpsimd, "gd")
    ins = nc.gpsimd.collective_compute(
        "AllReduce", ALU.max, replica_groups=[list(range(8))],
        ins=[cc_in.opt()], outs=[cc_out.opt()])
    c.inc(ins, "cc")
    c.wait(nc.gpsimd, "cc", 1)
    ins = nc.gpsimd.dma_start(out=B["kmax"][0:1, 1:2], in_=cc_out[0:1, 0:1])
    c.inc(ins, "gd", 16)
    c.wait(nc.scalar, "gd")
    ins = nc.scalar.activation(out=B["kmax"][0:1, 2:3], in_=B["kmax"][0:1, 1:2],
                               func=AF.Exp, scale=-1.0)
    c.inc(ins, "ac")
    c.wait(nc.vector, "ac")
    nc.vector.tensor_scalar_mul(B["kmax"][0:1, 2:3], B["kmax"][0:1, 2:3], PRATIO)
    ins = nc.vector.tensor_copy(B["kmbf"][0:1, 1:2], B["kmax"][0:1, 2:3])
    c.inc(ins, "dv")
    c.wait(nc.tensor, "dv")
    ins = nc.tensor.matmul(B["p_c0"][:, 0:1], B["onesrow"][0:1, 0:128],
                           B["kmbf"][0:1, 1:2], start=True, stop=True)
    c.inc(ins, "pe")
    c.wait(nc.scalar, "pe")
    ins = nc.scalar.copy(B["acol"][:, 0:1], B["p_c0"][:, 0:1])
    c.inc(ins, "ac")
    emit_join(c)

    cv = Carver(stream.ap()[:, 0:NTH, :].rearrange("p a b -> p (a b)"))
    emit_l0_attn_pass2(c, stream, W, B, 1, cv, rebuild_xfm=False,
                       spill_to=scrB)
    emit_join(c)
    emit_unspill(c, stream, 0, scrA)
    cv = Carver(stream.ap()[:, NTH:NT, :].rearrange("p a b -> p (a b)"))
    emit_l0_attn_pass2(c, stream, W, B, 0, cv, rebuild_xfm=True,
                       spill_to=scrA)
    emit_join(c)
    emit_unspill(c, stream, 1, scrB)
    c.wait(nc.sync, "sp")
    emit_join(c)


def prep_l0_attn_weights(inp):
    return {
        "wq": np.ascontiguousarray(np.asarray(inp["L0_attn_wq"])),
        "wk": np.ascontiguousarray(np.asarray(inp["L0_attn_wk"])),
        "wv": np.ascontiguousarray(np.asarray(inp["L0_attn_wv"])),
        "wo": np.ascontiguousarray(np.asarray(inp["L0_attn_wo"])),
        "bo_row": np.ascontiguousarray(
            np.asarray(inp["L0_attn_bo"]).reshape(1, D)),
        "projT": np.ascontiguousarray(np.asarray(inp["L0_attn_proj"]).T),
    }


L0_ATTN_SHAPES = {"wq": [D, D], "wk": [D, D], "wv": [D, D], "wo": [D, D],
                  "bo_row": [1, D], "projT": [64, 256]}




def emit_l1_ffn_rps(c, stream, W, B, half, cv):
    nc = c.nc
    xfm, stats, silu32 = B["xfm"], B["stats"], B["silu32"]
    w1b, w2b, rw3, rb1t, rb2t = B["w1b"], B["w2b"], B["rw3"], B["rb1t"], B["rb2t"]
    utb, iota8, selbuf, slotbuf = B["utb"], B["iota8"], B["selbuf"], B["slotbuf"]
    offs = B["offs"]
    p_h1, p_h2, p_small = B["p_h1"], B["p_h2"], B["p_small"]
    p_l3 = p_small[:, 0:32]
    p_pb = p_small[:, 32:40]
    p_cs = p_small[0:1, 40:48]
    dispT = B["dispT"]
    tiles = list(range(half * NTH, (half + 1) * NTH))

    h1 = cv.bf16([128, 32, 1024])
    h2sb = B["h2sb"]
    t8a, t8b, t8c = B["t8a"], B["t8b"], B["t8c"]
    c8b, sc8, m8 = B["c8b"], B["sc8"], B["m8"]
    i8 = B["i8"]
    xbf = B["xbf"]
    p_eo = B["p_eo"]
    pbs = [p_h1[:], p_h2[:], p_eo[:, 0:512], p_eo[:, 512:1024]]

    c.wait(nc.vector, "sp")
    emit_tm_to_fm(c, stream, tiles, xfm, B)
    ins = nc.vector.memset(offs[0:1, 0:8], 0.0)
    c.inc(ins, "dv")

    for tt in range(2):
        c.wait(nc.tensor, "ac")
        c.wait(nc.tensor, "sp")
        for H1c in range(32):
            wb = w1b[H1c % 2]
            c.guard_buf(nc.gpsimd, "l1f_w1b%d" % (H1c % 2))
            ins = nc.gpsimd.dma_start(
                out=wb[:],
                in_=W["rW1"][:, H1c * 128:(H1c + 1) * 128].rearrange(
                    "(cc p) m -> p cc m", p=128))
            c.inc(ins, "gd", 16)
            gd_w = c.n["gd"]
            nc.tensor.wait_ge(c.s["gd"], gd_w)
            for win in range(2):
                ttw = slice(tt * 1024 + win * 512, tt * 1024 + win * 512 + 512)
                pb = pbs[(2 * H1c + win) % 4]
                if H1c >= 1:
                    c.wait(nc.tensor, "ac", c.n["ac"] - 2)
                for cc in range(NCC):
                    lm = nc.tensor.matmul(pb, wb[:, cc, :], xfm[:, cc, ttw],
                                          start=(cc == 0), stop=(cc == NCC - 1))
                c.inc(lm, "pe")
                c.consumed("l1f_w1b%d" % (H1c % 2))
                c.wait(nc.scalar, "pe")
                ins = nc.scalar.activation(
                    out=h1[:, H1c, win * 512:(win + 1) * 512], in_=pb,
                    func=AF.Relu, bias=rb1t[:, H1c:H1c + 1])
                c.inc(ins, "ac")
        c.wait(nc.tensor, "ac")
        p_l3 = p_small[:, 0:64]
        pend_l3 = None
        relm = {}

        def emit_l3(pl):
            par, rm, h2c = pl
            nc.tensor.wait_ge(c.s["ac"], rm)
            for sl in range(8):
                lm = nc.tensor.matmul(
                    p_l3[:, sl * 8:(sl + 1) * 8],
                    h2sb[:, par, sl * 128:(sl + 1) * 128],
                    rw3[:, h2c, :], start=(h2c == 0 and sl == 0), stop=False,
                    skip_group_check=True)
            relm[h2c] = c.inc(lm, "pe")

        for H2c in range(32):
            wb = w2b[H2c % 2]
            par = H2c % 2
            c.guard_buf(nc.gpsimd, "l1f_w2b%d" % (H2c % 2))
            ins = nc.gpsimd.dma_start(
                out=wb[:, 0:16, :],
                in_=W["rW2"][0:2048, H2c * 128:(H2c + 1) * 128].rearrange(
                    "(hc p) m -> p hc m", p=128))
            c.inc(ins, "gd", 16)
            ins = nc.gpsimd.dma_start(
                out=wb[:, 16:32, :],
                in_=W["rW2"][2048:4096, H2c * 128:(H2c + 1) * 128].rearrange(
                    "(hc p) m -> p hc m", p=128))
            c.inc(ins, "gd", 16)
            gd_w = c.n["gd"]
            nc.tensor.wait_ge(c.s["gd"], gd_w)
            for win in range(2):
                pb = pbs[(2 * H2c + win) % 4]
                if H2c >= 1:
                    c.wait(nc.tensor, "ac", c.n["ac"] - 2)
                for hc in range(32):
                    lm = nc.tensor.matmul(
                        pb, wb[:, hc, :],
                        h1[:, hc, win * 512:(win + 1) * 512],
                        start=(hc == 0), stop=(hc == 31))
                c.inc(lm, "pe")
                c.consumed("l1f_w2b%d" % (H2c % 2))
                if win == 0 and pend_l3 is not None:
                    emit_l3(pend_l3)
                    pend_l3 = None
                c.wait(nc.scalar, "pe")
                prev = relm.get(H2c - 2)
                if prev is not None:
                    nc.scalar.wait_ge(c.s["pe"], prev)
                ins = nc.scalar.activation(
                    out=h2sb[:, par, win * 512:(win + 1) * 512], in_=pb,
                    func=AF.Relu, bias=rb2t[:, H2c:H2c + 1])
                c.inc(ins, "ac")
            pend_l3 = (par, c.n["ac"], H2c)
        emit_l3(pend_l3)
        for sl in range(8):
            lm = nc.tensor.matmul(p_l3[:, sl * 8:(sl + 1) * 8],
                                  B["onesrow"][0:1, 0:128], B["rb3r"][0:1, :],
                                  start=False, stop=(sl == 7),
                                  skip_group_check=True)
        c.inc(lm, "pe")
        for sl in range(8):
            ts = tt * 8 + sl
            l3 = p_l3[:, sl * 8:(sl + 1) * 8]
            c.wait(nc.vector, "pe")
            vw(c, nc.vector.reduce_max(stats[:, 0:1], l3, axis=AX.X))
            vw(c, nc.vector.tensor_scalar_mul(stats[:, 1:2], stats[:, 0:1], -1.0))
            c.wait(nc.scalar, "dv")
            ins = nc.scalar.activation(out=sc8[:], in_=l3, func=AF.Exp,
                                       bias=stats[:, 1:2],
                                       accum_out=stats[:, 2:3])
            c.inc(ins, "ac")
            c.wait(nc.vector, "ac")
            vw(c, nc.vector.reciprocal(stats[:, 3:4], stats[:, 2:3]))
            vw(c, nc.vector.tensor_scalar_mul(sc8[:], sc8[:], stats[:, 3:4]))
            vw(c, nc.vector.max(m8[:], sc8[:]))
            vw(c, nc.vector.max_index(i8[:], m8[:], sc8[:]))
            vw(c, nc.vector.tensor_copy(selbuf[:, half, ts, 0:1], i8[:, 0:1]))
            vw(c, nc.vector.tensor_copy(selbuf[:, half, ts, 1:2], i8[:, 1:2]))
            vw(c, nc.vector.tensor_copy(selbuf[:, half, ts, 2:3], m8[:, 0:1]))
            vw(c, nc.vector.tensor_copy(selbuf[:, half, ts, 3:4], m8[:, 1:2]))
    for ts in range(NTH):
        e1 = selbuf[:, half, ts, 0:1]
        e2 = selbuf[:, half, ts, 1:2]
        c.wait(nc.vector, "pe")
        nc.vector.tensor_scalar(out=t8a[:], in0=iota8[:], scalar1=e1,
                                scalar2=None, op0=ALU.is_equal)
        ins = nc.vector.tensor_scalar(out=t8b[:], in0=iota8[:], scalar1=e2,
                                      scalar2=None, op0=ALU.is_equal)
        c.inc(ins, "dv")
        c.wait(nc.vector, "dv")
        vw(c, nc.vector.tensor_add(t8c[:], t8a[:], t8b[:]))
        vw(c, nc.vector.tensor_copy(c8b[:], t8c[:]))
        c.wait(nc.tensor, "dv")
        nc.tensor.matmul(p_pb, utb[:], c8b[:], start=True, stop=False,
                         skip_group_check=True)
        ins = nc.tensor.matmul(p_pb, B["onesrow32"][0:1, 0:128],
                               offs[0:1, 0:8], start=False, stop=True,
                               skip_group_check=True)
        c.inc(ins, "pe")
        ins = nc.tensor.matmul(p_cs, B["onescol"][:, 0:1], c8b[:],
                               start=True, stop=True, skip_group_check=True)
        c.inc(ins, "pe")
        c.wait(nc.vector, "pe")
        if "dbg_pb" in B:
            nc.vector.tensor_copy(B["dbg_pb"][:], p_pb)
            nc.vector.tensor_copy(B["dbg_t8c"][:], t8c[:])
            nc.vector.tensor_copy(B["dbg_c8b"][:], c8b[:])
        vw(c, nc.vector.tensor_mul(t8a[:], t8a[:], p_pb))
        vw(c, nc.vector.reduce_sum(stats[:, 0:1], t8a[:], axis=AX.X))
        vw(c, nc.vector.tensor_mul(t8b[:], t8b[:], p_pb))
        vw(c, nc.vector.reduce_sum(stats[:, 1:2], t8b[:], axis=AX.X))
        vw(c, nc.vector.tensor_tensor(stats[:, 2:3], e1, e2, ALU.is_equal))
        vw(c, nc.vector.tensor_add(stats[:, 1:2], stats[:, 1:2], stats[:, 2:3]))
        vw(c, nc.vector.tensor_add(offs[0:1, 0:8], offs[0:1, 0:8], p_cs))
        for kk in range(2):
            pos = stats[:, kk:kk + 1]
            ekf = selbuf[:, half, ts, kk:kk + 1]
            skf = selbuf[:, half, ts, 2 + kk:3 + kk]
            keep = stats[:, 4:5]
            vw(c, nc.vector.tensor_scalar(out=keep, in0=pos, scalar1=float(CAP),
                                          scalar2=None, op0=ALU.is_lt))
            import os as _os
            if _os.environ.get("DBG_L1F_ZS"):
                vw(c, nc.vector.memset(skf, 0.0))
            else:
                vw(c, nc.vector.tensor_mul(skf, skf, keep))
            vw(c, nc.vector.tensor_scalar(out=stats[:, 5:6], in0=ekf,
                                          scalar1=640.0,
                                          scalar2=float(half * CAP),
                                          op0=ALU.mult, op1=ALU.add))
            vw(c, nc.vector.tensor_add(stats[:, 5:6], stats[:, 5:6], pos))
            vw(c, nc.vector.tensor_mul(stats[:, 5:6], stats[:, 5:6], keep))
            vw(c, nc.vector.tensor_scalar(out=stats[:, 6:7], in0=keep,
                                          scalar1=-float(E1 * 2 * CAP),
                                          scalar2=float(E1 * 2 * CAP),
                                          op0=ALU.mult, op1=ALU.add))
            vw(c, nc.vector.tensor_add(stats[:, 5:6], stats[:, 5:6], stats[:, 6:7]))
            vw(c, nc.vector.tensor_copy(slotbuf[:, half, ts, kk:kk + 1],
                                        stats[:, 5:6]))
        ins = nc.vector.tensor_copy(xbf[:], stream[:, tiles[ts], :])
        c.inc(ins, "dv")
        c.wait(nc.gpsimd, "dv")
        for kk in range(2):
            ins = nc.gpsimd.indirect_dma_start(
                out=dispT,
                out_offset=IndirectOffsetOnAxis(
                    ap=slotbuf[:, half, ts, kk:kk + 1], axis=0),
                in_=xbf[:], in_offset=None)
            c.inc(ins, "gd", 16)


def emit_l1_ffn_experts(c, W, B, cv):
    nc = c.nc
    ew1, ew2 = B["ew1"], B["ew2"]
    ebt1, ebt2 = B["ebt1"], B["ebt2"]
    p_h1, p_h2, p_eo = B["p_h1"], B["p_h2"], B["p_eo"]
    dispT, eoutD = B["dispT"], B["eoutD"]
    NR = 2 * CAP

    hh = cv.bf16([128, 32, NR])
    efo = cv.f32([128, NR])
    tmr = cv.f32([128, 5, 128])
    dfs = [B["df"], cv.bf16([128, NCC, NR])]

    sdm = {}
    dfree = {}

    def prefetch(e):
        if e >= E1:
            return
        if e >= 2:
            nc.scalar.wait_ge(c.s["pe"], dfree[e - 2])
        dfx = dfs[e % 2]
        for ccd in range(NCC):
            ins = nc.scalar.dma_start_transpose(
                dfx[:, ccd, :],
                dispT[e * NR:(e + 1) * NR, ccd * 128:(ccd + 1) * 128])
            c.inc(ins, "sd", 16)
        sdm[e] = c.n["sd"]

    prefetch(0)
    prefetch(1)
    for e in range(E1):
        df = dfs[e % 2]
        c.wait(nc.sync, "ac")
        ins = nc.sync.dma_start(out=ebt1[:], in_=W["eb1t"][e])
        c.inc(ins, "ld", 16)
        ins = nc.sync.dma_start(out=ebt2[:], in_=W["eb2t"][e])
        c.inc(ins, "ld", 16)
        ld_b = c.n["ld"]
        ac_h = c.n["ac"]

        def _issue_w1(Hc):
            wbx = ew1[Hc % 2]
            c.guard_buf(nc.gpsimd, "l1f_w1b%d" % (Hc % 2))
            ins = nc.gpsimd.dma_start(
                out=wbx[:],
                in_=W["eW1"][e][:, Hc * 128:(Hc + 1) * 128].rearrange(
                    "(cc p) m -> p cc m", p=128))
            c.inc(ins, "gd", 16)
            return c.n["gd"]
        gd_marks1 = [_issue_w1(0), _issue_w1(1)]
        for Hc in range(32):
            wb = ew1[Hc % 2]
            gd_w = gd_marks1[Hc]
            nc.tensor.wait_ge(c.s["gd"], gd_w)
            nc.tensor.wait_ge(c.s["sd"], sdm[e])
            if Hc >= 2:
                nc.tensor.wait_ge(c.s["ac"], ac_h + 2 * (Hc - 1))
            else:
                nc.tensor.wait_ge(c.s["ld"], ld_b)
                c.wait(nc.tensor, "ac")
                c.wait(nc.tensor, "dv")
            for nh in range(2):
                pb = p_h1 if nh == 0 else p_h2
                for cc in range(NCC):
                    lm = nc.tensor.matmul(
                        pb[:, 0:CAP], wb[:, cc, :],
                        df[:, cc, nh * CAP:(nh + 1) * CAP],
                        start=(cc == 0), stop=(cc == NCC - 1))
                w1_pe = c.inc(lm, "pe")
                c.consumed("l1f_w1b%d" % (Hc % 2))
                c.wait(nc.scalar, "pe")
                ins = nc.scalar.activation(out=hh[:, Hc, nh * CAP:(nh + 1) * CAP],
                                           in_=pb[:, 0:CAP], func=AF.Relu,
                                           bias=ebt1[:, Hc:Hc + 1])
                c.inc(ins, "ac")
            if Hc + 2 < 32:
                gd_marks1.append(_issue_w1(Hc + 2))
        dfree[e] = w1_pe
        prefetch(e + 1)
        c.wait(nc.tensor, "ac")

        def _issue_w2(fc):
            wbx = ew2[fc % 2]
            c.guard_buf(nc.gpsimd, "l1f_w2b%d" % (fc % 2))
            ins = nc.gpsimd.dma_start(
                out=wbx[:],
                in_=W["eW2"][e][:, fc * 128:(fc + 1) * 128].rearrange(
                    "(hc p) m -> p hc m", p=128))
            c.inc(ins, "gd", 16)
            return c.n["gd"]
        gd_marks2 = [_issue_w2(0), _issue_w2(1)]
        for fc in range(NCC):
            wb = ew2[fc % 2]
            gd_w = gd_marks2[fc]
            nc.tensor.wait_ge(c.s["gd"], gd_w)
            if fc >= 2:
                c.wait(nc.tensor, "ac")
                c.wait(nc.tensor, "dv")
            for nh in range(2):
                pb = p_h1 if nh == 0 else p_h2
                for Hc in range(32):
                    lm = nc.tensor.matmul(
                        pb[:, 0:CAP], wb[:, Hc, :],
                        hh[:, Hc, nh * CAP:(nh + 1) * CAP],
                        start=(Hc == 0), stop=(Hc == 31))
                c.inc(lm, "pe")
                c.consumed("l1f_w2b%d" % (fc % 2))
                c.wait(nc.scalar, "pe")
                ins = nc.scalar.activation(
                    out=efo[:, nh * CAP:(nh + 1) * CAP], in_=pb[:, 0:CAP],
                    func=AF.Identity, bias=ebt2[:, fc:fc + 1])
                c.inc(ins, "ac")
            if fc + 2 < NCC:
                gd_marks2.append(_issue_w2(fc + 2))
            c.wait(nc.tensor, "ac")
            for j in range(5):
                ins = nc.tensor.transpose(
                    p_eo[:, j * 128:(j + 1) * 128],
                    efo[:, j * 128:(j + 1) * 128], B["idf32"][:])
            c.inc(ins, "pe")
            c.wait(nc.vector, "pe")
            ins = nc.vector.tensor_copy(tmr[:], p_eo[:, 0:640].rearrange(
                "p (j m) -> p j m", j=5))
            c.inc(ins, "dv")
            c.wait(nc.gpsimd, "dv")
            ins = nc.gpsimd.dma_start(
                out=eoutD[e * NR:(e + 1) * NR,
                          fc * 128:(fc + 1) * 128].rearrange(
                    "(j p) m -> p j m", p=128),
                in_=tmr[:])
            c.inc(ins, "gd", 16)
    c.wait(nc.gpsimd, "gd")


def emit_l1_ffn_gather(c, stream, B, half, spilled, spill_to=None):
    nc = c.nc
    selbuf, slotbuf = B["selbuf"], B["slotbuf"]
    eoutD = B["eoutD"]
    stats, silu32 = B["stats"], B["silu32"]
    tiles = list(range(half * NTH, (half + 1) * NTH))
    cv = Carver(stream.ap()[:, spilled * NTH:(spilled + 1) * NTH, :]
                .rearrange("p a b -> p (a b)"))
    gbufs = [cv.f32([128, 1024]) for _ in range(4)]
    c.wait(nc.vector, "sp")
    c.wait(nc.gpsimd, "sp")
    gm, am = {}, {}
    for i, (ts, kk) in enumerate((t, k) for t in range(NTH) for k in range(2)):
        if i >= 4:
            nc.gpsimd.wait_ge(c.s["dv"], am[i - 4])
        ins = nc.gpsimd.indirect_dma_start(
            out=gbufs[i % 4][:], out_offset=None,
            in_=eoutD,
            in_offset=IndirectOffsetOnAxis(
                ap=slotbuf[:, half, ts, kk:kk + 1], axis=0))
        gm[i] = c.inc(ins, "gd", 16)
        nc.vector.wait_ge(c.s["gd"], gm[i])
        ins = nc.vector.scalar_tensor_tensor(
            out=stream[:, tiles[ts], :], in0=gbufs[i % 4][:],
            scalar=selbuf[:, half, ts, 2 + kk:3 + kk],
            in1=stream[:, tiles[ts], :], op0=ALU.mult, op1=ALU.add)
        am[i] = c.inc(ins, "dv")
    c.wait(nc.scalar, "dv")
    c.wait(nc.scalar, "sp")
    emit_ln(c, stream, tiles, silu32, stats, spill_to=spill_to)


def emit_l1_ffn(c, stream, W, B, scrA, scrB):
    nc = c.nc
    ins = nc.gpsimd.dma_start(out=B["rw3"][:], in_=W["rW3"].rearrange(
        "(hc p) m -> p hc m", p=128))
    c.inc(ins, "gd", 16)
    ins = nc.sync.dma_start(out=B["rb1t"][:], in_=W["rb1t"])
    c.inc(ins, "ld", 16)
    ins = nc.sync.dma_start(out=B["rb2t"][:], in_=W["rb2t"])
    c.inc(ins, "ld", 16)
    ins = nc.gpsimd.dma_start(out=B["rb3r"][:], in_=W["rb3"])
    c.inc(ins, "gd", 16)
    ins = nc.gpsimd.dma_start(out=B["utb"][:], in_=B["cin_ut"])
    c.inc(ins, "gd", 16)
    ins = nc.sync.dma_start(out=B["iota8"][:], in_=B["cin_iota"])
    c.inc(ins, "ld", 16)
    ins = nc.vector.memset(B["onesrow"][:], 1.0)
    ins = nc.vector.memset(B["onescol"][:], 1.0)
    ins = nc.vector.memset(B["onesrow32"][:], 1.0)
    ins = nc.vector.memset(B["g"][:], 0.0)
    c.inc(ins, "dv")
    c.wait(nc.gpsimd, "dv")
    ins = nc.gpsimd.dma_start(out=B["eoutD"][E1 * 2 * CAP:E1 * 2 * CAP + 1, :],
                              in_=B["g"][0:1, :])
    c.inc(ins, "gd", 16)
    emit_join(c)

    cv = Carver(stream.ap()[:, NTH:NT, :].rearrange("p a b -> p (a b)"))
    emit_l1_ffn_rps(c, stream, W, B, 0, cv)
    emit_join(c)
    emit_unspill(c, stream, 1, scrB)
    cv = Carver(stream.ap()[:, 0:NTH, :].rearrange("p a b -> p (a b)"))
    emit_l1_ffn_rps(c, stream, W, B, 1, cv)
    emit_join(c)
    import os as _os
    if _os.environ.get("DBG_L1F"):
        ins = nc.vector.tensor_copy(
            stream[:, NTH, 0:128],
            B["selbuf"].ap().rearrange("p a b cdim -> p (a b cdim)"))
        c.inc(ins, "dv")
        ins = nc.vector.tensor_copy(
            stream[:, NTH + 1, 0:64],
            B["slotbuf"].ap().rearrange("p a b cdim -> p (a b cdim)"))
        c.inc(ins, "dv")
        ins = nc.vector.tensor_copy(stream[:, NTH + 2, 0:128], B["utb"][:])
        nc.vector.tensor_copy(stream[:, NTH + 3, 0:8], B["dbg_t8c"][:])
        nc.vector.tensor_copy(stream[:, NTH + 3, 8:16], B["dbg_c8b"][:])
        nc.vector.tensor_copy(stream[:, NTH + 3, 16:24], B["iota8"][:])
        nc.vector.tensor_copy(stream[0:1, NTH + 3, 24:32], B["offs"][0:1, 0:8])
        ins2 = nc.vector.tensor_copy(stream[:, NTH + 3, 32:40], B["dbg_pb"][:])
        c.inc(ins2, "dv")
        emit_join(c)
        emit_unspill(c, stream, 0, scrA)
        c.wait(nc.sync, "sp")
        emit_join(c)
        return
    cv = Carver(stream.ap()[:, 0:NTH, :].rearrange("p a b -> p (a b)"))
    emit_l1_ffn_experts(c, W, B, cv)
    emit_join(c)
    emit_l1_ffn_gather(c, stream, B, 1, 0, spill_to=scrB)
    emit_join(c)
    emit_unspill(c, stream, 0, scrA)
    emit_l1_ffn_gather(c, stream, B, 0, 1, spill_to=scrA)
    emit_join(c)
    emit_unspill(c, stream, 1, scrB)
    c.wait(nc.sync, "sp")
    emit_join(c)


def prep_l1_ffn_weights(inp):
    rb1 = np.asarray(inp["L1_ffn_rb1"])
    rb2 = np.asarray(inp["L1_ffn_rb2"])
    eb1 = np.asarray(inp["L1_ffn_eb1"])
    return {
        "rW1": np.ascontiguousarray(np.asarray(inp["L1_ffn_rW1"])),
        "rW2": np.ascontiguousarray(np.asarray(inp["L1_ffn_rW2"])),
        "rW3": np.ascontiguousarray(np.asarray(inp["L1_ffn_rW3"])),
        "rb1t": np.ascontiguousarray(rb1.reshape(32, 128).T),
        "rb2t": np.ascontiguousarray(rb2.reshape(32, 128).T),
        "rb3": np.ascontiguousarray(np.asarray(inp["L1_ffn_rb3"]).reshape(1, 8)),
        "eW1": np.ascontiguousarray(np.asarray(inp["L1_ffn_eW1"])),
        "eW2": np.ascontiguousarray(np.asarray(inp["L1_ffn_eW2"])),
        "eb1t": np.ascontiguousarray(
            eb1.reshape(E1, 32, 128).transpose(0, 2, 1)),
        "eb2t": np.ascontiguousarray(
            np.asarray(inp["L1_ffn_eb2"]).reshape(E1, 8, 128).transpose(0, 2, 1)),
    }


L1_FFN_SHAPES = {"rW1": [D, H1R], "rW2": [H1R, H1R], "rW3": [H1R, 8],
                 "rb1t": [128, 32], "rb2t": [128, 32], "rb3": [1, 8],
                 "eW1": [E1, D, H1E], "eW2": [E1, H1E, D],
                 "eb1t": [E1, 128, 32], "eb2t": [E1, 128, 8]}


CAP0 = 768


def emit_l0_sparse_rps(c, stream, W, B, half, cv, spill_to=None):
    nc = c.nc
    xfm, stats, silu32 = B["xfm"], B["stats"], B["silu32"]
    rt, utb, iota8 = B["rt"], B["utb"], B["iota8"]
    selbuf, slotbuf, offs = B["selbuf"], B["slotbuf"], B["offs"]
    t8a, t8c, c8b = B["t8a"], B["t8c"], B["c8b"]
    m8, i8 = B["m8"], B["i8"]
    xbf = B["xcast"][0]
    p_u, p_g, p_y = B["p_u"], B["p_g"], B["p_y"]
    p_s = B["p_s"]
    p_r = p_s[:, 0:8]
    p_pb = p_s[:, 32:40]
    p_cs = p_s[0:1, 40:48]
    dispL0 = B["dispL0"]
    tiles = list(range(half * NTH, (half + 1) * NTH))

    hstore = cv.bf16([128, 16, 2048])

    c.wait(nc.vector, "sp")
    emit_tm_to_fm(c, stream, tiles, xfm, B)
    c.wait(nc.tensor, "gd")
    c.wait(nc.tensor, "ac")
    c.wait(nc.tensor, "sp")
    for i in range(NTH):
        ti = half * NTH + i
        c.wait(nc.tensor, "dv")
        for cc in range(NCC):
            last = nc.tensor.matmul(p_r, xfm[:, cc, i * 128:(i + 1) * 128],
                                    rt[:, cc, :], start=(cc == 0),
                                    stop=(cc == NCC - 1), skip_group_check=True)
        c.inc(last, "pe")
        c.wait(nc.vector, "pe")
        vw(c, nc.vector.max(m8[:], p_r))
        vw(c, nc.vector.max_index(i8[:], m8[:], p_r))
        vw(c, nc.vector.tensor_copy(selbuf[:, ti, 0:1], i8[:, 0:1]))
        e1 = selbuf[:, ti, 0:1]
        vw(c, nc.vector.tensor_scalar(out=t8a[:], in0=iota8[:], scalar1=e1,
                                      scalar2=None, op0=ALU.is_equal))
        vw(c, nc.vector.tensor_copy(c8b[:], t8a[:]))
        c.wait(nc.tensor, "dv")
        nc.tensor.matmul(p_pb, utb[:], c8b[:], start=True, stop=False,
                         skip_group_check=True)
        ins = nc.tensor.matmul(p_pb, B["onesrow32"][0:1, 0:128],
                               offs[0:1, 0:8], start=False, stop=True,
                               skip_group_check=True)
        c.inc(ins, "pe")
        ins = nc.tensor.matmul(p_cs, B["onescol"][:, 0:1], c8b[:],
                               start=True, stop=True, skip_group_check=True)
        c.inc(ins, "pe")
        c.wait(nc.vector, "pe")
        vw(c, nc.vector.tensor_mul(t8c[:], t8a[:], p_pb))
        vw(c, nc.vector.reduce_sum(stats[:, 0:1], t8c[:], axis=AX.X))
        vw(c, nc.vector.tensor_add(offs[0:1, 0:8], offs[0:1, 0:8], p_cs))
        pos = stats[:, 0:1]
        keep = stats[:, 4:5]
        vw(c, nc.vector.tensor_scalar(out=keep, in0=pos, scalar1=float(CAP0),
                                      scalar2=None, op0=ALU.is_lt))
        vw(c, nc.vector.tensor_scalar(out=stats[:, 5:6], in0=e1,
                                      scalar1=float(CAP0), scalar2=None,
                                      op0=ALU.mult))
        vw(c, nc.vector.tensor_add(stats[:, 5:6], stats[:, 5:6], pos))
        vw(c, nc.vector.tensor_mul(stats[:, 5:6], stats[:, 5:6], keep))
        vw(c, nc.vector.tensor_scalar(out=stats[:, 6:7], in0=keep,
                                      scalar1=-float(E0 * CAP0),
                                      scalar2=float(E0 * CAP0),
                                      op0=ALU.mult, op1=ALU.add))
        vw(c, nc.vector.tensor_add(stats[:, 5:6], stats[:, 5:6], stats[:, 6:7]))
        vw(c, nc.vector.tensor_copy(slotbuf[:, ti, 0:1], stats[:, 5:6]))
        vw(c, nc.vector.tensor_copy(xbf[:], stream[:, ti, :]))
        c.wait(nc.gpsimd, "dv")
        ins = nc.gpsimd.indirect_dma_start(
            out=dispL0,
            out_offset=IndirectOffsetOnAxis(ap=slotbuf[:, ti, 0:1], axis=0),
            in_=xbf[:], in_offset=None)
        c.inc(ins, "gd", 16)

    upT, gateT, downT = W["up_sT"], W["gate_sT"], W["down_sT"]
    wug, wdn = B["wug"], B["wdn"]
    pe_b, dv_b, ac_b = c.n["pe"], c.n["dv"], c.n["ac"]
    k = 0
    for hp in range(8):
        wb = wug[hp % 2]
        c.guard_buf(nc.gpsimd, "l0f_wug%d" % (hp % 2))
        ins = nc.gpsimd.dma_start(
            out=wb[:, :, 0:256],
            in_=upT[:, hp * 256:(hp + 1) * 256].rearrange(
                "(cc p) hh -> p cc hh", p=128))
        c.inc(ins, "gd", 16)
        ins = nc.gpsimd.dma_start(
            out=wb[:, :, 256:512],
            in_=gateT[:, hp * 256:(hp + 1) * 256].rearrange(
                "(cc p) hh -> p cc hh", p=128))
        c.inc(ins, "gd", 16)
        gd_mark = c.n["gd"]
        for hh in range(2):
            hc = hp * 2 + hh
            for hw in range(4):
                tw = slice(hw * 512, (hw + 1) * 512)
                nc.tensor.wait_ge(c.s["gd"], gd_mark)
                if k >= 1:
                    nc.tensor.wait_ge(c.s["ac"], ac_b + k)
                    nc.tensor.wait_ge(c.s["dv"], dv_b + k)
                else:
                    c.wait(nc.tensor, "ac", ac_b)
                    c.wait(nc.tensor, "dv", dv_b)
                for cc in range(NCC):
                    nc.tensor.matmul(
                        p_u[:], wb[:, cc, hh * 128:(hh + 1) * 128],
                        xfm[:, cc, tw], start=(cc == 0),
                        stop=(cc == NCC - 1))
                for cc in range(NCC):
                    lg = nc.tensor.matmul(
                        p_g[:], wb[:, cc, 256 + hh * 128:256 + (hh + 1) * 128],
                        xfm[:, cc, tw], start=(cc == 0),
                        stop=(cc == NCC - 1))
                c.inc(lg, "pe")
                c.consumed("l0f_wug%d" % (hp % 2))
                c.wait(nc.scalar, "pe")
                if k >= 1:
                    c.wait(nc.scalar, "dv", dv_b + k)
                ins = nc.scalar.activation(out=silu32[:, 0:512],
                                           in_=p_g[:], func=AF.Silu)
                c.inc(ins, "ac")
                c.wait(nc.vector, "ac")
                ins = nc.vector.tensor_mul(hstore[:, hc, tw], p_u[:],
                                           silu32[:, 0:512])
                c.inc(ins, "dv")
                k += 1
    for ch in range(2):
        cw = slice(ch * 512, (ch + 1) * 512)
        c.guard_buf(nc.gpsimd, "l0f_wdn")
        ins = nc.gpsimd.dma_start(
            out=wdn[:, 0:8, :],
            in_=downT[0:1024, ch * 512:(ch + 1) * 512].rearrange(
                "(hc p) cf -> p hc cf", p=128))
        c.inc(ins, "gd", 16)
        ins = nc.gpsimd.dma_start(
            out=wdn[:, 8:16, :],
            in_=downT[1024:2048, ch * 512:(ch + 1) * 512].rearrange(
                "(hc p) cf -> p hc cf", p=128))
        c.inc(ins, "gd", 16)
        gd_dn = c.n["gd"]
        dv_c = c.n["dv"]
        for ts in range(NTH):
            pb = p_y[ts % 2]
            tile = tiles[ts]
            nc.tensor.wait_ge(c.s["gd"], gd_dn)
            c.wait(nc.tensor, "dv", dv_b + 64)
            if ts >= 2:
                c.wait(nc.tensor, "dv", dv_c + ts - 1)
            else:
                c.wait(nc.tensor, "dv", dv_c)
            for hc in range(16):
                last = nc.tensor.matmul(
                    pb[:], hstore[:, hc, ts * 128:(ts + 1) * 128],
                    wdn[:, hc, :], start=(hc == 0), stop=(hc == 15))
            c.inc(last, "pe")
            c.consumed("l0f_wdn")
            c.wait(nc.vector, "pe")
            ins = nc.vector.tensor_add(stream[:, tile, cw], pb[:],
                                       stream[:, tile, cw])
            am = c.inc(ins, "dv")
            if ch == 1 and spill_to is not None:
                nc.sync.wait_ge(c.s["dv"], am)
                ins = nc.sync.dma_start(out=spill_to[:, ts, :],
                                        in_=stream[:, tile, :])
                c.inc(ins, "sp", 16)


def emit_l0_sparse_experts(c, W, B, cv):
    nc = c.nc
    wug, wdn = B["wug"], B["wdn"]
    p_u, p_g, p_y = B["p_u"], B["p_g"], B["p_y"]
    dispL0, eoutL0 = B["dispL0"], B["eoutL0"]
    NS = CAP0 // 128

    hst = cv.bf16([128, 16, CAP0])
    efo = [cv.f32([128, 512]) for _ in range(4)]
    dfs = [cv.bf16([128, NCC, CAP0]) for _ in range(2)]

    sdm = {}
    dfree = {}

    def prefetch(e):
        if e >= E0:
            return
        if e >= 2:
            nc.scalar.wait_ge(c.s["pe"], dfree[e - 2])
        dfx = dfs[e % 2]
        for ccd in range(NCC):
            ins = nc.scalar.dma_start_transpose(
                dfx[:, ccd, :],
                dispL0[e * CAP0:(e + 1) * CAP0, ccd * 128:(ccd + 1) * 128])
            c.inc(ins, "sd", 16)
        sdm[e] = c.n["sd"]

    prefetch(0)
    prefetch(1)
    gdm = [0, 0, 0, 0]
    ym = [0, 0]
    for e in range(E0):
        df = dfs[e % 2]
        upT, gateT, downT = W["upT"][e], W["gateT"][e], W["downT"][e]
        ac_h = c.n["ac"]
        hm = {}
        k = 0
        for hp in range(8):
            wb = wug[hp % 2]
            c.guard_buf(nc.gpsimd, "l0f_wug%d" % (hp % 2))
            ins = nc.gpsimd.dma_start(
                out=wb[:, :, 0:256],
                in_=upT[:, hp * 256:(hp + 1) * 256].rearrange(
                    "(cc p) hh -> p cc hh", p=128))
            c.inc(ins, "gd", 16)
            ins = nc.gpsimd.dma_start(
                out=wb[:, :, 256:512],
                in_=gateT[:, hp * 256:(hp + 1) * 256].rearrange(
                    "(cc p) hh -> p cc hh", p=128))
            c.inc(ins, "gd", 16)
            gd_mark = c.n["gd"]
            for hh in range(2):
                hc = hp * 2 + hh
                for nh in range(2):
                    ns = slice(nh * 384, (nh + 1) * 384)
                    nc.tensor.wait_ge(c.s["gd"], gd_mark)
                    nc.tensor.wait_ge(c.s["sd"], sdm[e])
                    if k >= 1:
                        nc.tensor.wait_ge(c.s["ac"], ac_h + k)
                        nc.tensor.wait_ge(c.s["dv"], hm[k - 1])
                    else:
                        c.wait(nc.tensor, "ac")
                        c.wait(nc.tensor, "dv")
                    for cc in range(NCC):
                        nc.tensor.matmul(
                            p_u[:, 0:384], wb[:, cc, hh * 128:(hh + 1) * 128],
                            df[:, cc, ns], start=(cc == 0),
                            stop=(cc == NCC - 1))
                    for cc in range(NCC):
                        lg = nc.tensor.matmul(
                            p_g[:, 0:384],
                            wb[:, cc, 256 + hh * 128:256 + (hh + 1) * 128],
                            df[:, cc, ns], start=(cc == 0),
                            stop=(cc == NCC - 1))
                    ug_pe = c.inc(lg, "pe")
                    c.consumed("l0f_wug%d" % (hp % 2))
                    c.wait(nc.scalar, "pe")
                    ins = nc.scalar.activation(out=B["silu32"][:, 0:384],
                                               in_=p_g[:, 0:384], func=AF.Silu)
                    sm = c.inc(ins, "ac")
                    nc.vector.wait_ge(c.s["ac"], sm)
                    ins = nc.vector.tensor_mul(hst[:, hc, ns], p_u[:, 0:384],
                                               B["silu32"][:, 0:384])
                    hm[k] = c.inc(ins, "dv")
                    k += 1
        dfree[e] = ug_pe
        prefetch(e + 1)
        for ch in range(2):
            c.guard_buf(nc.gpsimd, "l0f_wdn")
            ins = nc.gpsimd.dma_start(
                out=wdn[:, 0:8, :],
                in_=downT[0:1024, ch * 512:(ch + 1) * 512].rearrange(
                    "(hc p) cf -> p hc cf", p=128))
            c.inc(ins, "gd", 16)
            ins = nc.gpsimd.dma_start(
                out=wdn[:, 8:16, :],
                in_=downT[1024:2048, ch * 512:(ch + 1) * 512].rearrange(
                    "(hc p) cf -> p hc cf", p=128))
            c.inc(ins, "gd", 16)
            gd_dn = c.n["gd"]
            for ts in range(NS):
                pb = p_y[ts % 2]
                slot = (ch * NS + ts) % 4
                nc.tensor.wait_ge(c.s["gd"], gd_dn)
                nc.tensor.wait_ge(c.s["dv"], hm[31])
                nc.tensor.wait_ge(c.s["dv"], ym[ts % 2])
                for hc in range(16):
                    last = nc.tensor.matmul(
                        pb[:], hst[:, hc, ts * 128:(ts + 1) * 128],
                        wdn[:, hc, :], start=(hc == 0), stop=(hc == 15))
                ypem = c.inc(last, "pe")
                c.consumed("l0f_wdn")
                nc.vector.wait_ge(c.s["pe"], ypem)
                nc.vector.wait_ge(c.s["gd"], gdm[slot])
                ins = nc.vector.tensor_copy(efo[slot][:], pb[:])
                em = ym[ts % 2] = c.inc(ins, "dv")
                nc.gpsimd.wait_ge(c.s["dv"], em)
                ins = nc.gpsimd.dma_start(
                    out=eoutL0[e * CAP0 + ts * 128:e * CAP0 + (ts + 1) * 128,
                               ch * 512:(ch + 1) * 512],
                    in_=efo[slot][:])
                gdm[slot] = c.inc(ins, "gd", 16)
    c.wait(nc.gpsimd, "gd")


def emit_l0_sparse_gather(c, stream, B, half, spilled, spill_to=None):
    nc = c.nc
    slotbuf = B["slotbuf"]
    eoutL0 = B["eoutL0"]
    stats, silu32 = B["stats"], B["silu32"]
    tiles = list(range(half * NTH, (half + 1) * NTH))
    cv = Carver(stream.ap()[:, spilled * NTH:(spilled + 1) * NTH, :]
                .rearrange("p a b -> p (a b)"))
    gbufs = [cv.f32([128, 1024]) for _ in range(4)]
    c.wait(nc.vector, "sp")
    c.wait(nc.gpsimd, "sp")
    gm, am = {}, {}
    for ts in range(NTH):
        ti = tiles[ts]
        if ts >= 4:
            nc.gpsimd.wait_ge(c.s["dv"], am[ts - 4])
        ins = nc.gpsimd.indirect_dma_start(
            out=gbufs[ts % 4][:], out_offset=None,
            in_=eoutL0,
            in_offset=IndirectOffsetOnAxis(ap=slotbuf[:, ti, 0:1], axis=0))
        gm[ts] = c.inc(ins, "gd", 16)
        nc.vector.wait_ge(c.s["gd"], gm[ts])
        ins = nc.vector.tensor_add(stream[:, ti, :], gbufs[ts % 4][:],
                                   stream[:, ti, :])
        am[ts] = c.inc(ins, "dv")
    c.wait(nc.scalar, "dv")
    c.wait(nc.scalar, "sp")
    emit_ln(c, stream, tiles, silu32, stats, spill_to=spill_to)


def emit_l0_ffn_sparse(c, stream, W, B, scrA, scrB):
    nc = c.nc
    ins = nc.gpsimd.dma_start(out=B["rt"][:], in_=W["routerT_tiled"])
    c.inc(ins, "gd", 16)
    ins = nc.gpsimd.dma_start(out=B["utb"][:], in_=B["cin_ut"])
    c.inc(ins, "gd", 16)
    ins = nc.sync.dma_start(out=B["iota8"][:], in_=B["cin_iota"])
    c.inc(ins, "ld", 16)
    nc.vector.memset(B["onesrow32"][:], 1.0)
    nc.vector.memset(B["onescol"][:], 1.0)
    nc.vector.memset(B["offs"][0:1, 0:8], 0.0)
    ins = nc.vector.memset(B["g"][:], 0.0)
    c.inc(ins, "dv")
    c.wait(nc.gpsimd, "dv")
    ins = nc.gpsimd.dma_start(out=B["eoutL0"][E0 * CAP0:E0 * CAP0 + 1, :],
                              in_=B["g"][0:1, :])
    c.inc(ins, "gd", 16)
    emit_join(c)

    import os as _os
    _noexp = bool(_os.environ.get("L0SP_NOEXP"))
    cv = Carver(stream.ap()[:, NTH:NT, :].rearrange("p a b -> p (a b)"))
    emit_l0_sparse_rps(c, stream, W, B, 0, cv, spill_to=scrA)
    emit_join(c)
    emit_unspill(c, stream, 1, scrB)
    cv = Carver(stream.ap()[:, 0:NTH, :].rearrange("p a b -> p (a b)"))
    emit_l0_sparse_rps(c, stream, W, B, 1, cv)
    emit_join(c)
    if _noexp:
        emit_ln(c, stream, list(range(NTH, NT)), B["silu32"], B["stats"])
        emit_join(c)
        emit_spill(c, stream, 1, scrB)
        emit_unspill(c, stream, 0, scrA)
        emit_ln(c, stream, list(range(0, NTH)), B["silu32"], B["stats"])
        emit_join(c)
        emit_unspill(c, stream, 1, scrB)
        c.wait(nc.sync, "sp")
        emit_join(c)
        return
    cv = Carver(stream.ap()[:, 0:NTH, :].rearrange("p a b -> p (a b)"))
    emit_l0_sparse_experts(c, W, B, cv)
    emit_join(c)
    emit_l0_sparse_gather(c, stream, B, 1, 0, spill_to=scrB)
    emit_join(c)
    emit_unspill(c, stream, 0, scrA)
    emit_l0_sparse_gather(c, stream, B, 0, 1, spill_to=scrA)
    emit_join(c)
    emit_unspill(c, stream, 1, scrB)
    c.wait(nc.sync, "sp")
    emit_join(c)


def prep_l0_ffn_weights(inp):
    up = np.asarray(inp["L0_ffn_up"])
    gate = np.asarray(inp["L0_ffn_gate"])
    down = np.asarray(inp["L0_ffn_down"])
    router = np.asarray(inp["L0_ffn_router"])
    rt = np.ascontiguousarray(
        router.T.reshape(NCC, 128, E0).transpose(1, 0, 2))
    return {
        "upT": np.ascontiguousarray(up.transpose(0, 2, 1)),
        "gateT": np.ascontiguousarray(gate.transpose(0, 2, 1)),
        "downT": np.ascontiguousarray(down.transpose(0, 2, 1)),
        "routerT_tiled": rt,
        "up_sT": np.ascontiguousarray(np.asarray(inp["L0_ffn_w_up_s"]).T),
        "gate_sT": np.ascontiguousarray(np.asarray(inp["L0_ffn_w_gate_s"]).T),
        "down_sT": np.ascontiguousarray(np.asarray(inp["L0_ffn_w_down_s"]).T),
    }


L0_FFN_SHAPES = {"upT": [E0, D, H0], "gateT": [E0, D, H0], "downT": [E0, H0, D],
                 "routerT_tiled": [128, NCC, E0], "up_sT": [D, H0],
                 "gate_sT": [D, H0], "down_sT": [H0, D]}


def build_launch(block_names):
    nc = bass.Bass("TRN2", target_bir_lowering=False)
    consts = build_consts()
    xin = nc.dram_tensor("xin", [T, D], F32, kind="ExternalInput").ap()
    xout = nc.dram_tensor("xout", [T, D], F32, kind="ExternalOutput").ap()
    cin = {k: nc.dram_tensor("c_" + k, list(v.shape), F32,
                             kind="ExternalInput").ap()
           for k, v in consts.items()}
    scrA = nc.dram_tensor("scrA", [128, NTH, D], F32, kind="Internal").ap()
    scrB = nc.dram_tensor("scrB", [128, NTH, D], F32, kind="Internal").ap()
    W = {}
    cc_in = nc.dram_tensor("cc_in", [1, 8], F32, kind="Internal").ap()
    cc_out = nc.dram_tensor("cc_out", [1, 8], F32, kind="Internal").ap()
    if "l0_attn" in block_names:
        W["l0_attn"] = {k: nc.dram_tensor("l0a_" + k, s, F32,
                                          kind="ExternalInput").ap()
                        for k, s in L0_ATTN_SHAPES.items()}
    if "l1_attn" in block_names:
        W["l1_attn"] = {k: nc.dram_tensor("l1a_" + k, s, F32,
                                          kind="ExternalInput").ap()
                        for k, s in L1_ATTN_SHAPES.items()}
    if "l0_ffn" in block_names:
        W["l0_ffn"] = {k: nc.dram_tensor("l0f_" + k, s, F32,
                                         kind="ExternalInput").ap()
                       for k, s in L0_FFN_SHAPES.items()}
        W["l0_ffn"]["dispL0"] = nc.dram_tensor(
            "dispL0", [E0 * CAP0 + 1, D], BF16, kind="Internal").ap()
        W["l0_ffn"]["eoutL0"] = nc.dram_tensor(
            "eoutL0", [E0 * CAP0 + 1, D], F32, kind="Internal").ap()
    if "l1_ffn" in block_names:
        W["l1_ffn"] = {k: nc.dram_tensor("l1f_" + k, s, F32,
                                         kind="ExternalInput").ap()
                       for k, s in L1_FFN_SHAPES.items()}
        W["l1_ffn"]["dispT"] = nc.dram_tensor(
            "dispT", [E1 * 2 * CAP + 1, D], BF16, kind="Internal").ap()
        W["l1_ffn"]["eoutD"] = nc.dram_tensor(
            "eoutD", [E1 * 2 * CAP + 1, D], F32, kind="Internal").ap()

    sem_names = ["ld", "gd", "sd", "pe", "dv", "ac", "st", "cc", "sp"]
    with contextlib.ExitStack() as stack:
        sems = {n: stack.enter_context(nc.semaphore(n)) for n in sem_names}
        c = Ctx(nc, sems)
        with (
            nc.sbuf_tensor("stream", [128, NT, D], F32) as stream,
            nc.sbuf_tensor("xfm", [128, NCC, 2048], BF16) as xfm,
            nc.sbuf_tensor("xcast0", [128, D], BF16) as xcast0,
            nc.sbuf_tensor("xcast1", [128, D], BF16) as xcast1,
            nc.sbuf_tensor("idbf", [128, 128], BF16) as idbf,
            nc.sbuf_tensor("idf32", [128, 128], F32) as idf32,
            nc.sbuf_tensor("silu32", [128, 1024], F32) as silu32,
            nc.sbuf_tensor("causal", [128, 128], F32) as causal,
            nc.sbuf_tensor("stats", [128, 8], F32) as stats,
            nc.Block() as block,
        ):
            @block.sync
            def _(sync):
                ins = nc.gpsimd.dma_start(out=idbf[:], in_=cin["idbf"])
                c.inc(ins, "gd", 16)
                ins = nc.sync.dma_start(out=idf32[:], in_=cin["idf32"])
                c.inc(ins, "ld", 16)
                ins = nc.sync.dma_start(out=causal[:], in_=cin["causal_add"])
                c.inc(ins, "ld", 16)
                emit_stream_load(c, xin, stream)
                for eng in (nc.vector, nc.tensor, nc.scalar, nc.gpsimd):
                    c.wait(eng, "ld")
                    c.wait(eng, "gd")

                c.heatbuf = idbf[:]
                B = {"xcast": [xcast0, xcast1], "idbf": idbf, "idf32": idf32,
                     "stats": stats, "silu32": silu32, "xfm": xfm,
                     "causal": causal}
                for bn in block_names:
                    if bn == "l0_ffn":
                        with contextlib.ExitStack() as st:
                            def sb(name, shape, dt, **kw):
                                return st.enter_context(
                                    nc.sbuf_tensor(name, shape, dt, **kw))
                            def ps(name, shape, dt):
                                return st.enter_context(
                                    nc.psum_tensor(name, shape, dt))
                            Bl = dict(
                                B,
                                wug=[sb("wug0", [128, NCC, 512], BF16),
                                     sb("wug1", [128, NCC, 512], BF16)],
                                wdn=sb("wdn", [128, 16, 512], BF16),
                                rt=sb("rt", [128, NCC, 8], BF16),
                                utb=sb("utb0", [128, 128], BF16),
                                iota8=sb("iota80", [128, 8], F32),
                                selbuf=sb("selbuf0", [128, NT, 1], F32),
                                slotbuf=sb("slotbuf0", [128, NT, 1], I32),
                                offs=sb("offs0", [1, 16], F32),
                                onesrow32=sb("onesrow320", [1, 128], F32),
                                onescol=sb("onescol0", [128, 1], BF16),
                                t8a=sb("t8a0", [128, 8], F32),
                                t8c=sb("t8c0", [128, 8], F32),
                                c8b=sb("c8b0", [128, 8], BF16),
                                m8=sb("m80", [128, 8], F32),
                                i8=sb("i80", [128, 8], U32),
                                g=silu32,
                                dispL0=W["l0_ffn"]["dispL0"],
                                eoutL0=W["l0_ffn"]["eoutL0"],
                                cin_ut=cin["ut_strict"],
                                cin_iota=cin["iota8"],
                                p_tr=[ps("p_tr0", [128, 512], BF16),
                                      ps("p_tr1", [128, 512], BF16)],
                                p_u=ps("p_u", [128, 512], F32),
                                p_g=ps("p_g", [128, 512], F32),
                                p_y=[ps("p_y0", [128, 512], F32),
                                     ps("p_y1", [128, 512], F32)],
                                p_s=ps("p_s", [128, 64], F32))
                            emit_l0_ffn_sparse(c, stream, W["l0_ffn"], Bl,
                                               scrA, scrB)
                    elif bn == "l0_attn":
                        with contextlib.ExitStack() as st:
                            def sb(name, shape, dt):
                                return st.enter_context(
                                    nc.sbuf_tensor(name, shape, dt))
                            def ps(name, shape, dt):
                                return st.enter_context(
                                    nc.psum_tensor(name, shape, dt))
                            Bl = dict(
                                B,
                                wvt=sb("wvt0", [128, NCC, 1024], BF16),
                                wqk=[sb("wqkA0", [128, 8, 128], BF16),
                                     sb("wqkA1", [128, 8, 128], BF16)],
                                paug=sb("paug", [128, 257], BF16),
                                paugA=sb("paugA", [128, 257], BF16),
                                paugB=sb("paugB", [128, 257], BF16),
                                pload=sb("pload", [128, 256], F32),
                                ctxbuf=sb("ctxbuf", [128, 2, 16, 2, 65], BF16),
                                vsumbuf=sb("vsumbuf", [128, 2112], BF16),
                                kmax=sb("kmax", [128, 4], F32),
                                kmbf=sb("kmbf", [128, 2], BF16),
                                acol=sb("acol", [128, 1], F32),
                                onesrow=sb("onesrow", [128, 128], BF16),
                                borow=sb("borow", [1, 1024], BF16),
                                p_a=ps("p0_a", [128, 512], F32),
                                p_b=ps("p0_b", [128, 512], F32),
                                p_dd=ps("p0_dd", [128, 512], F32),
                                p_dd2=ps("p0_dd2", [128, 512], F32),
                                p_c0=ps("p0_c0", [128, 512], F32),
                                p_c1=ps("p0_c1", [128, 512], F32),
                                p_tr=[ps("p0_tr0", [128, 512], BF16),
                                      ps("p0_tr1", [128, 512], BF16)])
                            emit_l0_attn(c, stream, W["l0_attn"], Bl,
                                         scrA, scrB, cc_in, cc_out)
                    elif bn == "l1_attn":
                        with (
                            nc.sbuf_tensor("wvt", [128, NCC, 1024], BF16) as wvt,
                            nc.sbuf_tensor("wqk0", [128, 16, 128], BF16) as wqk0,
                            nc.sbuf_tensor("wqk1", [128, 16, 128], BF16) as wqk1,
                            nc.sbuf_tensor("wotF", [128, 8, 512], BF16) as wotF,
                            nc.psum_tensor("pa_a", [128, 512], F32) as p_a,
                            nc.psum_tensor("pa_b", [128, 512], F32) as p_b,
                            nc.psum_tensor("pa_z0", [128, 512], F32) as p_z0,
                            nc.psum_tensor("pa_z1", [128, 512], F32) as p_z1,
                            nc.psum_tensor("pa_o", [128, 512], F32) as p_o,
                            nc.psum_tensor("pa_to", [128, 512], F32) as p_to,
                            nc.psum_tensor("pa_tr0", [128, 512], BF16) as pa_tr0,
                            nc.psum_tensor("pa_tr1", [128, 512], BF16) as pa_tr1,
                        ):
                            Bl = dict(B, wvt=wvt, wqk=[wqk0, wqk1],
                                      wot_full=wotF, p_a=p_a, p_b=p_b,
                                      p_z0=p_z0, p_z1=p_z1, p_o=p_o, p_to=p_to,
                                      p_tr=[pa_tr0, pa_tr1])
                            emit_block_halves(
                                c, stream, scrA, scrB,
                                lambda half, cv, sp_to: emit_l1_attn_half(
                                    c, stream, W["l1_attn"], Bl, half, cv,
                                    sp_to))
                    elif bn == "l1_ffn":
                        with contextlib.ExitStack() as st:
                            def sb(name, shape, dt, **kw):
                                return st.enter_context(
                                    nc.sbuf_tensor(name, shape, dt, **kw))
                            def ps(name, shape, dt):
                                return st.enter_context(
                                    nc.psum_tensor(name, shape, dt))
                            Wf = W["l1_ffn"]
                            Bl = dict(
                                B,
                                w1b=[sb("w1b0", [128, NCC, 128], BF16),
                                     sb("w1b1", [128, NCC, 128], BF16)],
                                w2b=[sb("w2b0", [128, 32, 128], BF16),
                                     sb("w2b1", [128, 32, 128], BF16)],
                                rw3=sb("rw3", [128, 32, 8], BF16),
                                rb1t=sb("rb1t", [128, 32], F32),
                                rb2t=sb("rb2t", [128, 32], F32),
                                rb3r=sb("rb3r", [1, 8], BF16),
                                utb=sb("utb", [128, 128], BF16),
                                iota8=sb("iota8", [128, 8], F32),
                                selbuf=sb("selbuf", [128, 2, NTH, 4], F32),
                                slotbuf=sb("slotbuf", [128, 2, NTH, 2], I32),
                                offs=sb("offs", [1, 16], F32),
                                onesrow=sb("onesrowF", [1, 128], BF16),
                                onescol=sb("onescolF", [128, 1], BF16),
                                onesrow32=sb("onesrow32", [1, 128], F32),
                                ebt1=sb("ebt1", [128, 32], F32),
                                ebt2=sb("ebt2", [128, 8], F32),
                                df=sb("df", [128, NCC, 2 * CAP], BF16,
                                      is_transpose_dest=True),
                                g=silu32,
                                h2sb=sb("h2sb", [128, 2, 1024], BF16),
                                t8a=sb("t8a", [128, 8], F32),
                                t8b=sb("t8b", [128, 8], F32),
                                t8c=sb("t8c", [128, 8], F32),
                                c8b=sb("c8b", [128, 8], BF16),
                                sc8=sb("sc8", [128, 8], F32),
                                m8=sb("m8", [128, 8], F32),
                                i8=sb("i8", [128, 8], U32),

                                dispT=Wf["dispT"], eoutD=Wf["eoutD"],
                                cin_ut=cin["ut_strict"],
                                cin_iota=cin["iota8"],
                                p_h1=ps("pf_h1", [128, 512], F32),
                                p_h2=ps("pf_h2", [128, 512], F32),
                                p_small=ps("pf_small", [128, 64], F32),
                                p_eo=ps("pf_eo", [128, 1024], F32),
                                p_tr=[ps("pf_tr0", [128, 512], BF16),
                                      ps("pf_tr1", [128, 512], BF16)])
                            Bl["xbf"] = Bl["h2sb"][:, 0, :]
                            Bl["ew1"] = Bl["w1b"]
                            Bl["ew2"] = Bl["w2b"]
                            emit_l1_ffn(c, stream, Wf, Bl, scrA, scrB)
                    elif bn == "ln_only":
                        emit_ln(c, stream, list(range(NT)), silu32, stats)
                        emit_join(c)
                    elif bn == "noop":
                        pass
                    else:
                        raise ValueError(bn)

                for s in ("dv", "ac", "pe", "sp"):
                    c.wait(nc.sync, s)
                emit_stream_store(c, xout, stream)

    return nc, consts


def _shard_x(x):
    x = np.asarray(x, dtype=np.float32)
    return [np.ascontiguousarray(x[2 * i:2 * i + 2].reshape(T, D))
            for i in range(8)]


def _gather_out(res, cores):
    outs = [res.results[i]["xout"].reshape(NBATCH, SEQ, D)
            for i in range(len(cores))]
    return np.concatenate(outs, axis=0)


def run_blocks(block_names, inputs, x_override=None, core_ids=None,
               shards=None):
    nc, consts = build_launch(block_names)
    if shards is None:
        x = x_override if x_override is not None else inputs["L0_attn_x"]
        shards = _shard_x(np.asarray(x))
    wmaps = {}
    if "l0_attn" in block_names:
        w = prep_l0_attn_weights(inputs)
        wmaps.update({"l0a_" + k: v for k, v in w.items()})
    if "l1_attn" in block_names:
        w = prep_l1_attn_weights(inputs)
        wmaps.update({"l1a_" + k: v for k, v in w.items()})
    if "l0_ffn" in block_names:
        w = prep_l0_ffn_weights(inputs)
        wmaps.update({"l0f_" + k: v for k, v in w.items()})
    if "l1_ffn" in block_names:
        w = prep_l1_ffn_weights(inputs)
        wmaps.update({"l1f_" + k: v for k, v in w.items()})
    cmaps = {"c_" + k: v for k, v in consts.items()}
    cores = core_ids if core_ids is not None else list(range(8))
    in_maps = [dict(xin=shards[i], **cmaps, **wmaps) for i in cores]
    res = run_bass_kernel_spmd(nc, in_maps, core_ids=cores)
    return res, cores


def kernel(**inputs):
    res, cores = run_blocks(["l0_attn", "l0_ffn", "l1_attn", "l1_ffn"], inputs)
    return _gather_out(res, cores).astype(np.float32)
